# version 15
# baseline (speedup 1.0000x reference)
import contextlib
import math
import numpy as np
import ml_dtypes
import concourse.bass as bass
import concourse.mybir as mybir
from concourse.bass_utils import run_bass_kernel_spmd

F32 = mybir.dt.float32
BF16 = mybir.dt.bfloat16
I8 = mybir.dt.int8
I32 = mybir.dt.int32
AF = mybir.ActivationFunctionType
ALU = mybir.AluOpType
AX = mybir.AxisListType.X

H = 2048
S = 2048
OWN = 1024
NB = 4
NHEAD = 16
INTER = 8192
NCH = 16
NTT = 16
NTB = 4
NOB = 2
NOC = 64

TWO_PI = 2.0 * math.pi
CW_C1 = float(np.float32(6.28125))
CW_C2 = float(np.float32(TWO_PI - CW_C1))
CW_C3 = float(np.float32(TWO_PI - CW_C1 - np.float64(np.float32(TWO_PI - CW_C1))))
INV_2PI = 1.0 / TWO_PI
PI = math.pi
PI_LO = float(np.float32(3.1415925))
HALF_PI = float(np.float32(PI / 2.0))
TWO_PI_F = float(np.float32(TWO_PI))
SCALE_QK = 1.0 / math.sqrt(128.0)

ENGS = ("sync", "tensor", "vector", "scalar", "gpsimd")
CSEMS = ("tensor.c", "vector.c", "scalar.c", "gpsimd.c")


class _Op:
    __slots__ = ("eng", "sem", "amt", "fn", "waits", "need", "val")

    def __init__(self, eng, sem, amt, fn, waits, need):
        self.eng, self.sem, self.amt, self.fn = eng, sem, amt, fn
        self.waits, self.need, self.val = waits, need, None


class Prog:

    def __init__(self):
        self.streams = {e: [] for e in ENGS}
        self.chans = {}

    def _emit(self, eng, sem, amt, fn, waits, need):
        kept = []
        for w in waits:
            if w is None:
                continue
            if w.eng == eng and w.sem in CSEMS and sem in CSEMS:
                continue
            kept.append(w)
        op = _Op(eng, sem, amt, fn, kept, need)
        for w in kept:
            w.need = True
        self.streams[eng].append(op)
        return op

    def c(self, eng, fn, waits=()):
        return self._emit(eng, eng + ".c", 1, fn, list(waits), False)

    def d(self, eng, chan, fn, waits=()):
        op = self._emit(eng, chan, 16, fn, list(waits), True)
        n = self.chans.get(chan, 0) + 16
        self.chans[chan] = n
        op.val = n
        return op

    def barrier(self, exclude=()):
        out = []
        seen = {}
        for e in ENGS:
            for op in reversed(self.streams[e]):
                if op.sem in CSEMS:
                    op.need = True
                    out.append(op)
                    break
            for op in reversed(self.streams[e]):
                if op.sem not in CSEMS and op.sem not in seen \
                        and op.sem not in exclude:
                    seen[op.sem] = op
        out.extend(seen.values())
        return out

    def resolve(self):
        counts = {s: 0 for s in CSEMS}
        for e in ENGS:
            for op in self.streams[e]:
                if op.need and op.sem in CSEMS:
                    counts[op.sem] += op.amt
                    op.val = counts[op.sem]
        return counts

    def replay(self, eng_obj, eng_name, semh):
        last = {}
        for op in self.streams[eng_name]:
            for w in op.waits:
                if w.val is None:
                    raise RuntimeError("unresolved wait")
                if last.get(w.sem, 0) < w.val:
                    eng_obj.wait_ge(semh[w.sem], w.val)
                    last[w.sem] = w.val
            inst = op.fn(eng_obj)
            if op.need:
                inst.then_inc(semh[op.sem], op.amt)


PRECAST_CH = ("wgbp", "wubp", "wdbp")


class Slot:

    def __init__(self, guard=()):
        self.prods = []
        self.cons = {}
        self.guards = [g for g in guard if g is not None]

    def ww(self):
        return self.guards + list(self.cons.values()) + self.prods

    def wrote(self, t, reset=True):
        if reset:
            self.prods = [t]
            self.cons = {}
            self.guards = []
        else:
            self.prods.append(t)

    def add(self, t):
        self.prods.append(t)

    def rw(self):
        return list(self.prods)

    def read(self, t):
        self.cons[t.eng] = t


def _finish(nc, P, perm):
    P.resolve()
    semh = {}
    for s in CSEMS:
        semh[s] = perm.enter_context(nc.semaphore(s.replace(".", "_")))
    for s in P.chans:
        semh[s] = perm.enter_context(nc.semaphore("ch_" + s))
    with nc.Block() as block:
        @block.sync
        def _(e):
            P.replay(e, "sync", semh)

        @block.tensor
        def _(e):
            P.replay(e, "tensor", semh)

        @block.vector
        def _(e):
            P.replay(e, "vector", semh)

        @block.scalar
        def _(e):
            P.replay(e, "scalar", semh)

        @block.gpsimd
        def _(e):
            P.replay(e, "gpsimd", semh)

    perm.close()
    return nc


def build_nc(dbg=(), stop_after=None):
    nc = bass.Bass("TRN2", target_bir_lowering=False, detect_race_conditions=False)
    P = Prog()

    def dram(name, shape, dtype, kind="Internal"):
        k = "ExternalOutput" if (name in dbg and kind == "Internal") else kind
        return nc.dram_tensor(name, shape, dtype, kind=k).ap()

    xT = dram("xT", [H, S], F32, "ExternalInput")
    wqT = dram("wqT", [H, H], F32, "ExternalInput")
    wkT = dram("wkT", [H, H], F32, "ExternalInput")
    wvT = dram("wvT", [H, H], F32, "ExternalInput")
    woT = dram("woT", [H, H], F32, "ExternalInput")
    wg8 = dram("wg8", [H, INTER], I8, "ExternalInput")
    wu8 = dram("wu8", [H, INTER], I8, "ExternalInput")
    wd8 = dram("wd8", [INTER, H], I8, "ExternalInput")
    sgI = dram("sgI", [128, 64], F32, "ExternalInput")
    suI = dram("suI", [128, 64], F32, "ExternalInput")
    sdI = dram("sdI", [128, 16], F32, "ExternalInput")
    gateI = dram("gateI", [1, 16], F32, "ExternalInput")
    posB = dram("posB", [128, S], F32, "ExternalInput")
    yT = dram("yT", [H, OWN], F32, "ExternalOutput")

    xn_s = dram("xn_s", [NCH, 128, S], BF16)
    tbl_s = dram("tbl_s", [8, 2, 128, S], BF16)
    kfm_s = dram("kfm_s", [NCH, 128, S], BF16)
    qs_s = dram("qs_s", [NCH, 128, OWN], BF16)
    qf_s = dram("qf_s", [NCH, 128, OWN], BF16)
    vt_s = dram("vt_s", [NCH, NTT, 128, 128], BF16)
    wgb = dram("wgb", [NCH, 128, INTER], BF16)
    wub = dram("wub", [NCH, 128, INTER], BF16)
    wdb = dram("wdb", [NOC, 128, H], BF16)
    fuse_s = dram("fuse_s", [NOC, 128, OWN], BF16)
    comb_s = dram("comb_s", [NCH, 128, OWN], BF16)
    ss_s = dram("ss_s", [NCH, 128, OWN], BF16)
    diag_s = dram("diag_s", [NHEAD, 4, 1024], F32)
    diagU_s = dram("diagU_s", [128, 16], F32)

    xnH = [Slot() for _ in range(NCH)]
    tblH = [[Slot(), Slot()] for _ in range(8)]
    kfmH = [Slot() for _ in range(NCH)]
    qsH = [Slot() for _ in range(NCH)]
    qfH = [Slot() for _ in range(NCH)]
    vtH = [Slot() for _ in range(NCH)]
    wgbH = [Slot() for _ in range(NCH)]
    wubH = [Slot() for _ in range(NCH)]
    wdbH = [Slot() for _ in range(NOC)]
    fuseH = [Slot() for _ in range(NOC)]
    combH = [Slot() for _ in range(NCH)]
    ssH = [Slot() for _ in range(NCH)]

    perm = contextlib.ExitStack()

    def sbuf(stk, name, shape, dt=F32):
        return stk.enter_context(nc.sbuf_tensor(name, shape, dt))

    scr = [sbuf(perm, f"scr{i}", [128, S]) for i in range(4)]
    scrS = [Slot() for _ in range(4)]
    tblb = sbuf(perm, "tblb", [128, 2 * S], BF16)
    tblbS = [Slot() for _ in range(4)]
    tblfS = [Slot(), Slot()]
    omgc = sbuf(perm, "omgc", [128, 8]); omgcS = Slot()
    grow = sbuf(perm, "grow", [1, 16]); growS = Slot()
    ogrow = sbuf(perm, "ogrow", [1, 16]); ogrowS = Slot()
    qm2 = sbuf(perm, "qm2", [1, 16]); qm2S = Slot()
    km2 = sbuf(perm, "km2", [1, 16]); km2S = Slot()
    urow = sbuf(perm, "urow", [1, 16]); urowS = Slot()
    u2row = sbuf(perm, "u2row", [1, 16]); u2S = Slot()
    tmp11 = sbuf(perm, "tmp11", [1, 2]); tmp11S = Slot()
    m2all = sbuf(perm, "m2all", [1, 128]); m2allS = Slot()
    negU = sbuf(perm, "negU", [128, 16]); negUS = Slot()
    rl_r = sbuf(perm, "rl_r", [1, 1024]); rlS = Slot()
    rs_r = sbuf(perm, "rs_r", [1, 512]); rsS = Slot()
    onesc = sbuf(perm, "onesc", [128, 1])
    onescb = sbuf(perm, "onescb", [128, 1], BF16)
    onesM = sbuf(perm, "onesM", [128, 128], BF16)
    onesr = sbuf(perm, "onesr", [1, 128])
    zeroc = sbuf(perm, "zeroc", [128, 1])
    epsc = sbuf(perm, "epsc", [128, 1])
    pjt = sbuf(perm, "pjt", [128, 128])
    ident = sbuf(perm, "ident", [128, 128], BF16); identS = Slot()
    constS = Slot()
    sgb = sbuf(perm, "sgb", [128, 64]); sgS = Slot()
    sub_ = sbuf(perm, "sub_", [128, 64]); suS = Slot()
    sdb = sbuf(perm, "sdb", [128, 16]); sdS = Slot()
    k2scr = sbuf(perm, "k2scr", [128, 2 * 512], BF16); k2S = [Slot(), Slot()]

    SYD = lambda ch, fn, waits=(): P.d("sync", ch, fn, waits)
    GPD = lambda ch, fn, waits=(): P.d("gpsimd", ch, fn, waits)
    SCD = lambda ch, fn, waits=(): P.d("scalar", ch, fn, waits)
    TC = lambda fn, waits=(): P.c("tensor", fn, waits)
    VC = lambda fn, waits=(): P.c("vector", fn, waits)
    SC = lambda fn, waits=(): P.c("scalar", fn, waits)
    GC = lambda fn, waits=(): P.c("gpsimd", fn, waits)

    s12 = contextlib.ExitStack()

    sE = contextlib.ExitStack()
    Wsb = sbuf(sE, "Wsb", [128, NCH * H], BF16)
    WsbS = [Slot() for _ in range(NCH)]
    vob = sbuf(sE, "vob", [128, 2 * S], BF16)
    vobS = [Slot(), Slot()]

    def load_W(src, guard=()):
        gw = list(guard)
        for g in range(4):
            last = None
            for j in range(g * 4, (g + 1) * 4):
                last = GPD(f"wsbg{g}", lambda e, j=j, src=src: e.dma_start(
                    out=Wsb[:, j * H:(j + 1) * H], in_=src[j * 128:(j + 1) * 128, :]),
                    waits=WsbS[j].ww() + gw)
            for j in range(g * 4, (g + 1) * 4):
                WsbS[j].wrote(last)

    t = GC(lambda e: e.iota(omgc[:], [[128, 8]], channel_multiplier=1,
                            allow_small_or_imprecise_dtypes=True))
    omgcS.wrote(t)
    pjS = Slot()
    t = GC(lambda e: e.iota(pjt[:], [[-1, 128]], channel_multiplier=1,
                            allow_small_or_imprecise_dtypes=True))
    pjS.wrote(t)
    load_W(wvT)
    lg = None
    for j in range(NCH):
        lg = GPD("wgbp", lambda e, j=j: e.dma_start(out=wgb[j], in_=wg8[j * 128:(j + 1) * 128, :]))
    for j in range(NCH):
        wgbH[j].wrote(lg)
    c1 = SYD("consts", lambda e: e.dma_start(out=grow[:], in_=gateI[:]))
    c2 = SYD("consts", lambda e: e.dma_start(out=sgb[:], in_=sgI[:]))
    c3 = SYD("consts", lambda e: e.dma_start(out=sub_[:], in_=suI[:]))
    c4 = SYD("consts", lambda e: e.dma_start(out=sdb[:], in_=sdI[:]))
    for _s in (growS, sgS, suS, sdS):
        _s.wrote(c4)
    VC(lambda e: e.memset(onesc[:], 1.0))
    VC(lambda e: e.memset(onesM[:], 1.0))
    VC(lambda e: e.memset(onescb[:], 1.0))
    VC(lambda e: e.memset(onesr[:], 1.0))
    VC(lambda e: e.memset(zeroc[:], 0.0))
    VC(lambda e: e.memset(epsc[:], 1e-5))
    qm2S.wrote(VC(lambda e: e.memset(qm2[:], 0.0)))
    km2S.wrote(VC(lambda e: e.memset(km2[:], 0.0)))
    VC(lambda e: e.memset(tmp11[:], 0.0))
    t = VC(lambda e: e.tensor_scalar(ident[:], pjt[:], 0.0, None, ALU.is_equal),
           waits=pjS.rw())
    identS.wrote(t)
    constS.wrote(t)
    tg = SC(lambda e: e.activation(grow[:], grow[:], AF.Sigmoid,
                                   bias=zeroc[0:1, 0:1], scale=1.0),
            waits=growS.rw() + constS.rw())
    growS.wrote(tg)
    tog = VC(lambda e: e.tensor_scalar(ogrow[:], grow[:], -1.0, 1.0, ALU.mult, ALU.add),
             waits=growS.rw())
    ogrowS.wrote(tog)
    growS.read(tog)
    tw = SC(lambda e: e.activation(omgc[:], omgc[:], AF.Exp, bias=zeroc[:, 0:1],
                                   scale=-math.log(10000.0) / 1024.0),
            waits=omgcS.rw() + constS.rw())
    omgcS.wrote(tw)


    rows_a = sbuf(s12, "rows_a", [1, S]); rowsaS = Slot()
    rows_b = sbuf(s12, "rows_b", [1, S]); rowsbS = Slot()
    rb_b = sbuf(s12, "rb_b", [128, S]); rbS = Slot()
    nm_b = sbuf(s12, "nm_b", [128, S]); nmS = Slot()
    xnp = sbuf(s12, "xnp", [128, 2 * S], BF16); xnpS = [Slot(), Slot()]

    with nc.psum_tensor("pstx", [1, S], F32) as pstx, \
         nc.psum_tensor("pstx2", [1, S], F32) as pstx2:
        pstxS, pstx2S = Slot(), Slot()
        lastx = lastx2 = None
        for j in range(NCH):
            xb, xbS = scr[j % 2], scrS[j % 2]
            t = (SYD if j % 2 == 0 else SCD)(
                f"scrx{j % 2}", lambda e, j=j, xb=xb: e.dma_start(
                    out=xb[:], in_=xT[j * 128:(j + 1) * 128, :]), waits=xbS.ww())
            xbS.wrote(t)
            tq = VC(lambda e, xb=xb: e.tensor_tensor(scr[2][:], xb[:], xb[:], ALU.mult),
                    waits=xbS.rw() + scrS[2].ww())
            xbS.read(tq)
            scrS[2].wrote(tq)
            for blk in range(4):
                mm = TC(lambda e, xb=xb, blk=blk, j=j: e.matmul(
                    pstx[0:1, blk * 512:(blk + 1) * 512], onesc[:],
                    xb[:, blk * 512:(blk + 1) * 512],
                    start=(j == 0), stop=(j == NCH - 1)),
                    waits=xbS.rw() + (constS.rw() if j == 0 and blk == 0 else []))
                xbS.read(mm)
                lastx = mm
            for blk in range(4):
                mm = TC(lambda e, blk=blk, j=j: e.matmul(
                    pstx2[0:1, blk * 512:(blk + 1) * 512], onesc[:],
                    scr[2][:, blk * 512:(blk + 1) * 512],
                    start=(j == 0), stop=(j == NCH - 1)), waits=scrS[2].rw())
                scrS[2].read(mm)
                lastx2 = mm
        pstxS.wrote(lastx)
        pstx2S.wrote(lastx2)
        tm = VC(lambda e: e.tensor_scalar(rows_b[:], pstx[0:1, :], 1.0 / H, None, ALU.mult),
                waits=pstxS.rw())
        rowsbS.wrote(tm)
        pstxS.read(tm)
        te = VC(lambda e: e.tensor_scalar(rows_a[:], pstx2[0:1, :], 1.0 / H, None, ALU.mult),
                waits=pstx2S.rw())
        rowsaS.wrote(te)
        pstx2S.read(te)
        t3 = VC(lambda e: e.tensor_tensor(pstx[0:1, :], rows_b[:], rows_b[:], ALU.mult))
        pstxS.wrote(t3)
        t4 = VC(lambda e: e.tensor_tensor(rows_a[:], rows_a[:], pstx[0:1, :], ALU.subtract))
        rowsaS.wrote(t4)
        pstxS.read(t4)
        t5 = SC(lambda e: e.activation(pstx2[0:1, :], rows_a[:], AF.Sqrt,
                                       bias=epsc[0:1, 0:1], scale=1.0),
                waits=rowsaS.rw() + pstx2S.rw())
        pstx2S.wrote(t5)
        rowsaS.read(t5)
        t6 = VC(lambda e: e.reciprocal(rows_a[:], pstx2[0:1, :]),
                waits=pstx2S.rw() + rowsaS.ww())
        rowsaS.wrote(t6)
        pstx2S.read(t6)
        t7 = VC(lambda e: e.scalar_tensor_tensor(rows_b[:], rows_b[:], -1.0, rows_a[:],
                                                 ALU.mult, ALU.mult))
        rowsbS.wrote(t7)
        rowsaS.read(t7)
    with nc.psum_tensor("pbc1", [128, S], F32) as pbc1:
        pbc1S = Slot(guard=[t7])
        lastmm = None
        for blk in range(4):
            mm = TC(lambda e, blk=blk: e.matmul(
                pbc1[:, blk * 512:(blk + 1) * 512], onesr[0:1, :],
                rows_a[0:1, blk * 512:(blk + 1) * 512], start=True, stop=True),
                waits=rowsaS.rw() + (pbc1S.ww() if blk == 0 else []))
            rowsaS.read(mm)
            lastmm = mm
        pbc1S.wrote(lastmm)
        ev = VC(lambda e: e.tensor_copy(rb_b[:], pbc1[:]), waits=pbc1S.rw())
        rbS.wrote(ev)
        pbc1S.read(ev)
        lastmm = None
        for blk in range(4):
            mm = TC(lambda e, blk=blk: e.matmul(
                pbc1[:, blk * 512:(blk + 1) * 512], onesr[0:1, :],
                rows_b[0:1, blk * 512:(blk + 1) * 512], start=True, stop=True),
                waits=rowsbS.rw() + [pbc1S.cons.get("vector")])
            rowsbS.read(mm)
            lastmm = mm
        pbc1S.wrote(lastmm)
        ev = VC(lambda e: e.tensor_copy(nm_b[:], pbc1[:]), waits=pbc1S.rw())
        nmS.wrote(ev)
        pbc1S.read(ev)

    cp_ = SYD("posbL", lambda e: e.dma_start(out=scr[3][:], in_=posB[:]),
              waits=scrS[3].ww())
    scrS[3].wrote(cp_)

    for j in range(NCH):
        xb, xbS = scr[j % 2], scrS[j % 2]
        t = (SYD if j % 2 == 0 else SCD)(
            f"scrx{j % 2}", lambda e, j=j, xb=xb: e.dma_start(
                out=xb[:], in_=xT[j * 128:(j + 1) * 128, :]), waits=xbS.ww())
        xbS.wrote(t)
        t1 = VC(lambda e, xb=xb: e.tensor_tensor(scr[2][:], xb[:], rb_b[:], ALU.mult),
                waits=xbS.rw() + rbS.rw() + scrS[2].ww())
        xbS.read(t1)
        rbS.read(t1)
        scrS[2].wrote(t1)
        par = j % 2
        xo = xnp[:, par * S:(par + 1) * S]
        t2 = VC(lambda e, xo=xo: e.tensor_tensor(xo, scr[2][:], nm_b[:], ALU.add),
                waits=nmS.rw() + xnpS[par].ww())
        nmS.read(t2)
        scrS[2].read(t2)
        xnpS[par].wrote(t2)
        st = SCD(f"st_xnp{par}", lambda e, j=j, xo=xo: e.dma_start(out=xn_s[j], in_=xo),
                 waits=xnpS[par].rw())
        xnpS[par].read(st)
        xnH[j].wrote(st)

    def emit_table_block(pr):
        if True:
            a0 = SC(lambda e, pr=pr: e.activation(scr[2][:], scr[3][:], AF.Copy,
                                                  bias=0.0,
                                                  scale=omgc[:, pr:pr + 1]),
                    waits=scrS[2].ww() + scrS[3].rw() + omgcS.rw())
            scrS[2].wrote(a0)
            scrS[3].read(a0)
            omgcS.read(a0)
            t1 = VC(lambda e: e.tensor_scalar(i32scr[:], scr[2][:], INV_2PI, None,
                                              ALU.mult),
                    waits=scrS[2].rw() + i32S.ww())
            i32S.wrote(t1)
            scrS[2].read(t1)
            t3 = SC(lambda e: e.copy(scr[1][:], i32scr[:]),
                    waits=i32S.rw() + scrS[1].ww())
            scrS[1].wrote(t3)
            i32S.read(t3)
            t4a = VC(lambda e: e.scalar_tensor_tensor(scr[2][:], scr[1][:], -CW_C1,
                                                      scr[2][:], ALU.mult, ALU.add),
                     waits=scrS[1].rw() + scrS[2].rw())
            scrS[2].wrote(t4a)
            t4b = VC(lambda e: e.scalar_tensor_tensor(scr[2][:], scr[1][:], -CW_C2,
                                                      scr[2][:], ALU.mult, ALU.add))
            scrS[2].wrote(t4b)
            t4 = VC(lambda e: e.scalar_tensor_tensor(scr[2][:], scr[1][:], -CW_C3,
                                                     scr[2][:], ALU.mult, ALU.add))
            scrS[2].wrote(t4)
            scrS[1].read(t4)
            wh1 = VC(lambda e: e.tensor_scalar(scr[0][:], scr[2][:], PI_LO, None,
                                               ALU.is_gt), waits=scrS[0].ww())
            scrS[0].wrote(wh1)
            wh2 = VC(lambda e: e.scalar_tensor_tensor(scr[2][:], scr[0][:], -TWO_PI_F,
                                                      scr[2][:], ALU.mult, ALU.add))
            scrS[2].wrote(wh2)
            scrS[0].read(wh2)
            wl1 = VC(lambda e: e.tensor_scalar(scr[0][:], scr[2][:], -PI_LO, None,
                                               ALU.is_lt), waits=scrS[0].ww())
            scrS[0].wrote(wl1)
            wl2 = VC(lambda e: e.scalar_tensor_tensor(scr[2][:], scr[0][:], TWO_PI_F,
                                                      scr[2][:], ALU.mult, ALU.add))
            scrS[2].wrote(wl2)
            scrS[0].read(wl2)
            for ti in (0, 1):
                if ti == 0:
                    t5 = VC(lambda e: e.tensor_scalar(scr[0][:], scr[2][:],
                                                      PI_LO, -PI_LO, ALU.min, ALU.max),
                            waits=scrS[0].ww())
                else:
                    tm_ = VC(lambda e: e.tensor_scalar(scr[0][:], scr[2][:],
                                                       HALF_PI, None, ALU.is_gt),
                             waits=scrS[0].ww())
                    scrS[0].wrote(tm_)
                    tw_ = VC(lambda e: e.scalar_tensor_tensor(scr[0][:], scr[0][:],
                                                              -TWO_PI_F, scr[2][:],
                                                              ALU.mult, ALU.add))
                    scrS[0].wrote(tw_)
                    t5 = VC(lambda e: e.tensor_scalar(scr[0][:], scr[0][:],
                                                      HALF_PI, PI_LO, ALU.add, ALU.min))
                scrS[0].wrote(t5)
                scrS[2].read(t5)
                dst = tblb[:, ti * S:(ti + 1) * S]
                s5 = SC(lambda e, dst=dst: e.activation(dst, scr[0][:], AF.Sin,
                                                        bias=zeroc[:, 0:1], scale=1.0),
                        waits=scrS[0].rw() + tblfS[ti].ww())
                tblfS[ti].wrote(s5)
                scrS[0].read(s5)
                d5 = SCD(f"st_tbl{ti}", lambda e, pr=pr, ti=ti, dst=dst: e.dma_start(
                    out=tbl_s[pr, ti], in_=dst), waits=tblfS[ti].rw())
                tblfS[ti].read(d5)
                tblH[pr][ti].wrote(d5)

    s12.close()
    if stop_after == "tables":
        i32scr = sbuf(perm, "i32scr", [128, S], I32)
        i32S = Slot()
        for _pr in range(8):
            emit_table_block(_pr)
        bar_end = P.barrier()
        P.c("scalar", lambda e: e.nop(), waits=bar_end)
        return _finish(nc, P, perm)

    bar_pre3 = P.barrier(exclude=PRECAST_CH)
    i32scr = sbuf(sE, "i32scr", [128, S], I32)
    i32S = Slot(guard=bar_pre3)
    sF = contextlib.ExitStack()
    xncol = sbuf(sF, "xncol", [128, 2 * NCH * 512], BF16)
    xncolS = [[Slot() for _ in range(NCH)] for _ in range(2)]
    kraw = sbuf(sF, "kraw", [128, NCH * 512], BF16)
    krawS = [Slot() for _ in range(NCH)]
    ropeb = sbuf(sF, "ropeb", [128, 4 * 512], BF16)
    ropebS = [Slot() for _ in range(4)]
    qsb = sbuf(sF, "qsb", [128, 2 * 512], BF16)
    qsbS = [Slot(), Slot()]
    qfb = sbuf(sF, "qfb", [128, 2 * 512], BF16)
    qfbS = [Slot(), Slot()]

    bar3 = P.barrier(exclude=PRECAST_CH)
    with nc.psum_tensor("pv0", [128, S], F32) as pv0, \
         nc.psum_tensor("pv1", [128, S], F32) as pv1:
        pvs = [pv0, pv1]
        pvS = [Slot(guard=bar3), Slot(guard=bar3)]
        for tb in range(NTB):
            par = tb % 2
            glast = None
            for j in range(NCH):
                sl = xncolS[par][j]
                glast = SYD(f"xncol{par}", lambda e, j=j, tb=tb, par=par: e.dma_start(
                    out=xncol[:, (par * NCH + j) * 512:(par * NCH + j + 1) * 512],
                    in_=xn_s[j][:, tb * 512:(tb + 1) * 512]),
                    waits=sl.ww() + xnH[j].rw() +
                    (bar_pre3 if (tb == 0 and j == 0) else []))
                xnH[j].read(glast)
            for j in range(NCH):
                xncolS[par][j].wrote(glast)
            for tloc in range(4):
                tt = tb * 4 + tloc
                pp = tt % 2
                pv, pS = pvs[pp], pvS[pp]
                lastmm = None
                for j in range(NCH):
                    sl = xncolS[par][j]
                    lhs = xncol[:, (par * NCH + j) * 512 + tloc * 128:
                                (par * NCH + j) * 512 + (tloc + 1) * 128]
                    for ib in range(4):
                        w = sl.rw() + WsbS[j].rw()
                        if j == 0 and ib == 0:
                            w += pS.ww()
                        mm = TC(lambda e, lhs=lhs, j=j, ib=ib, pv=pv: e.matmul(
                            pv[:, ib * 512:(ib + 1) * 512], lhs,
                            Wsb[:, j * H + ib * 512:j * H + (ib + 1) * 512],
                            start=(j == 0), stop=(j == NCH - 1)), waits=w)
                        sl.read(mm)
                        WsbS[j].read(mm)
                        lastmm = mm
                pS.wrote(lastmm)
                ev = VC(lambda e, pv=pv, pp=pp: e.tensor_copy(
                    vob[:, pp * S:(pp + 1) * S], pv[:]),
                    waits=pS.rw() + vobS[pp].ww())
                vobS[pp].wrote(ev)
                pS.read(ev)
                st = SCD(f"st_vt{pp}", lambda e, tt=tt, pp=pp: e.dma_start(
                    out=vt_s[:, tt].rearrange("c p f -> p c f"),
                    in_=vob[:, pp * S:(pp + 1) * S].rearrange(
                        "p (c f) -> p c f", c=NCH)),
                    waits=vobS[pp].rw())
                vobS[pp].read(st)
                for ci in range(NCH):
                    vtH[ci].add(st)
                if tt % 2 == 0:
                    emit_table_block(tt // 2)

    def proj_rope(kind):
        bar = P.barrier(exclude=PRECAST_CH)
        for _i in range(4):
            tblbS[_i] = Slot(guard=bar)
        load_W(wkT if kind == "k" else wqT)
        nblk = NTB if kind == "k" else NOB
        with nc.psum_tensor(f"pk0_{kind}", [128, 512], F32) as pk0, \
             nc.psum_tensor(f"pk1_{kind}", [128, 512], F32) as pk1, \
             nc.psum_tensor(f"pkn_{kind}", [128, 1024], F32) as pkn:
            pks = [pk0, pk1]
            pkS = [Slot(guard=bar), Slot(guard=bar)]
            pknS = [Slot(guard=bar), Slot(guard=bar)]
            for tb in range(nblk):
                par = tb % 2
                glast = None
                for j in range(NCH):
                    sl = xncolS[par][j]
                    glast = SYD(f"xncol{par}", lambda e, j=j, tb=tb, par=par: e.dma_start(
                        out=xncol[:, (par * NCH + j) * 512:(par * NCH + j + 1) * 512],
                        in_=xn_s[j][:, tb * 512:(tb + 1) * 512]),
                        waits=sl.ww() + xnH[j].rw())
                    xnH[j].read(glast)
                for j in range(NCH):
                    xncolS[par][j].wrote(glast)
                for c in range(NCH):
                    pp = c % 2
                    pk, pS = pks[pp], pkS[pp]
                    lastmm = None
                    for j in range(NCH):
                        sl = xncolS[par][j]
                        w = sl.rw() + WsbS[j].rw()
                        if j == 0:
                            w += pS.ww()
                        mm = TC(lambda e, j=j, c=c, pk=pk, par=par: e.matmul(
                            pk[:], Wsb[:, j * H + c * 128:j * H + (c + 1) * 128],
                            xncol[:, (par * NCH + j) * 512:(par * NCH + j + 1) * 512],
                            start=(j == 0), stop=(j == NCH - 1)), waits=w)
                        sl.read(mm)
                        WsbS[j].read(mm)
                        lastmm = mm
                    pS.wrote(lastmm)
                    ev = VC(lambda e, pk=pk, c=c: e.tensor_copy(
                        kraw[:, c * 512:(c + 1) * 512], pk[:]),
                        waits=pS.rw() + krawS[c].ww())
                    krawS[c].wrote(ev)
                    pS.read(ev)
                for pr in range(8):
                    c0, c8 = pr, pr + 8
                    hp = pr % 2
                    slS, slC = tblbS[2 * hp], tblbS[2 * hp + 1]
                    sinb = tblb[:, (2 * hp) * 512:(2 * hp + 1) * 512]
                    cosb = tblb[:, (2 * hp + 1) * 512:(2 * hp + 2) * 512]
                    tS = SYD(f"tblq{2 * hp}", lambda e, pr=pr, tb=tb, sinb=sinb: e.dma_start(
                        out=sinb, in_=tbl_s[pr, 0][:, tb * 512:(tb + 1) * 512]),
                        waits=slS.ww() + tblH[pr][0].rw())
                    slS.wrote(tS)
                    tblH[pr][0].read(tS)
                    tCo = SYD(f"tblq{2 * hp + 1}", lambda e, pr=pr, tb=tb, cosb=cosb: e.dma_start(
                        out=cosb, in_=tbl_s[pr, 1][:, tb * 512:(tb + 1) * 512]),
                        waits=slC.ww() + tblH[pr][1].rw())
                    slC.wrote(tCo)
                    tblH[pr][1].read(tCo)
                    kr0 = kraw[:, c0 * 512:(c0 + 1) * 512]
                    kr8 = kraw[:, c8 * 512:(c8 + 1) * 512]
                    f0 = ropeb[:, (2 * hp) * 512:(2 * hp + 1) * 512]
                    f8 = ropeb[:, (2 * hp + 1) * 512:(2 * hp + 2) * 512]
                    f0S, f8S = ropebS[2 * hp], ropebS[2 * hp + 1]
                    sa = scr[0][:, 0:512]
                    sb2 = scr[0][:, 512:1024]
                    t1 = VC(lambda e, kr8=kr8, cosb=cosb, sa=sa: e.tensor_tensor(
                        sa, kr8, cosb, ALU.mult),
                        waits=krawS[c8].rw() + slC.rw() + scrS[0].ww())
                    krawS[c8].read(t1)
                    slC.read(t1)
                    scrS[0].wrote(t1)
                    t2 = VC(lambda e, kr0=kr0, sinb=sinb, sb2=sb2: e.tensor_tensor(
                        sb2, kr0, sinb, ALU.mult),
                        waits=krawS[c0].rw() + slS.rw())
                    krawS[c0].read(t2)
                    slS.read(t2)
                    t3 = VC(lambda e, f8=f8, sa=sa, sb2=sb2: e.tensor_tensor(
                        f8, sa, sb2, ALU.add), waits=f8S.ww())
                    f8S.wrote(t3)
                    scrS[0].read(t3)
                    t4 = VC(lambda e, kr0=kr0, cosb=cosb, sa=sa: e.tensor_tensor(
                        sa, kr0, cosb, ALU.mult),
                        waits=krawS[c0].rw() + slC.rw())
                    krawS[c0].read(t4)
                    slC.read(t4)
                    t5 = VC(lambda e, kr8=kr8, sinb=sinb, sb2=sb2: e.tensor_tensor(
                        sb2, kr8, sinb, ALU.mult),
                        waits=krawS[c8].rw() + slS.rw())
                    krawS[c8].read(t5)
                    slS.read(t5)
                    t6 = VC(lambda e, f0=f0, sa=sa, sb2=sb2: e.tensor_tensor(
                        f0, sa, sb2, ALU.subtract), waits=f0S.ww())
                    f0S.wrote(t6)
                    scrS[0].read(t6)
                    for (cc, fb, fbS) in ((c0, f0, f0S), (c8, f8, f8S)):
                        if kind == "k":
                            st = SCD(f"st_rope{2 * hp + (0 if cc == c0 else 1)}",
                                     lambda e, cc=cc, tb=tb, fb=fb: e.dma_start(
                                out=kfm_s[cc][:, tb * 512:(tb + 1) * 512], in_=fb),
                                waits=fbS.rw())
                            fbS.read(st)
                            kfmH[cc].add(st)
                            nsrc, nsrcS = fb, fbS
                        else:
                            qp = cc % 2
                            qv = qsb[:, qp * 512:(qp + 1) * 512]
                            tqs = VC(lambda e, fb=fb, qv=qv: e.tensor_scalar(
                                qv, fb, SCALE_QK, None, ALU.mult),
                                waits=qsbS[qp].ww())
                            qsbS[qp].wrote(tqs)
                            fbS.read(tqs)
                            st = SCD(f"st_qsb{qp}", lambda e, cc=cc, tb=tb, qv=qv: e.dma_start(
                                out=qs_s[cc][:, tb * 512:(tb + 1) * 512], in_=qv),
                                waits=qsbS[qp].rw())
                            qsbS[qp].read(st)
                            qsH[cc].add(st)
                            smn = scr[1][:, 0:512]
                            sex = scr[1][:, 512:1024]
                            srl = scr[1][:, 1024:1536]
                            tmn = VC(lambda e, fb=fb, smn=smn: e.tensor_scalar(
                                smn, fb, 0.0, None, ALU.min),
                                waits=scrS[1].ww())
                            scrS[1].wrote(tmn)
                            fbS.read(tmn)
                            tex = SC(lambda e, smn=smn, sex=sex: e.activation(
                                sex, smn, AF.Exp, bias=zeroc[:, 0:1], scale=1.0),
                                waits=scrS[1].rw())
                            scrS[1].wrote(tex)
                            trl = VC(lambda e, fb=fb, srl=srl: e.tensor_scalar(
                                srl, fb, 0.0, None, ALU.max),
                                waits=[tex])
                            scrS[1].wrote(trl)
                            fbS.read(trl)
                            qv2 = qfb[:, qp * 512:(qp + 1) * 512]
                            tqf = VC(lambda e, qv2=qv2, srl=srl, sex=sex: e.tensor_tensor(
                                qv2, srl, sex, ALU.add),
                                waits=qfbS[qp].ww())
                            qfbS[qp].wrote(tqf)
                            scrS[1].read(tqf)
                            st2 = SCD(f"st_qfb{qp}", lambda e, cc=cc, tb=tb, qv2=qv2: e.dma_start(
                                out=qf_s[cc][:, tb * 512:(tb + 1) * 512], in_=qv2),
                                waits=qfbS[qp].rw())
                            qfbS[qp].read(st2)
                            qfH[cc].add(st2)
                            nsrc, nsrcS = qv, qsbS[qp]
                        kp = cc % 2
                        k2v = k2scr[:, kp * 512:(kp + 1) * 512]
                        tk2 = VC(lambda e, nsrc=nsrc, k2v=k2v: e.tensor_tensor(
                            k2v, nsrc, nsrc, ALU.mult),
                            waits=k2S[kp].ww())
                        k2S[kp].wrote(tk2)
                        nsrcS.read(tk2)
                        pknv = pkn[:, kp * 512:(kp + 1) * 512]
                        mmn = TC(lambda e, k2v=k2v, pknv=pknv: e.matmul(
                            pknv, onesM[:], k2v, start=True, stop=True),
                                 waits=k2S[kp].rw() + pknS[kp].ww())
                        k2S[kp].read(mmn)
                        pknS[kp].wrote(mmn)
                        slot = (0 if kind == "k" else 64) + cc * 4 + tb
                        trm = VC(lambda e, pknv=pknv, slot=slot: e.reduce_max(
                            m2all[0:1, slot:slot + 1], pknv[0:1, :], axis=AX),
                                 waits=pknS[kp].rw())
                        m2allS.wrote(trm, reset=False)
                        pknS[kp].read(trm)

    proj_rope("k")
    lu = None
    for j in range(NCH):
        lu = GPD("wubp", lambda e, j=j: e.dma_start(out=wub[j], in_=wu8[j * 128:(j + 1) * 128, :]))
    for j in range(NCH):
        wubH[j].wrote(lu)
    proj_rope("q")
    ld_ = None
    for o in range(NOC):
        ld_ = GPD("wdbp", lambda e, o=o: e.dma_start(out=wdb[o], in_=wd8[o * 128:(o + 1) * 128, :]))
    for o in range(NOC):
        wdbH[o].wrote(ld_)

    for cc in range(NCH):
        t_ = VC(lambda e, cc=cc: e.reduce_max(km2[0:1, cc:cc + 1],
                                              m2all[0:1, cc * 4:cc * 4 + 4], axis=AX),
                waits=m2allS.rw() + km2S.ww())
        km2S.wrote(t_, reset=False)
        t_ = VC(lambda e, cc=cc: e.reduce_max(qm2[0:1, cc:cc + 1],
                                              m2all[0:1, 64 + cc * 4:64 + cc * 4 + 2], axis=AX),
                waits=m2allS.rw() + qm2S.ww())
        qm2S.wrote(t_, reset=False)
    tqm = SC(lambda e: e.activation(urow[:], qm2[:], AF.Sqrt, bias=zeroc[0:1, 0:1], scale=1.0),
             waits=qm2S.rw() + urowS.ww())
    urowS.wrote(tqm)
    qm2S.read(tqm)
    tkm = SC(lambda e: e.activation(u2row[:], km2[:], AF.Sqrt, bias=zeroc[0:1, 0:1], scale=1.0),
             waits=km2S.rw() + u2S.ww())
    u2S.wrote(tkm)
    km2S.read(tkm)
    if "diag_s" in dbg:
        tdq = SCD("st_diag", lambda e: e.dma_start(out=diag_s[0, 1:2, 0:16], in_=urow[:]),
                  waits=urowS.rw())
        urowS.read(tdq)
        tdk = SCD("st_diag", lambda e: e.dma_start(out=diag_s[1, 1:2, 0:16], in_=u2row[:]),
                  waits=u2S.rw())
        u2S.read(tdk)
    tun = VC(lambda e: e.scalar_tensor_tensor(urow[:], urow[:], -1.0, u2row[:],
                                              ALU.mult, ALU.mult),
             waits=urowS.rw() + u2S.rw())
    urowS.wrote(tun)
    u2S.read(tun)
    barU = P.barrier(exclude=PRECAST_CH)
    with nc.psum_tensor("pbu", [128, 16], F32) as pbu:
        pbuS = Slot(guard=barU)
        mm = TC(lambda e: e.matmul(pbu[:], onesr[0:1, :], urow[0:1, :], start=True, stop=True),
                waits=urowS.rw() + pbuS.ww())
        urowS.read(mm)
        pbuS.wrote(mm)
        ev = VC(lambda e: e.tensor_copy(negU[:], pbu[:]), waits=pbuS.rw() + negUS.ww())
        negUS.wrote(ev)
        pbuS.read(ev)
    if "diag_s" in dbg:
        tdu = SCD("st_diag", lambda e: e.dma_start(out=diagU_s[:], in_=negU[:]), waits=negUS.rw())
        negUS.read(tdu)

    sF.close()

    if stop_after == "proj":
        bar_end = P.barrier()
        P.c("scalar", lambda e: e.nop(), waits=bar_end)
        sE.close()
        return _finish(nc, P, perm)

    sG = contextlib.ExitStack()
    kfmh = sbuf(sG, "kfmh", [128, 2 * S], BF16); kfmhS = [Slot(), Slot()]
    vh = sbuf(sG, "vh", [128, 2 * S], BF16); vhS = [Slot(), Slot()]
    qsh = sbuf(sG, "qsh", [128, 2 * OWN], BF16); qshS = [Slot(), Slot()]
    qfh = sbuf(sG, "qfh", [128, 2 * OWN], BF16); qfhS = [Slot(), Slot()]
    kfbuf = sbuf(sG, "kfbuf", [128, 2 * S], BF16); kfbufS = [Slot(), Slot()]
    kftb = sbuf(sG, "kftb", [128, 2 * 128], BF16); kftbS = [Slot(), Slot()]
    memsb = sbuf(sG, "memsb", [128, 2 * 128], BF16); memsbS = [Slot(), Slot()]
    normsb = sbuf(sG, "normsb", [128, 2], BF16); normsbS = [Slot(), Slot()]
    ptile = sbuf(sG, "ptile", [128, 2 * 1024], BF16); ptileS = [Slot(), Slot()]
    rlgb = sbuf(sG, "rlgb", [128, 2 * OWN]); rlgbS = [Slot(), Slot()]
    qfsb = sbuf(sG, "qfsb", [128, 2 * OWN], BF16); qfsbS = [Slot(), Slot()]
    combb = sbuf(sG, "combb", [128, OWN], BF16); combbS = Slot()
    attnsb = sbuf(sG, "attnsb", [128, OWN]); attnsbS = Slot()
    rs2 = sbuf(sG, "rs2", [1, OWN]); rs2S = Slot()
    rsgb = sbuf(sG, "rsgb", [128, 512]); rsgbS = Slot()

    bar6 = P.barrier(exclude=PRECAST_CH)

    def pre_head(h, guard=()):
        par = h % 2
        t = SYD(f"kfmh{par}", lambda e, h=h, par=par: e.dma_start(
            out=kfmh[:, par * S:(par + 1) * S], in_=kfm_s[h]),
            waits=kfmhS[par].ww() + kfmH[h].rw() + list(guard))
        kfmhS[par].wrote(t)
        vdst = vh[:, par * S:(par + 1) * S].rearrange("p (t f) -> p t f", t=NTT)
        t = SYD(f"vh{par}", lambda e, vdst=vdst, h=h: e.dma_start(
            out=vdst, in_=vt_s[h].rearrange("t p f -> p t f")),
            waits=vhS[par].ww() + vtH[h].rw())
        vhS[par].wrote(t)
        t = SYD(f"qsh{par}", lambda e, h=h, par=par: e.dma_start(
            out=qsh[:, par * OWN:(par + 1) * OWN], in_=qs_s[h]),
            waits=qshS[par].ww() + qsH[h].rw())
        qshS[par].wrote(t)
        t = SYD(f"qfh{par}", lambda e, h=h, par=par: e.dma_start(
            out=qfh[:, par * OWN:(par + 1) * OWN], in_=qf_s[h]),
            waits=qfhS[par].ww() + qfH[h].rw())
        qfhS[par].wrote(t)

    def eluA(h):
        par = h % 2
        kfm_v = kfmh[:, par * S:(par + 1) * S]
        kfb_v = kfbuf[:, par * S:(par + 1) * S]
        t1 = VC(lambda e, kfm_v=kfm_v: e.tensor_scalar(scr[0][:], kfm_v, 0.0, None, ALU.min),
                waits=kfmhS[par].rw() + scrS[0].ww())
        scrS[0].wrote(t1)
        kfmhS[par].read(t1)
        t2 = SC(lambda e: e.activation(scr[1][:], scr[0][:], AF.Exp,
                                       bias=zeroc[:, 0:1], scale=1.0),
                waits=scrS[0].rw() + scrS[1].ww())
        scrS[1].wrote(t2)
        scrS[0].read(t2)
        t3 = VC(lambda e, kfm_v=kfm_v: e.tensor_scalar(scr[0][:], kfm_v, 0.0, None, ALU.max),
                waits=[t2] + scrS[0].ww())
        scrS[0].wrote(t3)
        kfmhS[par].read(t3)
        t4 = VC(lambda e, kfb_v=kfb_v: e.tensor_tensor(kfb_v, scr[0][:], scr[1][:], ALU.add),
                waits=kfbufS[par].ww())
        kfbufS[par].wrote(t4)
        scrS[0].read(t4)
        scrS[1].read(t4)

    def stageA(h, guard):
        par = h % 2
        vh_v = vh[:, par * S:(par + 1) * S]
        kfb_v = kfbuf[:, par * S:(par + 1) * S]
        with nc.psum_tensor(f"ptr0_{h}", [128, 128], BF16) as ptr0, \
             nc.psum_tensor(f"ptr1_{h}", [128, 128], BF16) as ptr1, \
             nc.psum_tensor(f"pmem_{h}", [128, 128], F32) as pmem, \
             nc.psum_tensor(f"pnorm_{h}", [128, 1], F32) as pnorm:
            ptrs = [ptr0, ptr1]
            ptrS = [Slot(guard=guard), Slot(guard=guard)]
            pmemS, pnormS = Slot(guard=guard), Slot(guard=guard)
            mm1 = mm2 = None
            for tt in range(NTT):
                pp = tt % 2
                tr = TC(lambda e, tt=tt, pp=pp, kfb_v=kfb_v: e.transpose(
                    ptrs[pp][:], kfb_v[:, tt * 128:(tt + 1) * 128], ident[:]),
                    waits=kfbufS[par].rw() + identS.rw() + ptrS[pp].ww())
                kfbufS[par].read(tr)
                identS.read(tr)
                ptrS[pp].wrote(tr)
                ev = VC(lambda e, pp=pp: e.tensor_copy(
                    kftb[:, pp * 128:(pp + 1) * 128], ptrs[pp][:]),
                    waits=ptrS[pp].rw() + kftbS[pp].ww())
                kftbS[pp].wrote(ev)
                ptrS[pp].read(ev)
                w1 = kftbS[pp].rw() + vhS[par].rw() + (pmemS.ww() if tt == 0 else [])
                mm1 = TC(lambda e, tt=tt, pp=pp, vh_v=vh_v: e.matmul(
                    pmem[:], kftb[:, pp * 128:(pp + 1) * 128],
                    vh_v[:, tt * 128:(tt + 1) * 128],
                    start=(tt == 0), stop=(tt == NTT - 1)), waits=w1)
                kftbS[pp].read(mm1)
                vhS[par].read(mm1)
                w2 = kftbS[pp].rw() + (pnormS.ww() if tt == 0 else [])
                mm2 = TC(lambda e, tt=tt, pp=pp: e.matmul(
                    pnorm[:], kftb[:, pp * 128:(pp + 1) * 128], onescb[:],
                    start=(tt == 0), stop=(tt == NTT - 1)), waits=w2)
                kftbS[pp].read(mm2)
            pmemS.wrote(mm1)
            pnormS.wrote(mm2)
            evm = VC(lambda e, par=par: e.tensor_copy(
                memsb[:, par * 128:(par + 1) * 128], pmem[:]),
                waits=pmemS.rw() + memsbS[par].ww())
            memsbS[par].wrote(evm)
            pmemS.read(evm)
            evn = VC(lambda e, par=par: e.tensor_copy(
                normsb[:, par:par + 1], pnorm[:]),
                waits=pnormS.rw() + normsbS[par].ww())
            normsbS[par].wrote(evn)
            pnormS.read(evn)
        return evm, evn

    def stageB(h, evm, evn):
        par = h % 2
        qf_v = qfh[:, par * OWN:(par + 1) * OWN]
        with nc.psum_tensor(f"pdl_{h}", [1, OWN], F32) as pdl, \
             nc.psum_tensor(f"pbcb_{h}", [128, OWN], F32) as pbcb:
            pdlS = Slot(guard=[evm, evn])
            pbcbS = Slot(guard=[evm, evn])
            mm = None
            for bb in range(2):
                w = normsbS[par].rw() + qfhS[par].rw() + (pdlS.ww() if bb == 0 else [])
                mm = TC(lambda e, bb=bb, qf_v=qf_v, par=par: e.matmul(
                    pdl[0:1, bb * 512:(bb + 1) * 512], normsb[:, par:par + 1],
                    qf_v[:, bb * 512:(bb + 1) * 512], start=True, stop=True), waits=w)
                normsbS[par].read(mm)
                qfhS[par].read(mm)
            pdlS.wrote(mm)
            trc = VC(lambda e: e.reciprocal(rl_r[:], pdl[0:1, :]),
                     waits=pdlS.rw() + rlS.ww())
            rlS.wrote(trc)
            pdlS.read(trc)
            trg = VC(lambda e, h=h: e.tensor_scalar(rl_r[:], rl_r[:],
                                                    grow[0:1, h:h + 1], None, ALU.mult),
                     waits=growS.rw())
            rlS.wrote(trg)
            growS.read(trg)
            mmb = None
            for bb in range(2):
                w = rlS.rw() + (pbcbS.ww() if bb == 0 else [])
                mmb = TC(lambda e, bb=bb: e.matmul(
                    pbcb[:, bb * 512:(bb + 1) * 512], onesr[0:1, :],
                    rl_r[0:1, bb * 512:(bb + 1) * 512], start=True, stop=True), waits=w)
                rlS.read(mmb)
            pbcbS.wrote(mmb)
            evb = VC(lambda e, par=par: e.tensor_copy(rlgb[:, par * OWN:(par + 1) * OWN],
                                                      pbcb[:]),
                     waits=pbcbS.rw() + rlgbS[par].ww())
            rlgbS[par].wrote(evb)
            pbcbS.read(evb)
        tqf = VC(lambda e, qf_v=qf_v, par=par: e.tensor_tensor(
            qfsb[:, par * OWN:(par + 1) * OWN], qf_v,
            rlgb[:, par * OWN:(par + 1) * OWN], ALU.mult),
            waits=qfhS[par].rw() + qfsbS[par].ww())
        qfsbS[par].wrote(tqf)
        qfhS[par].read(tqf)
        rlgbS[par].read(tqf)
        return evb

    pre_head(0, guard=bar6)
    eluA(0)
    evm0, evn0 = stageA(0, guard=bar6)
    evb0 = stageB(0, evm0, evn0)
    dguard = [evb0]
    for h in range(NHEAD):
        par = h % 2
        if h + 1 < NHEAD:
            pre_head(h + 1)
            eluA(h + 1)
        kfm_v = kfmh[:, par * S:(par + 1) * S]
        vh_v = vh[:, par * S:(par + 1) * S]
        qs_v = qsh[:, par * OWN:(par + 1) * OWN]
        with nc.psum_tensor(f"pat0_{h}", [128, 512], F32) as pat0, \
             nc.psum_tensor(f"pat1_{h}", [128, 512], F32) as pat1, \
             nc.psum_tensor(f"pden_{h}", [1, OWN], F32) as pden:
            pats = [pat0, pat1]
            patS = [Slot(guard=dguard), Slot(guard=dguard)]
            pdenS = [Slot(guard=dguard), Slot(guard=dguard)]
            lastex = None
            with nc.psum_tensor(f"psc0_{h}", [128, 1024], F32) as psc0, \
                 nc.psum_tensor(f"psc1_{h}", [128, 1024], F32) as psc1:
                pscs = [psc0, psc1]
                pscS = [Slot(guard=dguard), Slot(guard=dguard)]
                NP = NTT // 2
                sched = []
                for b in range(2):
                    for p in range(NP):
                        sched.append(("sc", b, p))
                        if p >= 1:
                            sched.append(("pvdn", b, p - 1))
                    sched.append(("pvdn", b, NP - 1))
                for item in sched:
                    kind_, b, p = item
                    pp = p % 2
                    if kind_ == "sc":
                        sc_mm = None
                        for half in range(2):
                            s = 2 * p + half
                            w = kfmhS[par].rw() + qshS[par].rw() + \
                                (pscS[pp].ww() if half == 0 else [])
                            sc_mm = TC(lambda e, s=s, b=b, pp=pp, half=half,
                                       kfm_v=kfm_v, qs_v=qs_v:
                                       e.matmul(pscs[pp][:, half * 512:(half + 1) * 512],
                                                kfm_v[:, s * 128:(s + 1) * 128],
                                                qs_v[:, b * 512:(b + 1) * 512],
                                                start=True, stop=True), waits=w)
                            kfmhS[par].read(sc_mm)
                            qshS[par].read(sc_mm)
                        pscS[pp].wrote(sc_mm)
                        ex = SC(lambda e, pp=pp, h=h: e.activation(
                            ptile[:, pp * 1024:(pp + 1) * 1024], pscs[pp][:], AF.Exp,
                            bias=negU[:, h:h + 1], scale=1.0),
                            waits=pscS[pp].rw() + ptileS[pp].ww() + negUS.rw())
                        ptileS[pp].wrote(ex)
                        pscS[pp].read(ex)
                        negUS.read(ex)
                        lastex = ex
                    else:
                        for half in range(2):
                            s = 2 * p + half
                            pt_v = ptile[:, pp * 1024 + half * 512:
                                         pp * 1024 + (half + 1) * 512]
                            w1 = ptileS[pp].rw() + vhS[par].rw() + \
                                (patS[b].ww() if s == 0 else [])
                            pv = TC(lambda e, s=s, b=b, pt_v=pt_v, vh_v=vh_v: e.matmul(
                                pats[b][:], vh_v[:, s * 128:(s + 1) * 128], pt_v,
                                start=(s == 0), stop=(s == NTT - 1)), waits=w1)
                            ptileS[pp].read(pv)
                            vhS[par].read(pv)
                            w2 = ptileS[pp].rw() + (pdenS[b].ww() if s == 0 else [])
                            dn = TC(lambda e, s=s, b=b, pt_v=pt_v: e.matmul(
                                pden[0:1, b * 512:(b + 1) * 512], onescb[:], pt_v,
                                start=(s == 0), stop=(s == NTT - 1)), waits=w2)
                            ptileS[pp].read(dn)
                            if s == NTT - 1:
                                patS[b].wrote(pv)
                                pdenS[b].wrote(dn)
            e1t = []
            for b in range(2):
                ev = VC(lambda e, b=b: e.tensor_copy(
                    attnsb[:, b * 512:(b + 1) * 512], pats[b][:]),
                    waits=patS[b].rw() + (attnsbS.ww() if b == 0 else []))
                attnsbS.wrote(ev) if b == 0 else attnsbS.add(ev)
                patS[b].read(ev)
                e1t.append(ev)
                trs = VC(lambda e, b=b: e.reciprocal(
                    rs2[0:1, b * 512:(b + 1) * 512], pden[0:1, b * 512:(b + 1) * 512]),
                    waits=pdenS[b].rw() + (rs2S.ww() if b == 0 else []))
                rs2S.wrote(trs) if b == 0 else rs2S.add(trs)
                pdenS[b].read(trs)
                e1t.append(trs)
            tg2 = VC(lambda e, h=h: e.tensor_scalar(rs2[:], rs2[:],
                                                    ogrow[0:1, h:h + 1], None, ALU.mult),
                     waits=ogrowS.rw())
            rs2S.wrote(tg2)
            ogrowS.read(tg2)
            e1t.append(tg2)
        e1t.append(lastex)
        if h + 1 < NHEAD:
            evm1, evn1 = stageA(h + 1, guard=e1t)
            evb1 = stageB(h + 1, evm1, evn1)
            dguard = [evb1]
        with nc.psum_tensor(f"pmoE_{h}", [128, 512], F32) as pmoE, \
             nc.psum_tensor(f"pbcE_{h}", [128, 512], F32) as pbcE:
            pmoES = Slot(guard=e1t)
            pbcES = Slot(guard=e1t)
            ecombs = []
            for b in range(2):
                mo = TC(lambda e, b=b, par=par: e.matmul(
                    pmoE[:], memsb[:, par * 128:(par + 1) * 128],
                    qfsb[:, par * OWN + b * 512:par * OWN + (b + 1) * 512],
                    start=True, stop=True),
                    waits=memsbS[par].rw() + qfsbS[par].rw() + pmoES.ww())
                memsbS[par].read(mo)
                qfsbS[par].read(mo)
                pmoES.wrote(mo)
                bc = TC(lambda e, b=b: e.matmul(
                    pbcE[:], onesr[0:1, :], rs2[0:1, b * 512:(b + 1) * 512],
                    start=True, stop=True),
                    waits=rs2S.rw() + pbcES.ww())
                rs2S.read(bc)
                pbcES.wrote(bc)
                evb2 = VC(lambda e, b=b: e.tensor_copy(rsgb[:], pbcE[:]),
                          waits=pbcES.rw() + rsgbS.ww())
                rsgbS.wrote(evb2)
                pbcES.read(evb2)
                tc1 = VC(lambda e, b=b: e.tensor_tensor(
                    scr[3][:, 0:512], attnsb[:, b * 512:(b + 1) * 512],
                    rsgb[:], ALU.mult),
                    waits=attnsbS.rw() + rsgbS.rw() + scrS[3].ww())
                scrS[3].wrote(tc1)
                rsgbS.read(tc1)
                attnsbS.read(tc1)
                tc2 = VC(lambda e, b=b: e.tensor_tensor(
                    combb[:, b * 512:(b + 1) * 512], scr[3][:, 0:512], pmoE[:],
                    ALU.add),
                    waits=pmoES.rw() + (combbS.ww() if b == 0 else []))
                if b == 0:
                    combbS.wrote(tc2)
                else:
                    combbS.add(tc2)
                scrS[3].read(tc2)
                pmoES.read(tc2)
                ecombs.append(tc2)
            st = SCD("st_comb", lambda e, h=h: e.dma_start(out=comb_s[h], in_=combb[:]),
                     waits=ecombs)
            combbS.read(st)
            combH[h].wrote(st)

    sG.close()

    sH = contextlib.ExitStack()
    combsl = sbuf(sH, "combsl", [128, NCH * OWN], BF16)
    combslS = [Slot() for _ in range(NCH)]
    bar7 = P.barrier()
    load_W(woT, guard=bar7)
    glast = None
    for j in range(NCH):
        glast = SYD("combsl", lambda e, j=j: e.dma_start(
            out=combsl[:, j * OWN:(j + 1) * OWN], in_=comb_s[j]),
            waits=combslS[j].ww() + combH[j].rw() + (bar7 if j < 2 else []))
        combH[j].read(glast)
    for j in range(NCH):
        combslS[j].wrote(glast)
    with nc.psum_tensor("ps0", [128, OWN], F32) as ps0, \
         nc.psum_tensor("ps1", [128, OWN], F32) as ps1:
        pss = [ps0, ps1]
        psS = [Slot(guard=bar7), Slot(guard=bar7)]
        for ic in range(NCH):
            pp = ic % 2
            ps, pS = pss[pp], psS[pp]
            lastmm = None
            for j in range(NCH):
                for bb in range(2):
                    w = combslS[j].rw() + WsbS[j].rw()
                    if j == 0 and bb == 0:
                        w += pS.ww()
                    mm = TC(lambda e, j=j, bb=bb, ic=ic, ps=ps: e.matmul(
                        ps[:, bb * 512:(bb + 1) * 512],
                        Wsb[:, j * H + ic * 128:j * H + (ic + 1) * 128],
                        combsl[:, j * OWN + bb * 512:j * OWN + (bb + 1) * 512],
                        start=(j == 0), stop=(j == NCH - 1)), waits=w)
                    combslS[j].read(mm)
                    WsbS[j].read(mm)
                    lastmm = mm
            pS.wrote(lastmm)
            ev = VC(lambda e, ps=ps, pp=pp: e.tensor_copy(
                vob[:, pp * S:pp * S + OWN], ps[:]),
                waits=pS.rw() + vobS[pp].ww())
            vobS[pp].wrote(ev)
            pS.read(ev)
            st = SCD(f"st_vob{pp}", lambda e, ic=ic, pp=pp: e.dma_start(
                out=ss_s[ic], in_=vob[:, pp * S:pp * S + OWN]),
                waits=vobS[pp].rw())
            vobS[pp].read(st)
            ssH[ic].wrote(st)
    sH.close()
    sE.close()

    sI = contextlib.ExitStack()
    xnown = sbuf(sI, "xnown", [128, NCH * OWN], BF16)
    xnownS = [Slot() for _ in range(NCH)]
    wgsl = sbuf(sI, "wgsl", [128, 2 * NCH * 256], BF16)
    wgslS = [[Slot() for _ in range(4)] for _ in range(2)]
    wusl = sbuf(sI, "wusl", [128, 2 * NCH * 256], BF16)
    wuslS = [[Slot() for _ in range(4)] for _ in range(2)]
    fuseb = sbuf(sI, "fuseb", [128, 2 * 512], BF16); fusebS = [Slot(), Slot()]
    silug = sbuf(sI, "silug", [128, 2 * 512]); silugS = [Slot(), Slot()]
    wdsl = sbuf(sI, "wdsl", [128, 2 * 1024], BF16); wdslS = [Slot(), Slot()]
    gbuf = sbuf(sI, "gbuf", [128, 2 * 512], BF16); gbufS = [Slot(), Slot()]
    xtile = sbuf(sI, "xtile", [128, 2 * 512]); xtileS = [Slot(), Slot()]
    ybuf = sbuf(sI, "ybuf", [128, 2 * 512]); ybufS = [Slot(), Slot()]
    stile = sbuf(sI, "stile", [128, 2 * 512], BF16); stileS = [Slot(), Slot()]
    fuse_sb = sbuf(sI, "fuse_sb", [128, NOC * 512], BF16)
    fuse_sbS = [Slot() for _ in range(NOC)]

    bar8 = P.barrier()
    glast = None
    for j in range(NCH):
        glast = SYD("xnown", lambda e, j=j: e.dma_start(
            out=xnown[:, j * OWN:(j + 1) * OWN], in_=xn_s[j][:, 0:OWN]),
            waits=xnownS[j].ww() + xnH[j].rw() + (bar8 if j < 2 else []))
        xnH[j].read(glast)
    for j in range(NCH):
        xnownS[j].wrote(glast)
    final_stores_by_chan = {}
    for tb in range(NOB):
        bargu = P.barrier()
        with nc.psum_tensor(f"pgu{tb}", [128, 8 * 512], F32) as pgu:
            pguS = [Slot(guard=bargu) for _ in range(8)]
            for og2 in range(32):
                gpar2 = og2 % 2
                dpar = og2 % 2
                base = gpar2 * 4
                lasts = [None] * 4
                for q in range(4):
                    wg_v = wgsl[:, (dpar * NCH + q * 4) * 256:(dpar * NCH + q * 4 + 4) * 256]
                    src_g = wgb[q * 4:(q + 1) * 4, :, og2 * 256:(og2 + 1) * 256] \
                        .rearrange("j p f -> p j f")
                    tg_ = SYD(f"wg{dpar}q{q}", lambda e, wg_v=wg_v, src_g=src_g: e.dma_start(
                        out=wg_v.rearrange("p (j f) -> p j f", j=4), in_=src_g),
                        waits=wgslS[dpar][q].ww() + wgbH[q * 4].rw())
                    wgslS[dpar][q].wrote(tg_)
                    wgbH[q * 4].read(tg_)
                    wu_v = wusl[:, (dpar * NCH + q * 4) * 256:(dpar * NCH + q * 4 + 4) * 256]
                    src_u = wub[q * 4:(q + 1) * 4, :, og2 * 256:(og2 + 1) * 256] \
                        .rearrange("j p f -> p j f")
                    tu_ = SYD(f"wu{dpar}q{q}", lambda e, wu_v=wu_v, src_u=src_u: e.dma_start(
                        out=wu_v.rearrange("p (j f) -> p j f", j=4), in_=src_u),
                        waits=wuslS[dpar][q].ww() + wubH[q * 4].rw())
                    wuslS[dpar][q].wrote(tu_)
                    wubH[q * 4].read(tu_)
                for j in range(NCH):
                    q = j // 4
                    wg_v = wgsl[:, (dpar * NCH + j) * 256:(dpar * NCH + j + 1) * 256]
                    wu_v = wusl[:, (dpar * NCH + j) * 256:(dpar * NCH + j + 1) * 256]
                    for c2 in range(2):
                        w = wgslS[dpar][q].rw() + xnownS[j].rw() + \
                            (pguS[base + c2].ww() if j == 0 else [])
                        mm = TC(lambda e, j=j, c2=c2, tb=tb, wg_v=wg_v, base=base: e.matmul(
                            pgu[:, (base + c2) * 512:(base + c2 + 1) * 512],
                            wg_v[:, c2 * 128:(c2 + 1) * 128],
                            xnown[:, j * OWN + tb * 512:j * OWN + (tb + 1) * 512],
                            start=(j == 0), stop=(j == NCH - 1)), waits=w)
                        wgslS[dpar][q].read(mm)
                        xnownS[j].read(mm)
                        lasts[c2] = mm
                        w = wuslS[dpar][q].rw() + xnownS[j].rw() + \
                            (pguS[base + 2 + c2].ww() if j == 0 else [])
                        mm = TC(lambda e, j=j, c2=c2, tb=tb, wu_v=wu_v, base=base: e.matmul(
                            pgu[:, (base + 2 + c2) * 512:(base + 3 + c2) * 512],
                            wu_v[:, c2 * 128:(c2 + 1) * 128],
                            xnown[:, j * OWN + tb * 512:j * OWN + (tb + 1) * 512],
                            start=(j == 0), stop=(j == NCH - 1)), waits=w)
                        wuslS[dpar][q].read(mm)
                        xnownS[j].read(mm)
                        lasts[2 + c2] = mm
                for c2 in range(2):
                    oc = og2 * 2 + c2
                    pguS[base + c2].wrote(lasts[c2])
                    pguS[base + 2 + c2].wrote(lasts[2 + c2])
                    fpar = c2 % 2
                    sg_ = SC(lambda e, c2=c2, oc=oc, fpar=fpar, base=base: e.activation(
                        silug[:, fpar * 512:(fpar + 1) * 512],
                        pgu[:, (base + c2) * 512:(base + c2 + 1) * 512],
                        AF.Silu, bias=zeroc[:, 0:1], scale=sgb[:, oc:oc + 1]),
                        waits=pguS[base + c2].rw() + silugS[fpar].ww() + sgS.rw())
                    silugS[fpar].wrote(sg_)
                    pguS[base + c2].read(sg_)
                    fu_ = VC(lambda e, c2=c2, oc=oc, fpar=fpar, base=base:
                             e.scalar_tensor_tensor(
                        fuse_sb[:, oc * 512:(oc + 1) * 512],
                        pgu[:, (base + 2 + c2) * 512:(base + 3 + c2) * 512],
                        sub_[:, oc:oc + 1], silug[:, fpar * 512:(fpar + 1) * 512],
                        ALU.mult, ALU.mult),
                        waits=pguS[base + 2 + c2].rw() + silugS[fpar].rw() +
                        fuse_sbS[oc].ww() + suS.rw())
                    fuse_sbS[oc].wrote(fu_)
                    silugS[fpar].read(fu_)
                    pguS[base + 2 + c2].read(fu_)
        with nc.psum_tensor(f"pd{tb}", [128, 8 * 512], F32) as pd:
            bard = P.barrier()
            pdS = [Slot(guard=bard) for _ in range(8)]
            for ig in range(2):
                lasts = [None] * 8
                for o in range(NOC):
                    dpar = o % 2
                    wd_v = wdsl[:, dpar * 1024:(dpar + 1) * 1024]
                    td = SYD(f"wd{dpar}", lambda e, o=o, ig=ig, wd_v=wd_v: e.dma_start(
                        out=wd_v, in_=wdb[o][:, ig * 1024:(ig + 1) * 1024]),
                        waits=wdslS[dpar].ww() + wdbH[o].rw())
                    wdslS[dpar].wrote(td)
                    wdbH[o].read(td)
                    ft_v = fuse_sb[:, o * 512:(o + 1) * 512]
                    for i8 in range(8):
                        w = wdslS[dpar].rw() + fuse_sbS[o].rw() + \
                            (pdS[i8].ww() if o == 0 else [])
                        mm = TC(lambda e, i8=i8, o=o, wd_v=wd_v, ft_v=ft_v: e.matmul(
                            pd[:, i8 * 512:(i8 + 1) * 512],
                            wd_v[:, i8 * 128:(i8 + 1) * 128], ft_v,
                            start=(o == 0), stop=(o == NOC - 1)), waits=w)
                        wdslS[dpar].read(mm)
                        fuse_sbS[o].read(mm)
                        lasts[i8] = mm
                for i8 in range(8):
                    ic = ig * 8 + i8
                    pdS[i8].wrote(lasts[i8])
                    gpar = i8 % 2
                    sg_ = SC(lambda e, i8=i8, ic=ic, gpar=gpar: e.activation(
                        gbuf[:, gpar * 512:(gpar + 1) * 512],
                        pd[:, i8 * 512:(i8 + 1) * 512],
                        AF.Silu, bias=zeroc[:, 0:1], scale=sdb[:, ic:ic + 1]),
                        waits=pdS[i8].rw() + gbufS[gpar].ww() + sdS.rw())
                    gbufS[gpar].wrote(sg_)
                    pdS[i8].read(sg_)
                    ts_ = SYD(f"stile{gpar}", lambda e, ic=ic, tb=tb, gpar=gpar: e.dma_start(
                        out=stile[:, gpar * 512:(gpar + 1) * 512],
                        in_=ss_s[ic][:, tb * 512:(tb + 1) * 512]),
                        waits=stileS[gpar].ww() + ssH[ic].rw())
                    stileS[gpar].wrote(ts_)
                    tx_ = SYD(f"xtile{gpar}", lambda e, ic=ic, tb=tb, gpar=gpar: e.dma_start(
                        out=xtile[:, gpar * 512:(gpar + 1) * 512],
                        in_=xT[ic * 128:(ic + 1) * 128, tb * 512:(tb + 1) * 512]),
                        waits=xtileS[gpar].ww())
                    xtileS[gpar].wrote(tx_)
                    t1 = VC(lambda e, gpar=gpar: e.tensor_tensor(
                        ybuf[:, gpar * 512:(gpar + 1) * 512],
                        stile[:, gpar * 512:(gpar + 1) * 512],
                        gbuf[:, gpar * 512:(gpar + 1) * 512], ALU.mult),
                        waits=stileS[gpar].rw() + gbufS[gpar].rw() + ybufS[gpar].ww())
                    ybufS[gpar].wrote(t1)
                    stileS[gpar].read(t1)
                    gbufS[gpar].read(t1)
                    t2 = VC(lambda e, gpar=gpar: e.tensor_tensor(
                        ybuf[:, gpar * 512:(gpar + 1) * 512],
                        ybuf[:, gpar * 512:(gpar + 1) * 512],
                        xtile[:, gpar * 512:(gpar + 1) * 512], ALU.add),
                        waits=xtileS[gpar].rw())
                    ybufS[gpar].wrote(t2)
                    xtileS[gpar].read(t2)
                    sty = SCD(f"st_y{gpar}", lambda e, ic=ic, tb=tb, gpar=gpar: e.dma_start(
                        out=yT[ic * 128:(ic + 1) * 128, tb * 512:(tb + 1) * 512],
                        in_=ybuf[:, gpar * 512:(gpar + 1) * 512]),
                        waits=ybufS[gpar].rw())
                    ybufS[gpar].read(sty)
                    final_stores_by_chan[gpar] = sty
    sI.close()

    P.c("scalar", lambda e: e.nop(), waits=list(final_stores_by_chan.values()))
    return _finish(nc, P, perm)


_NC_CACHE = {}


def _get_nc(dbg=()):
    key = tuple(sorted(dbg))
    if key not in _NC_CACHE:
        _NC_CACHE[key] = build_nc(key)
    return _NC_CACHE[key]


def make_in_maps(inputs):
    x = np.asarray(inputs["L0_attn_hidden_states"], dtype=np.float32)
    wq = np.asarray(inputs["L0_attn_Wq"], dtype=np.float32)
    wk = np.asarray(inputs["L0_attn_Wk"], dtype=np.float32)
    wv = np.asarray(inputs["L0_attn_Wv"], dtype=np.float32)
    wo = np.asarray(inputs["L0_attn_Wo"], dtype=np.float32)
    wg8 = np.ascontiguousarray(np.asarray(inputs["L0_ffn_w_gate"], dtype=np.int8))
    wu8 = np.ascontiguousarray(np.asarray(inputs["L0_ffn_w_up"], dtype=np.int8))
    wd8 = np.ascontiguousarray(np.asarray(inputs["L0_ffn_w_down"], dtype=np.int8))
    sg = np.asarray(inputs["L0_ffn_s_gate"], dtype=np.float32)
    su = np.asarray(inputs["L0_ffn_s_up"], dtype=np.float32)
    sd = np.asarray(inputs["L0_ffn_s_down"], dtype=np.float32)
    gate = np.asarray(inputs["L0_attn_gate"], dtype=np.float32)

    wqT = np.ascontiguousarray(wq.T)
    wkT = np.ascontiguousarray(wk.T)
    wvT = np.ascontiguousarray(wv.T)
    woT = np.ascontiguousarray(wo.T)
    sgI = np.ascontiguousarray(sg.reshape(64, 128).T)
    suI = np.ascontiguousarray(su.reshape(64, 128).T)
    sdI = np.ascontiguousarray(sd.reshape(16, 128).T)
    gateI = np.ascontiguousarray(gate.reshape(1, 16))

    in_maps = []
    for core in range(8):
        b, half = core // 2, core % 2
        prm = np.concatenate([np.arange(half * OWN, (half + 1) * OWN),
                              np.arange((1 - half) * OWN, (2 - half) * OWN)])
        xTc = np.ascontiguousarray(x[b].T[:, prm])
        posc = np.ascontiguousarray(
            np.tile(prm.astype(np.float32)[None, :], (128, 1)))
        in_maps.append({
            "xT": xTc, "wqT": wqT, "wkT": wkT, "wvT": wvT, "woT": woT,
            "wg8": wg8, "wu8": wu8, "wd8": wd8,
            "sgI": sgI, "suI": suI, "sdI": sdI, "gateI": gateI, "posB": posc,
        })
    return in_maps


def kernel(**inputs):
    nc = _get_nc()
    in_maps = make_in_maps(inputs)
    res = run_bass_kernel_spmd(nc, in_maps, core_ids=list(range(8)))
    y = np.empty((NB, S, H), dtype=np.float32)
    for core in range(8):
        b, half = core // 2, core % 2
        y[b, half * OWN:(half + 1) * OWN, :] = np.asarray(res.results[core]["yT"]).T
    return y


# version 16
# speedup vs baseline: 1.0180x; 1.0180x over previous
import contextlib
import math
import numpy as np
import ml_dtypes
import concourse.bass as bass
import concourse.mybir as mybir
from concourse.bass_utils import run_bass_kernel_spmd

F32 = mybir.dt.float32
BF16 = mybir.dt.bfloat16
I8 = mybir.dt.int8
I32 = mybir.dt.int32
AF = mybir.ActivationFunctionType
ALU = mybir.AluOpType
AX = mybir.AxisListType.X

H = 2048
S = 2048
OWN = 1024
NB = 4
NHEAD = 16
INTER = 8192
NCH = 16
NTT = 16
NTB = 4
NOB = 2
NOC = 64

TWO_PI = 2.0 * math.pi
CW_C1 = float(np.float32(6.28125))
CW_C2 = float(np.float32(TWO_PI - CW_C1))
CW_C3 = float(np.float32(TWO_PI - CW_C1 - np.float64(np.float32(TWO_PI - CW_C1))))
INV_2PI = 1.0 / TWO_PI
PI = math.pi
PI_LO = float(np.float32(3.1415925))
HALF_PI = float(np.float32(PI / 2.0))
TWO_PI_F = float(np.float32(TWO_PI))
SCALE_QK = 1.0 / math.sqrt(128.0)

ENGS = ("sync", "tensor", "vector", "scalar", "gpsimd")
CSEMS = ("tensor.c", "vector.c", "scalar.c", "gpsimd.c")


class _Op:
    __slots__ = ("eng", "sem", "amt", "fn", "waits", "need", "val")

    def __init__(self, eng, sem, amt, fn, waits, need):
        self.eng, self.sem, self.amt, self.fn = eng, sem, amt, fn
        self.waits, self.need, self.val = waits, need, None


class Prog:

    def __init__(self):
        self.streams = {e: [] for e in ENGS}
        self.chans = {}

    def _emit(self, eng, sem, amt, fn, waits, need):
        kept = []
        for w in waits:
            if w is None:
                continue
            if w.eng == eng and w.sem in CSEMS and sem in CSEMS:
                continue
            kept.append(w)
        op = _Op(eng, sem, amt, fn, kept, need)
        for w in kept:
            w.need = True
        self.streams[eng].append(op)
        return op

    def c(self, eng, fn, waits=()):
        return self._emit(eng, eng + ".c", 1, fn, list(waits), False)

    def d(self, eng, chan, fn, waits=()):
        op = self._emit(eng, chan, 16, fn, list(waits), True)
        n = self.chans.get(chan, 0) + 16
        self.chans[chan] = n
        op.val = n
        return op

    def barrier(self, exclude=()):
        out = []
        seen = {}
        for e in ENGS:
            for op in reversed(self.streams[e]):
                if op.sem in CSEMS:
                    op.need = True
                    out.append(op)
                    break
            for op in reversed(self.streams[e]):
                if op.sem not in CSEMS and op.sem not in seen \
                        and op.sem not in exclude:
                    seen[op.sem] = op
        out.extend(seen.values())
        return out

    def resolve(self):
        counts = {s: 0 for s in CSEMS}
        for e in ENGS:
            for op in self.streams[e]:
                if op.need and op.sem in CSEMS:
                    counts[op.sem] += op.amt
                    op.val = counts[op.sem]
        return counts

    def replay(self, eng_obj, eng_name, semh):
        last = {}
        for op in self.streams[eng_name]:
            for w in op.waits:
                if w.val is None:
                    raise RuntimeError("unresolved wait")
                if last.get(w.sem, 0) < w.val:
                    eng_obj.wait_ge(semh[w.sem], w.val)
                    last[w.sem] = w.val
            inst = op.fn(eng_obj)
            if op.need:
                inst.then_inc(semh[op.sem], op.amt)


PRECAST_CH = ("wgbp", "wubp", "wdbp")


class Slot:

    def __init__(self, guard=()):
        self.prods = []
        self.cons = {}
        self.guards = [g for g in guard if g is not None]

    def ww(self):
        return self.guards + list(self.cons.values()) + self.prods

    def wrote(self, t, reset=True):
        if reset:
            self.prods = [t]
            self.cons = {}
            self.guards = []
        else:
            self.prods.append(t)

    def add(self, t):
        self.prods.append(t)

    def rw(self):
        return list(self.prods)

    def read(self, t):
        self.cons[t.eng] = t


def _finish(nc, P, perm):
    P.resolve()
    semh = {}
    for s in CSEMS:
        semh[s] = perm.enter_context(nc.semaphore(s.replace(".", "_")))
    for s in P.chans:
        semh[s] = perm.enter_context(nc.semaphore("ch_" + s))
    with nc.Block() as block:
        @block.sync
        def _(e):
            P.replay(e, "sync", semh)

        @block.tensor
        def _(e):
            P.replay(e, "tensor", semh)

        @block.vector
        def _(e):
            P.replay(e, "vector", semh)

        @block.scalar
        def _(e):
            P.replay(e, "scalar", semh)

        @block.gpsimd
        def _(e):
            P.replay(e, "gpsimd", semh)

    perm.close()
    return nc


def build_nc(dbg=(), stop_after=None):
    nc = bass.Bass("TRN2", target_bir_lowering=False, detect_race_conditions=False)
    P = Prog()

    def dram(name, shape, dtype, kind="Internal"):
        k = "ExternalOutput" if (name in dbg and kind == "Internal") else kind
        return nc.dram_tensor(name, shape, dtype, kind=k).ap()

    xT = dram("xT", [H, S], F32, "ExternalInput")
    wqT = dram("wqT", [H, H], F32, "ExternalInput")
    wkT = dram("wkT", [H, H], F32, "ExternalInput")
    wvT = dram("wvT", [H, H], F32, "ExternalInput")
    woT = dram("woT", [H, H], F32, "ExternalInput")
    wg8 = dram("wg8", [H, INTER], I8, "ExternalInput")
    wu8 = dram("wu8", [H, INTER], I8, "ExternalInput")
    wd8 = dram("wd8", [INTER, H], I8, "ExternalInput")
    sgI = dram("sgI", [128, 64], F32, "ExternalInput")
    suI = dram("suI", [128, 64], F32, "ExternalInput")
    sdI = dram("sdI", [128, 16], F32, "ExternalInput")
    gateI = dram("gateI", [1, 16], F32, "ExternalInput")
    posB = dram("posB", [128, S], F32, "ExternalInput")
    yT = dram("yT", [H, OWN], F32, "ExternalOutput")

    xn_s = dram("xn_s", [NCH, 128, S], BF16)
    tbl_s = dram("tbl_s", [8, 2, 128, S], BF16)
    kfm_s = dram("kfm_s", [NCH, 128, S], BF16)
    qs_s = dram("qs_s", [NCH, 128, OWN], BF16)
    qf_s = dram("qf_s", [NCH, 128, OWN], BF16)
    vt_s = dram("vt_s", [NCH, NTT, 128, 128], BF16)
    wgb = dram("wgb", [NCH, 128, INTER], BF16)
    wub = dram("wub", [NCH, 128, INTER], BF16)
    wdb = dram("wdb", [NOC, 128, H], BF16)
    fuse_s = dram("fuse_s", [NOC, 128, OWN], BF16)
    comb_s = dram("comb_s", [NCH, 128, OWN], BF16)
    ss_s = dram("ss_s", [NCH, 128, OWN], BF16)
    diag_s = dram("diag_s", [NHEAD, 4, 1024], F32)
    diagU_s = dram("diagU_s", [128, 16], F32)

    xnH = [Slot() for _ in range(NCH)]
    tblH = [[Slot(), Slot()] for _ in range(8)]
    kfmH = [Slot() for _ in range(NCH)]
    qsH = [Slot() for _ in range(NCH)]
    qfH = [Slot() for _ in range(NCH)]
    vtH = [Slot() for _ in range(NCH)]
    wgbH = [Slot() for _ in range(NCH)]
    wubH = [Slot() for _ in range(NCH)]
    wdbH = [Slot() for _ in range(NOC)]
    fuseH = [Slot() for _ in range(NOC)]
    combH = [Slot() for _ in range(NCH)]
    ssH = [Slot() for _ in range(NCH)]

    perm = contextlib.ExitStack()

    def sbuf(stk, name, shape, dt=F32):
        return stk.enter_context(nc.sbuf_tensor(name, shape, dt))

    scr = [sbuf(perm, f"scr{i}", [128, S]) for i in range(4)]
    scrS = [Slot() for _ in range(4)]
    tblb = sbuf(perm, "tblb", [128, 2 * S], BF16)
    tblbS = [Slot() for _ in range(4)]
    tblfS = [Slot(), Slot()]
    omgc = sbuf(perm, "omgc", [128, 8]); omgcS = Slot()
    grow = sbuf(perm, "grow", [1, 16]); growS = Slot()
    ogrow = sbuf(perm, "ogrow", [1, 16]); ogrowS = Slot()
    qm2 = sbuf(perm, "qm2", [1, 16]); qm2S = Slot()
    km2 = sbuf(perm, "km2", [1, 16]); km2S = Slot()
    urow = sbuf(perm, "urow", [1, 16]); urowS = Slot()
    u2row = sbuf(perm, "u2row", [1, 16]); u2S = Slot()
    tmp11 = sbuf(perm, "tmp11", [1, 2]); tmp11S = Slot()
    m2all = sbuf(perm, "m2all", [1, 128]); m2allS = Slot()
    negU = sbuf(perm, "negU", [128, 16]); negUS = Slot()
    rl_r = sbuf(perm, "rl_r", [1, 1024]); rlS = Slot()
    rs_r = sbuf(perm, "rs_r", [1, 512]); rsS = Slot()
    onesc = sbuf(perm, "onesc", [128, 1])
    onescb = sbuf(perm, "onescb", [128, 1], BF16)
    onesM = sbuf(perm, "onesM", [128, 128], BF16)
    onesr = sbuf(perm, "onesr", [1, 128])
    zeroc = sbuf(perm, "zeroc", [128, 1])
    epsc = sbuf(perm, "epsc", [128, 1])
    pjt = sbuf(perm, "pjt", [128, 128])
    ident = sbuf(perm, "ident", [128, 128], BF16); identS = Slot()
    constS = Slot()
    sgb = sbuf(perm, "sgb", [128, 64]); sgS = Slot()
    sub_ = sbuf(perm, "sub_", [128, 64]); suS = Slot()
    sdb = sbuf(perm, "sdb", [128, 16]); sdS = Slot()
    k2scr = sbuf(perm, "k2scr", [128, 2 * 512], BF16); k2S = [Slot(), Slot()]

    SYD = lambda ch, fn, waits=(): P.d("sync", ch, fn, waits)
    GPD = lambda ch, fn, waits=(): P.d("gpsimd", ch, fn, waits)
    SCD = lambda ch, fn, waits=(): P.d("scalar", ch, fn, waits)
    TC = lambda fn, waits=(): P.c("tensor", fn, waits)
    VC = lambda fn, waits=(): P.c("vector", fn, waits)
    SC = lambda fn, waits=(): P.c("scalar", fn, waits)
    GC = lambda fn, waits=(): P.c("gpsimd", fn, waits)

    s12 = contextlib.ExitStack()

    sE = contextlib.ExitStack()
    Wsb = sbuf(sE, "Wsb", [128, NCH * H], BF16)
    WsbS = [Slot() for _ in range(NCH)]
    vob = sbuf(sE, "vob", [128, 2 * S], BF16)
    vobS = [Slot(), Slot()]

    def load_W(src, guard=()):
        gw = list(guard)
        for g in range(4):
            last = None
            for j in range(g * 4, (g + 1) * 4):
                last = GPD(f"wsbg{g}", lambda e, j=j, src=src: e.dma_start(
                    out=Wsb[:, j * H:(j + 1) * H], in_=src[j * 128:(j + 1) * 128, :]),
                    waits=WsbS[j].ww() + gw)
            for j in range(g * 4, (g + 1) * 4):
                WsbS[j].wrote(last)

    t = GC(lambda e: e.iota(omgc[:], [[128, 8]], channel_multiplier=1,
                            allow_small_or_imprecise_dtypes=True))
    omgcS.wrote(t)
    pjS = Slot()
    t = GC(lambda e: e.iota(pjt[:], [[-1, 128]], channel_multiplier=1,
                            allow_small_or_imprecise_dtypes=True))
    pjS.wrote(t)
    load_W(wvT)
    lg = ld_ = None
    for j in range(NCH):
        lg = GPD("wgbp", lambda e, j=j: e.dma_start(out=wgb[j], in_=wg8[j * 128:(j + 1) * 128, :]))
    for j in range(NCH):
        wgbH[j].wrote(lg)
    for o in range(NOC):
        ld_ = GPD("wdbp", lambda e, o=o: e.dma_start(out=wdb[o], in_=wd8[o * 128:(o + 1) * 128, :]))
    for o in range(NOC):
        wdbH[o].wrote(ld_)
    c1 = SYD("consts", lambda e: e.dma_start(out=grow[:], in_=gateI[:]))
    c2 = SYD("consts", lambda e: e.dma_start(out=sgb[:], in_=sgI[:]))
    c3 = SYD("consts", lambda e: e.dma_start(out=sub_[:], in_=suI[:]))
    c4 = SYD("consts", lambda e: e.dma_start(out=sdb[:], in_=sdI[:]))
    for _s in (growS, sgS, suS, sdS):
        _s.wrote(c4)
    VC(lambda e: e.memset(onesc[:], 1.0))
    VC(lambda e: e.memset(onesM[:], 1.0))
    VC(lambda e: e.memset(onescb[:], 1.0))
    VC(lambda e: e.memset(onesr[:], 1.0))
    VC(lambda e: e.memset(zeroc[:], 0.0))
    VC(lambda e: e.memset(epsc[:], 1e-5))
    qm2S.wrote(VC(lambda e: e.memset(qm2[:], 0.0)))
    km2S.wrote(VC(lambda e: e.memset(km2[:], 0.0)))
    VC(lambda e: e.memset(tmp11[:], 0.0))
    t = VC(lambda e: e.tensor_scalar(ident[:], pjt[:], 0.0, None, ALU.is_equal),
           waits=pjS.rw())
    identS.wrote(t)
    constS.wrote(t)
    tg = SC(lambda e: e.activation(grow[:], grow[:], AF.Sigmoid,
                                   bias=zeroc[0:1, 0:1], scale=1.0),
            waits=growS.rw() + constS.rw())
    growS.wrote(tg)
    tog = VC(lambda e: e.tensor_scalar(ogrow[:], grow[:], -1.0, 1.0, ALU.mult, ALU.add),
             waits=growS.rw())
    ogrowS.wrote(tog)
    growS.read(tog)
    tw = SC(lambda e: e.activation(omgc[:], omgc[:], AF.Exp, bias=zeroc[:, 0:1],
                                   scale=-math.log(10000.0) / 1024.0),
            waits=omgcS.rw() + constS.rw())
    omgcS.wrote(tw)


    rows_a = sbuf(s12, "rows_a", [1, S]); rowsaS = Slot()
    rows_b = sbuf(s12, "rows_b", [1, S]); rowsbS = Slot()
    rb_b = sbuf(s12, "rb_b", [128, S]); rbS = Slot()
    nm_b = sbuf(s12, "nm_b", [128, S]); nmS = Slot()
    xnp = sbuf(s12, "xnp", [128, 2 * S], BF16); xnpS = [Slot(), Slot()]

    with nc.psum_tensor("pstx", [1, S], F32) as pstx, \
         nc.psum_tensor("pstx2", [1, S], F32) as pstx2:
        pstxS, pstx2S = Slot(), Slot()
        lastx = lastx2 = None
        for j in range(NCH):
            xb, xbS = scr[j % 2], scrS[j % 2]
            t = (SYD if j % 2 == 0 else SCD)(
                f"scrx{j % 2}", lambda e, j=j, xb=xb: e.dma_start(
                    out=xb[:], in_=xT[j * 128:(j + 1) * 128, :]), waits=xbS.ww())
            xbS.wrote(t)
            tq = VC(lambda e, xb=xb: e.tensor_tensor(scr[2][:], xb[:], xb[:], ALU.mult),
                    waits=xbS.rw() + scrS[2].ww())
            xbS.read(tq)
            scrS[2].wrote(tq)
            for blk in range(4):
                mm = TC(lambda e, xb=xb, blk=blk, j=j: e.matmul(
                    pstx[0:1, blk * 512:(blk + 1) * 512], onesc[:],
                    xb[:, blk * 512:(blk + 1) * 512],
                    start=(j == 0), stop=(j == NCH - 1)),
                    waits=xbS.rw() + (constS.rw() if j == 0 and blk == 0 else []))
                xbS.read(mm)
                lastx = mm
            for blk in range(4):
                mm = TC(lambda e, blk=blk, j=j: e.matmul(
                    pstx2[0:1, blk * 512:(blk + 1) * 512], onesc[:],
                    scr[2][:, blk * 512:(blk + 1) * 512],
                    start=(j == 0), stop=(j == NCH - 1)), waits=scrS[2].rw())
                scrS[2].read(mm)
                lastx2 = mm
        pstxS.wrote(lastx)
        pstx2S.wrote(lastx2)
        tm = VC(lambda e: e.tensor_scalar(rows_b[:], pstx[0:1, :], 1.0 / H, None, ALU.mult),
                waits=pstxS.rw())
        rowsbS.wrote(tm)
        pstxS.read(tm)
        te = VC(lambda e: e.tensor_scalar(rows_a[:], pstx2[0:1, :], 1.0 / H, None, ALU.mult),
                waits=pstx2S.rw())
        rowsaS.wrote(te)
        pstx2S.read(te)
        t3 = VC(lambda e: e.tensor_tensor(pstx[0:1, :], rows_b[:], rows_b[:], ALU.mult))
        pstxS.wrote(t3)
        t4 = VC(lambda e: e.tensor_tensor(rows_a[:], rows_a[:], pstx[0:1, :], ALU.subtract))
        rowsaS.wrote(t4)
        pstxS.read(t4)
        t5 = SC(lambda e: e.activation(pstx2[0:1, :], rows_a[:], AF.Sqrt,
                                       bias=epsc[0:1, 0:1], scale=1.0),
                waits=rowsaS.rw() + pstx2S.rw())
        pstx2S.wrote(t5)
        rowsaS.read(t5)
        t6 = VC(lambda e: e.reciprocal(rows_a[:], pstx2[0:1, :]),
                waits=pstx2S.rw() + rowsaS.ww())
        rowsaS.wrote(t6)
        pstx2S.read(t6)
        t7 = VC(lambda e: e.scalar_tensor_tensor(rows_b[:], rows_b[:], -1.0, rows_a[:],
                                                 ALU.mult, ALU.mult))
        rowsbS.wrote(t7)
        rowsaS.read(t7)
    with nc.psum_tensor("pbc1", [128, S], F32) as pbc1:
        pbc1S = Slot(guard=[t7])
        lastmm = None
        for blk in range(4):
            mm = TC(lambda e, blk=blk: e.matmul(
                pbc1[:, blk * 512:(blk + 1) * 512], onesr[0:1, :],
                rows_a[0:1, blk * 512:(blk + 1) * 512], start=True, stop=True),
                waits=rowsaS.rw() + (pbc1S.ww() if blk == 0 else []))
            rowsaS.read(mm)
            lastmm = mm
        pbc1S.wrote(lastmm)
        ev = VC(lambda e: e.tensor_copy(rb_b[:], pbc1[:]), waits=pbc1S.rw())
        rbS.wrote(ev)
        pbc1S.read(ev)
        lastmm = None
        for blk in range(4):
            mm = TC(lambda e, blk=blk: e.matmul(
                pbc1[:, blk * 512:(blk + 1) * 512], onesr[0:1, :],
                rows_b[0:1, blk * 512:(blk + 1) * 512], start=True, stop=True),
                waits=rowsbS.rw() + [pbc1S.cons.get("vector")])
            rowsbS.read(mm)
            lastmm = mm
        pbc1S.wrote(lastmm)
        ev = VC(lambda e: e.tensor_copy(nm_b[:], pbc1[:]), waits=pbc1S.rw())
        nmS.wrote(ev)
        pbc1S.read(ev)

    cp_ = SYD("posbL", lambda e: e.dma_start(out=scr[3][:], in_=posB[:]),
              waits=scrS[3].ww())
    scrS[3].wrote(cp_)

    for j in range(NCH):
        xb, xbS = scr[j % 2], scrS[j % 2]
        t = (SYD if j % 2 == 0 else SCD)(
            f"scrx{j % 2}", lambda e, j=j, xb=xb: e.dma_start(
                out=xb[:], in_=xT[j * 128:(j + 1) * 128, :]), waits=xbS.ww())
        xbS.wrote(t)
        t1 = VC(lambda e, xb=xb: e.tensor_tensor(scr[2][:], xb[:], rb_b[:], ALU.mult),
                waits=xbS.rw() + rbS.rw() + scrS[2].ww())
        xbS.read(t1)
        rbS.read(t1)
        scrS[2].wrote(t1)
        par = j % 2
        xo = xnp[:, par * S:(par + 1) * S]
        t2 = VC(lambda e, xo=xo: e.tensor_tensor(xo, scr[2][:], nm_b[:], ALU.add),
                waits=nmS.rw() + xnpS[par].ww())
        nmS.read(t2)
        scrS[2].read(t2)
        xnpS[par].wrote(t2)
        st = SCD(f"st_xnp{par}", lambda e, j=j, xo=xo: e.dma_start(out=xn_s[j], in_=xo),
                 waits=xnpS[par].rw())
        xnpS[par].read(st)
        xnH[j].wrote(st)

    def emit_table_block(pr):
        if True:
            a0 = SC(lambda e, pr=pr: e.activation(scr[2][:], scr[3][:], AF.Copy,
                                                  bias=0.0,
                                                  scale=omgc[:, pr:pr + 1]),
                    waits=scrS[2].ww() + scrS[3].rw() + omgcS.rw())
            scrS[2].wrote(a0)
            scrS[3].read(a0)
            omgcS.read(a0)
            t1 = VC(lambda e: e.tensor_scalar(i32scr[:], scr[2][:], INV_2PI, None,
                                              ALU.mult),
                    waits=scrS[2].rw() + i32S.ww())
            i32S.wrote(t1)
            scrS[2].read(t1)
            t3 = SC(lambda e: e.copy(scr[1][:], i32scr[:]),
                    waits=i32S.rw() + scrS[1].ww())
            scrS[1].wrote(t3)
            i32S.read(t3)
            t4a = VC(lambda e: e.scalar_tensor_tensor(scr[2][:], scr[1][:], -CW_C1,
                                                      scr[2][:], ALU.mult, ALU.add),
                     waits=scrS[1].rw() + scrS[2].rw())
            scrS[2].wrote(t4a)
            t4b = VC(lambda e: e.scalar_tensor_tensor(scr[2][:], scr[1][:], -CW_C2,
                                                      scr[2][:], ALU.mult, ALU.add))
            scrS[2].wrote(t4b)
            t4 = VC(lambda e: e.scalar_tensor_tensor(scr[2][:], scr[1][:], -CW_C3,
                                                     scr[2][:], ALU.mult, ALU.add))
            scrS[2].wrote(t4)
            scrS[1].read(t4)
            wh1 = VC(lambda e: e.tensor_scalar(scr[0][:], scr[2][:], PI_LO, None,
                                               ALU.is_gt), waits=scrS[0].ww())
            scrS[0].wrote(wh1)
            wh2 = VC(lambda e: e.scalar_tensor_tensor(scr[2][:], scr[0][:], -TWO_PI_F,
                                                      scr[2][:], ALU.mult, ALU.add))
            scrS[2].wrote(wh2)
            scrS[0].read(wh2)
            wl1 = VC(lambda e: e.tensor_scalar(scr[0][:], scr[2][:], -PI_LO, None,
                                               ALU.is_lt), waits=scrS[0].ww())
            scrS[0].wrote(wl1)
            wl2 = VC(lambda e: e.scalar_tensor_tensor(scr[2][:], scr[0][:], TWO_PI_F,
                                                      scr[2][:], ALU.mult, ALU.add))
            scrS[2].wrote(wl2)
            scrS[0].read(wl2)
            for ti in (0, 1):
                if ti == 0:
                    t5 = VC(lambda e: e.tensor_scalar(scr[0][:], scr[2][:],
                                                      PI_LO, -PI_LO, ALU.min, ALU.max),
                            waits=scrS[0].ww())
                else:
                    tm_ = VC(lambda e: e.tensor_scalar(scr[0][:], scr[2][:],
                                                       HALF_PI, None, ALU.is_gt),
                             waits=scrS[0].ww())
                    scrS[0].wrote(tm_)
                    tw_ = VC(lambda e: e.scalar_tensor_tensor(scr[0][:], scr[0][:],
                                                              -TWO_PI_F, scr[2][:],
                                                              ALU.mult, ALU.add))
                    scrS[0].wrote(tw_)
                    t5 = VC(lambda e: e.tensor_scalar(scr[0][:], scr[0][:],
                                                      HALF_PI, PI_LO, ALU.add, ALU.min))
                scrS[0].wrote(t5)
                scrS[2].read(t5)
                dst = tblb[:, ti * S:(ti + 1) * S]
                s5 = SC(lambda e, dst=dst: e.activation(dst, scr[0][:], AF.Sin,
                                                        bias=zeroc[:, 0:1], scale=1.0),
                        waits=scrS[0].rw() + tblfS[ti].ww())
                tblfS[ti].wrote(s5)
                scrS[0].read(s5)
                d5 = SCD(f"st_tbl{ti}", lambda e, pr=pr, ti=ti, dst=dst: e.dma_start(
                    out=tbl_s[pr, ti], in_=dst), waits=tblfS[ti].rw())
                tblfS[ti].read(d5)
                tblH[pr][ti].wrote(d5)

    s12.close()
    if stop_after == "tables":
        i32scr = sbuf(perm, "i32scr", [128, S], I32)
        i32S = Slot()
        for _pr in range(8):
            emit_table_block(_pr)
        bar_end = P.barrier()
        P.c("scalar", lambda e: e.nop(), waits=bar_end)
        return _finish(nc, P, perm)

    bar_pre3 = P.barrier(exclude=PRECAST_CH)
    i32scr = sbuf(sE, "i32scr", [128, S], I32)
    i32S = Slot(guard=bar_pre3)
    sF = contextlib.ExitStack()
    xncol = sbuf(sF, "xncol", [128, 2 * NCH * 512], BF16)
    xncolS = [[Slot() for _ in range(NCH)] for _ in range(2)]
    xncol_cur = {0: None, 1: None}
    kraw = sbuf(sF, "kraw", [128, NCH * 512], BF16)
    krawS = [Slot() for _ in range(NCH)]
    ropeb = sbuf(sF, "ropeb", [128, 4 * 512], BF16)
    ropebS = [Slot() for _ in range(4)]
    qsb = sbuf(sF, "qsb", [128, 2 * 512], BF16)
    qsbS = [Slot(), Slot()]
    qfb = sbuf(sF, "qfb", [128, 2 * 512], BF16)
    qfbS = [Slot(), Slot()]

    bar3 = P.barrier(exclude=PRECAST_CH)
    with nc.psum_tensor("pv0", [128, S], F32) as pv0, \
         nc.psum_tensor("pv1", [128, S], F32) as pv1:
        pvs = [pv0, pv1]
        pvS = [Slot(guard=bar3), Slot(guard=bar3)]
        first_ld = [True]
        for tb in (2, 3, 0, 1):
            par = tb % 2
            glast = None
            for j in range(NCH):
                sl = xncolS[par][j]
                glast = SYD(f"xncol{par}", lambda e, j=j, tb=tb, par=par: e.dma_start(
                    out=xncol[:, (par * NCH + j) * 512:(par * NCH + j + 1) * 512],
                    in_=xn_s[j][:, tb * 512:(tb + 1) * 512]),
                    waits=sl.ww() + xnH[j].rw() +
                    (bar_pre3 if first_ld[0] else []))
                first_ld[0] = False
                xnH[j].read(glast)
            for j in range(NCH):
                xncolS[par][j].wrote(glast)
            xncol_cur[par] = tb
            for tloc in range(4):
                tt = tb * 4 + tloc
                pp = tt % 2
                pv, pS = pvs[pp], pvS[pp]
                lastmm = None
                for j in range(NCH):
                    sl = xncolS[par][j]
                    lhs = xncol[:, (par * NCH + j) * 512 + tloc * 128:
                                (par * NCH + j) * 512 + (tloc + 1) * 128]
                    for ib in range(4):
                        w = sl.rw() + WsbS[j].rw()
                        if j == 0 and ib == 0:
                            w += pS.ww()
                        mm = TC(lambda e, lhs=lhs, j=j, ib=ib, pv=pv: e.matmul(
                            pv[:, ib * 512:(ib + 1) * 512], lhs,
                            Wsb[:, j * H + ib * 512:j * H + (ib + 1) * 512],
                            start=(j == 0), stop=(j == NCH - 1)), waits=w)
                        sl.read(mm)
                        WsbS[j].read(mm)
                        lastmm = mm
                pS.wrote(lastmm)
                ev = VC(lambda e, pv=pv, pp=pp: e.tensor_copy(
                    vob[:, pp * S:(pp + 1) * S], pv[:]),
                    waits=pS.rw() + vobS[pp].ww())
                vobS[pp].wrote(ev)
                pS.read(ev)
                st = SCD(f"st_vt{pp}", lambda e, tt=tt, pp=pp: e.dma_start(
                    out=vt_s[:, tt].rearrange("c p f -> p c f"),
                    in_=vob[:, pp * S:(pp + 1) * S].rearrange(
                        "p (c f) -> p c f", c=NCH)),
                    waits=vobS[pp].rw())
                vobS[pp].read(st)
                for ci in range(NCH):
                    vtH[ci].add(st)
                if tt % 2 == 0:
                    emit_table_block(tt // 2)

    def proj_rope(kind):
        bar = P.barrier(exclude=PRECAST_CH)
        for _i in range(4):
            tblbS[_i] = Slot(guard=bar)
        load_W(wkT if kind == "k" else wqT)
        nblk = NTB if kind == "k" else NOB
        with nc.psum_tensor(f"pk0_{kind}", [128, 512], F32) as pk0, \
             nc.psum_tensor(f"pk1_{kind}", [128, 512], F32) as pk1, \
             nc.psum_tensor(f"pkn_{kind}", [128, 1024], F32) as pkn:
            pks = [pk0, pk1]
            pkS = [Slot(guard=bar), Slot(guard=bar)]
            pknS = [Slot(guard=bar), Slot(guard=bar)]
            for tb in range(nblk):
                par = tb % 2
                if xncol_cur[par] != tb:
                    glast = None
                    for j in range(NCH):
                        sl = xncolS[par][j]
                        glast = SYD(f"xncol{par}", lambda e, j=j, tb=tb, par=par: e.dma_start(
                            out=xncol[:, (par * NCH + j) * 512:(par * NCH + j + 1) * 512],
                            in_=xn_s[j][:, tb * 512:(tb + 1) * 512]),
                            waits=sl.ww() + xnH[j].rw())
                        xnH[j].read(glast)
                    for j in range(NCH):
                        xncolS[par][j].wrote(glast)
                    xncol_cur[par] = tb
                for c in range(NCH):
                    pp = c % 2
                    pk, pS = pks[pp], pkS[pp]
                    lastmm = None
                    for j in range(NCH):
                        sl = xncolS[par][j]
                        w = sl.rw() + WsbS[j].rw()
                        if j == 0:
                            w += pS.ww()
                        mm = TC(lambda e, j=j, c=c, pk=pk, par=par: e.matmul(
                            pk[:], Wsb[:, j * H + c * 128:j * H + (c + 1) * 128],
                            xncol[:, (par * NCH + j) * 512:(par * NCH + j + 1) * 512],
                            start=(j == 0), stop=(j == NCH - 1)), waits=w)
                        sl.read(mm)
                        WsbS[j].read(mm)
                        lastmm = mm
                    pS.wrote(lastmm)
                    ev = VC(lambda e, pk=pk, c=c: e.tensor_copy(
                        kraw[:, c * 512:(c + 1) * 512], pk[:]),
                        waits=pS.rw() + krawS[c].ww())
                    krawS[c].wrote(ev)
                    pS.read(ev)
                for pr in range(8):
                    c0, c8 = pr, pr + 8
                    hp = pr % 2
                    slS, slC = tblbS[2 * hp], tblbS[2 * hp + 1]
                    sinb = tblb[:, (2 * hp) * 512:(2 * hp + 1) * 512]
                    cosb = tblb[:, (2 * hp + 1) * 512:(2 * hp + 2) * 512]
                    tS = SYD(f"tblq{2 * hp}", lambda e, pr=pr, tb=tb, sinb=sinb: e.dma_start(
                        out=sinb, in_=tbl_s[pr, 0][:, tb * 512:(tb + 1) * 512]),
                        waits=slS.ww() + tblH[pr][0].rw())
                    slS.wrote(tS)
                    tblH[pr][0].read(tS)
                    tCo = SYD(f"tblq{2 * hp + 1}", lambda e, pr=pr, tb=tb, cosb=cosb: e.dma_start(
                        out=cosb, in_=tbl_s[pr, 1][:, tb * 512:(tb + 1) * 512]),
                        waits=slC.ww() + tblH[pr][1].rw())
                    slC.wrote(tCo)
                    tblH[pr][1].read(tCo)
                    kr0 = kraw[:, c0 * 512:(c0 + 1) * 512]
                    kr8 = kraw[:, c8 * 512:(c8 + 1) * 512]
                    f0 = ropeb[:, (2 * hp) * 512:(2 * hp + 1) * 512]
                    f8 = ropeb[:, (2 * hp + 1) * 512:(2 * hp + 2) * 512]
                    f0S, f8S = ropebS[2 * hp], ropebS[2 * hp + 1]
                    sa = scr[0][:, 0:512]
                    sb2 = scr[0][:, 512:1024]
                    t1 = VC(lambda e, kr8=kr8, cosb=cosb, sa=sa: e.tensor_tensor(
                        sa, kr8, cosb, ALU.mult),
                        waits=krawS[c8].rw() + slC.rw() + scrS[0].ww())
                    krawS[c8].read(t1)
                    slC.read(t1)
                    scrS[0].wrote(t1)
                    t2 = VC(lambda e, kr0=kr0, sinb=sinb, sb2=sb2: e.tensor_tensor(
                        sb2, kr0, sinb, ALU.mult),
                        waits=krawS[c0].rw() + slS.rw())
                    krawS[c0].read(t2)
                    slS.read(t2)
                    t3 = VC(lambda e, f8=f8, sa=sa, sb2=sb2: e.tensor_tensor(
                        f8, sa, sb2, ALU.add), waits=f8S.ww())
                    f8S.wrote(t3)
                    scrS[0].read(t3)
                    t4 = VC(lambda e, kr0=kr0, cosb=cosb, sa=sa: e.tensor_tensor(
                        sa, kr0, cosb, ALU.mult),
                        waits=krawS[c0].rw() + slC.rw())
                    krawS[c0].read(t4)
                    slC.read(t4)
                    t5 = VC(lambda e, kr8=kr8, sinb=sinb, sb2=sb2: e.tensor_tensor(
                        sb2, kr8, sinb, ALU.mult),
                        waits=krawS[c8].rw() + slS.rw())
                    krawS[c8].read(t5)
                    slS.read(t5)
                    t6 = VC(lambda e, f0=f0, sa=sa, sb2=sb2: e.tensor_tensor(
                        f0, sa, sb2, ALU.subtract), waits=f0S.ww())
                    f0S.wrote(t6)
                    scrS[0].read(t6)
                    for (cc, fb, fbS) in ((c0, f0, f0S), (c8, f8, f8S)):
                        if kind == "k":
                            st = SCD(f"st_rope{2 * hp + (0 if cc == c0 else 1)}",
                                     lambda e, cc=cc, tb=tb, fb=fb: e.dma_start(
                                out=kfm_s[cc][:, tb * 512:(tb + 1) * 512], in_=fb),
                                waits=fbS.rw())
                            fbS.read(st)
                            kfmH[cc].add(st)
                            nsrc, nsrcS = fb, fbS
                        else:
                            qp = cc % 2
                            qv = qsb[:, qp * 512:(qp + 1) * 512]
                            tqs = VC(lambda e, fb=fb, qv=qv: e.tensor_scalar(
                                qv, fb, SCALE_QK, None, ALU.mult),
                                waits=qsbS[qp].ww())
                            qsbS[qp].wrote(tqs)
                            fbS.read(tqs)
                            st = SCD(f"st_qsb{qp}", lambda e, cc=cc, tb=tb, qv=qv: e.dma_start(
                                out=qs_s[cc][:, tb * 512:(tb + 1) * 512], in_=qv),
                                waits=qsbS[qp].rw())
                            qsbS[qp].read(st)
                            qsH[cc].add(st)
                            smn = scr[1][:, 0:512]
                            sex = scr[1][:, 512:1024]
                            srl = scr[1][:, 1024:1536]
                            tmn = VC(lambda e, fb=fb, smn=smn: e.tensor_scalar(
                                smn, fb, 0.0, None, ALU.min),
                                waits=scrS[1].ww())
                            scrS[1].wrote(tmn)
                            fbS.read(tmn)
                            tex = SC(lambda e, smn=smn, sex=sex: e.activation(
                                sex, smn, AF.Exp, bias=zeroc[:, 0:1], scale=1.0),
                                waits=scrS[1].rw())
                            scrS[1].wrote(tex)
                            trl = VC(lambda e, fb=fb, srl=srl: e.tensor_scalar(
                                srl, fb, 0.0, None, ALU.max),
                                waits=[tex])
                            scrS[1].wrote(trl)
                            fbS.read(trl)
                            qv2 = qfb[:, qp * 512:(qp + 1) * 512]
                            tqf = VC(lambda e, qv2=qv2, srl=srl, sex=sex: e.tensor_tensor(
                                qv2, srl, sex, ALU.add),
                                waits=qfbS[qp].ww())
                            qfbS[qp].wrote(tqf)
                            scrS[1].read(tqf)
                            st2 = SCD(f"st_qfb{qp}", lambda e, cc=cc, tb=tb, qv2=qv2: e.dma_start(
                                out=qf_s[cc][:, tb * 512:(tb + 1) * 512], in_=qv2),
                                waits=qfbS[qp].rw())
                            qfbS[qp].read(st2)
                            qfH[cc].add(st2)
                            nsrc, nsrcS = qv, qsbS[qp]
                        kp = cc % 2
                        k2v = k2scr[:, kp * 512:(kp + 1) * 512]
                        tk2 = VC(lambda e, nsrc=nsrc, k2v=k2v: e.tensor_tensor(
                            k2v, nsrc, nsrc, ALU.mult),
                            waits=k2S[kp].ww())
                        k2S[kp].wrote(tk2)
                        nsrcS.read(tk2)
                        pknv = pkn[:, kp * 512:(kp + 1) * 512]
                        mmn = TC(lambda e, k2v=k2v, pknv=pknv: e.matmul(
                            pknv, onesM[:], k2v, start=True, stop=True),
                                 waits=k2S[kp].rw() + pknS[kp].ww())
                        k2S[kp].read(mmn)
                        pknS[kp].wrote(mmn)
                        slot = (0 if kind == "k" else 64) + cc * 4 + tb
                        trm = VC(lambda e, pknv=pknv, slot=slot: e.reduce_max(
                            m2all[0:1, slot:slot + 1], pknv[0:1, :], axis=AX),
                                 waits=pknS[kp].rw())
                        m2allS.wrote(trm, reset=False)
                        pknS[kp].read(trm)

    proj_rope("k")
    lu = None
    for j in range(NCH):
        lu = GPD("wubp", lambda e, j=j: e.dma_start(out=wub[j], in_=wu8[j * 128:(j + 1) * 128, :]))
    for j in range(NCH):
        wubH[j].wrote(lu)
    proj_rope("q")

    for cc in range(NCH):
        t_ = VC(lambda e, cc=cc: e.reduce_max(km2[0:1, cc:cc + 1],
                                              m2all[0:1, cc * 4:cc * 4 + 4], axis=AX),
                waits=m2allS.rw() + km2S.ww())
        km2S.wrote(t_, reset=False)
        t_ = VC(lambda e, cc=cc: e.reduce_max(qm2[0:1, cc:cc + 1],
                                              m2all[0:1, 64 + cc * 4:64 + cc * 4 + 2], axis=AX),
                waits=m2allS.rw() + qm2S.ww())
        qm2S.wrote(t_, reset=False)
    tqm = SC(lambda e: e.activation(urow[:], qm2[:], AF.Sqrt, bias=zeroc[0:1, 0:1], scale=1.0),
             waits=qm2S.rw() + urowS.ww())
    urowS.wrote(tqm)
    qm2S.read(tqm)
    tkm = SC(lambda e: e.activation(u2row[:], km2[:], AF.Sqrt, bias=zeroc[0:1, 0:1], scale=1.0),
             waits=km2S.rw() + u2S.ww())
    u2S.wrote(tkm)
    km2S.read(tkm)
    if "diag_s" in dbg:
        tdq = SCD("st_diag", lambda e: e.dma_start(out=diag_s[0, 1:2, 0:16], in_=urow[:]),
                  waits=urowS.rw())
        urowS.read(tdq)
        tdk = SCD("st_diag", lambda e: e.dma_start(out=diag_s[1, 1:2, 0:16], in_=u2row[:]),
                  waits=u2S.rw())
        u2S.read(tdk)
    tun = VC(lambda e: e.scalar_tensor_tensor(urow[:], urow[:], -1.0, u2row[:],
                                              ALU.mult, ALU.mult),
             waits=urowS.rw() + u2S.rw())
    urowS.wrote(tun)
    u2S.read(tun)
    barU = P.barrier(exclude=PRECAST_CH)
    with nc.psum_tensor("pbu", [128, 16], F32) as pbu:
        pbuS = Slot(guard=barU)
        mm = TC(lambda e: e.matmul(pbu[:], onesr[0:1, :], urow[0:1, :], start=True, stop=True),
                waits=urowS.rw() + pbuS.ww())
        urowS.read(mm)
        pbuS.wrote(mm)
        ev = VC(lambda e: e.tensor_copy(negU[:], pbu[:]), waits=pbuS.rw() + negUS.ww())
        negUS.wrote(ev)
        pbuS.read(ev)
    if "diag_s" in dbg:
        tdu = SCD("st_diag", lambda e: e.dma_start(out=diagU_s[:], in_=negU[:]), waits=negUS.rw())
        negUS.read(tdu)

    sF.close()

    if stop_after == "proj":
        bar_end = P.barrier()
        P.c("scalar", lambda e: e.nop(), waits=bar_end)
        sE.close()
        return _finish(nc, P, perm)

    sG = contextlib.ExitStack()
    kfmh = sbuf(sG, "kfmh", [128, 2 * S], BF16); kfmhS = [Slot(), Slot()]
    vh = sbuf(sG, "vh", [128, 2 * S], BF16); vhS = [Slot(), Slot()]
    qsh = sbuf(sG, "qsh", [128, 2 * OWN], BF16); qshS = [Slot(), Slot()]
    qfh = sbuf(sG, "qfh", [128, 2 * OWN], BF16); qfhS = [Slot(), Slot()]
    kfbuf = sbuf(sG, "kfbuf", [128, 2 * S], BF16); kfbufS = [Slot(), Slot()]
    kftb = sbuf(sG, "kftb", [128, 2 * 128], BF16); kftbS = [Slot(), Slot()]
    memsb = sbuf(sG, "memsb", [128, 2 * 128], BF16); memsbS = [Slot(), Slot()]
    normsb = sbuf(sG, "normsb", [128, 2], BF16); normsbS = [Slot(), Slot()]
    ptile = sbuf(sG, "ptile", [128, 2 * 1024], BF16); ptileS = [Slot(), Slot()]
    rlgb = sbuf(sG, "rlgb", [128, 2 * OWN]); rlgbS = [Slot(), Slot()]
    qfsb = sbuf(sG, "qfsb", [128, 2 * OWN], BF16); qfsbS = [Slot(), Slot()]
    combb = sbuf(sG, "combb", [128, OWN], BF16); combbS = Slot()
    attnsb = sbuf(sG, "attnsb", [128, OWN]); attnsbS = Slot()
    rs2 = sbuf(sG, "rs2", [1, OWN]); rs2S = Slot()
    rsgb = sbuf(sG, "rsgb", [128, 512]); rsgbS = Slot()

    bar6 = P.barrier(exclude=PRECAST_CH)

    def pre_head(h, guard=()):
        par = h % 2
        t = SYD(f"kfmh{par}", lambda e, h=h, par=par: e.dma_start(
            out=kfmh[:, par * S:(par + 1) * S], in_=kfm_s[h]),
            waits=kfmhS[par].ww() + kfmH[h].rw() + list(guard))
        kfmhS[par].wrote(t)
        vdst = vh[:, par * S:(par + 1) * S].rearrange("p (t f) -> p t f", t=NTT)
        t = SYD(f"vh{par}", lambda e, vdst=vdst, h=h: e.dma_start(
            out=vdst, in_=vt_s[h].rearrange("t p f -> p t f")),
            waits=vhS[par].ww() + vtH[h].rw())
        vhS[par].wrote(t)
        t = SYD(f"qsh{par}", lambda e, h=h, par=par: e.dma_start(
            out=qsh[:, par * OWN:(par + 1) * OWN], in_=qs_s[h]),
            waits=qshS[par].ww() + qsH[h].rw())
        qshS[par].wrote(t)
        t = SYD(f"qfh{par}", lambda e, h=h, par=par: e.dma_start(
            out=qfh[:, par * OWN:(par + 1) * OWN], in_=qf_s[h]),
            waits=qfhS[par].ww() + qfH[h].rw())
        qfhS[par].wrote(t)

    def eluA(h):
        par = h % 2
        kfm_v = kfmh[:, par * S:(par + 1) * S]
        kfb_v = kfbuf[:, par * S:(par + 1) * S]
        t1 = VC(lambda e, kfm_v=kfm_v: e.tensor_scalar(scr[0][:], kfm_v, 0.0, None, ALU.min),
                waits=kfmhS[par].rw() + scrS[0].ww())
        scrS[0].wrote(t1)
        kfmhS[par].read(t1)
        t2 = SC(lambda e: e.activation(scr[1][:], scr[0][:], AF.Exp,
                                       bias=zeroc[:, 0:1], scale=1.0),
                waits=scrS[0].rw() + scrS[1].ww())
        scrS[1].wrote(t2)
        scrS[0].read(t2)
        t3 = VC(lambda e, kfm_v=kfm_v: e.tensor_scalar(scr[0][:], kfm_v, 0.0, None, ALU.max),
                waits=[t2] + scrS[0].ww())
        scrS[0].wrote(t3)
        kfmhS[par].read(t3)
        t4 = VC(lambda e, kfb_v=kfb_v: e.tensor_tensor(kfb_v, scr[0][:], scr[1][:], ALU.add),
                waits=kfbufS[par].ww())
        kfbufS[par].wrote(t4)
        scrS[0].read(t4)
        scrS[1].read(t4)

    def stageA(h, guard):
        par = h % 2
        vh_v = vh[:, par * S:(par + 1) * S]
        kfb_v = kfbuf[:, par * S:(par + 1) * S]
        with nc.psum_tensor(f"ptr0_{h}", [128, 128], BF16) as ptr0, \
             nc.psum_tensor(f"ptr1_{h}", [128, 128], BF16) as ptr1, \
             nc.psum_tensor(f"pmem_{h}", [128, 128], F32) as pmem, \
             nc.psum_tensor(f"pnorm_{h}", [128, 1], F32) as pnorm:
            ptrs = [ptr0, ptr1]
            ptrS = [Slot(guard=guard), Slot(guard=guard)]
            pmemS, pnormS = Slot(guard=guard), Slot(guard=guard)
            mm1 = mm2 = None
            for tt in range(NTT):
                pp = tt % 2
                tr = TC(lambda e, tt=tt, pp=pp, kfb_v=kfb_v: e.transpose(
                    ptrs[pp][:], kfb_v[:, tt * 128:(tt + 1) * 128], ident[:]),
                    waits=kfbufS[par].rw() + identS.rw() + ptrS[pp].ww())
                kfbufS[par].read(tr)
                identS.read(tr)
                ptrS[pp].wrote(tr)
                ev = VC(lambda e, pp=pp: e.tensor_copy(
                    kftb[:, pp * 128:(pp + 1) * 128], ptrs[pp][:]),
                    waits=ptrS[pp].rw() + kftbS[pp].ww())
                kftbS[pp].wrote(ev)
                ptrS[pp].read(ev)
                w1 = kftbS[pp].rw() + vhS[par].rw() + (pmemS.ww() if tt == 0 else [])
                mm1 = TC(lambda e, tt=tt, pp=pp, vh_v=vh_v: e.matmul(
                    pmem[:], kftb[:, pp * 128:(pp + 1) * 128],
                    vh_v[:, tt * 128:(tt + 1) * 128],
                    start=(tt == 0), stop=(tt == NTT - 1)), waits=w1)
                kftbS[pp].read(mm1)
                vhS[par].read(mm1)
                w2 = kftbS[pp].rw() + (pnormS.ww() if tt == 0 else [])
                mm2 = TC(lambda e, tt=tt, pp=pp: e.matmul(
                    pnorm[:], kftb[:, pp * 128:(pp + 1) * 128], onescb[:],
                    start=(tt == 0), stop=(tt == NTT - 1)), waits=w2)
                kftbS[pp].read(mm2)
            pmemS.wrote(mm1)
            pnormS.wrote(mm2)
            evm = VC(lambda e, par=par: e.tensor_copy(
                memsb[:, par * 128:(par + 1) * 128], pmem[:]),
                waits=pmemS.rw() + memsbS[par].ww())
            memsbS[par].wrote(evm)
            pmemS.read(evm)
            evn = VC(lambda e, par=par: e.tensor_copy(
                normsb[:, par:par + 1], pnorm[:]),
                waits=pnormS.rw() + normsbS[par].ww())
            normsbS[par].wrote(evn)
            pnormS.read(evn)
        return evm, evn

    def stageB(h, evm, evn):
        par = h % 2
        qf_v = qfh[:, par * OWN:(par + 1) * OWN]
        with nc.psum_tensor(f"pdl_{h}", [1, OWN], F32) as pdl, \
             nc.psum_tensor(f"pbcb_{h}", [128, OWN], F32) as pbcb:
            pdlS = Slot(guard=[evm, evn])
            pbcbS = Slot(guard=[evm, evn])
            mm = None
            for bb in range(2):
                w = normsbS[par].rw() + qfhS[par].rw() + (pdlS.ww() if bb == 0 else [])
                mm = TC(lambda e, bb=bb, qf_v=qf_v, par=par: e.matmul(
                    pdl[0:1, bb * 512:(bb + 1) * 512], normsb[:, par:par + 1],
                    qf_v[:, bb * 512:(bb + 1) * 512], start=True, stop=True), waits=w)
                normsbS[par].read(mm)
                qfhS[par].read(mm)
            pdlS.wrote(mm)
            trc = VC(lambda e: e.reciprocal(rl_r[:], pdl[0:1, :]),
                     waits=pdlS.rw() + rlS.ww())
            rlS.wrote(trc)
            pdlS.read(trc)
            trg = VC(lambda e, h=h: e.tensor_scalar(rl_r[:], rl_r[:],
                                                    grow[0:1, h:h + 1], None, ALU.mult),
                     waits=growS.rw())
            rlS.wrote(trg)
            growS.read(trg)
            mmb = None
            for bb in range(2):
                w = rlS.rw() + (pbcbS.ww() if bb == 0 else [])
                mmb = TC(lambda e, bb=bb: e.matmul(
                    pbcb[:, bb * 512:(bb + 1) * 512], onesr[0:1, :],
                    rl_r[0:1, bb * 512:(bb + 1) * 512], start=True, stop=True), waits=w)
                rlS.read(mmb)
            pbcbS.wrote(mmb)
            evb = VC(lambda e, par=par: e.tensor_copy(rlgb[:, par * OWN:(par + 1) * OWN],
                                                      pbcb[:]),
                     waits=pbcbS.rw() + rlgbS[par].ww())
            rlgbS[par].wrote(evb)
            pbcbS.read(evb)
        tqf = VC(lambda e, qf_v=qf_v, par=par: e.tensor_tensor(
            qfsb[:, par * OWN:(par + 1) * OWN], qf_v,
            rlgb[:, par * OWN:(par + 1) * OWN], ALU.mult),
            waits=qfhS[par].rw() + qfsbS[par].ww())
        qfsbS[par].wrote(tqf)
        qfhS[par].read(tqf)
        rlgbS[par].read(tqf)
        return evb

    pre_head(0, guard=bar6)
    eluA(0)
    evm0, evn0 = stageA(0, guard=bar6)
    evb0 = stageB(0, evm0, evn0)
    dguard = [evb0]
    for h in range(NHEAD):
        par = h % 2
        if h + 1 < NHEAD:
            pre_head(h + 1)
            eluA(h + 1)
        kfm_v = kfmh[:, par * S:(par + 1) * S]
        vh_v = vh[:, par * S:(par + 1) * S]
        qs_v = qsh[:, par * OWN:(par + 1) * OWN]
        with nc.psum_tensor(f"pat0_{h}", [128, 512], F32) as pat0, \
             nc.psum_tensor(f"pat1_{h}", [128, 512], F32) as pat1, \
             nc.psum_tensor(f"pden_{h}", [1, OWN], F32) as pden:
            pats = [pat0, pat1]
            patS = [Slot(guard=dguard), Slot(guard=dguard)]
            pdenS = [Slot(guard=dguard), Slot(guard=dguard)]
            lastex = None
            with nc.psum_tensor(f"psc0_{h}", [128, 1024], F32) as psc0, \
                 nc.psum_tensor(f"psc1_{h}", [128, 1024], F32) as psc1:
                pscs = [psc0, psc1]
                pscS = [Slot(guard=dguard), Slot(guard=dguard)]
                NP = NTT // 2
                sched = []
                for b in range(2):
                    for p in range(NP):
                        sched.append(("sc", b, p))
                        if p >= 1:
                            sched.append(("pvdn", b, p - 1))
                    sched.append(("pvdn", b, NP - 1))
                for item in sched:
                    kind_, b, p = item
                    pp = p % 2
                    if kind_ == "sc":
                        sc_mm = None
                        for half in range(2):
                            s = 2 * p + half
                            w = kfmhS[par].rw() + qshS[par].rw() + \
                                (pscS[pp].ww() if half == 0 else [])
                            sc_mm = TC(lambda e, s=s, b=b, pp=pp, half=half,
                                       kfm_v=kfm_v, qs_v=qs_v:
                                       e.matmul(pscs[pp][:, half * 512:(half + 1) * 512],
                                                kfm_v[:, s * 128:(s + 1) * 128],
                                                qs_v[:, b * 512:(b + 1) * 512],
                                                start=True, stop=True), waits=w)
                            kfmhS[par].read(sc_mm)
                            qshS[par].read(sc_mm)
                        pscS[pp].wrote(sc_mm)
                        ex = SC(lambda e, pp=pp, h=h: e.activation(
                            ptile[:, pp * 1024:(pp + 1) * 1024], pscs[pp][:], AF.Exp,
                            bias=negU[:, h:h + 1], scale=1.0),
                            waits=pscS[pp].rw() + ptileS[pp].ww() + negUS.rw())
                        ptileS[pp].wrote(ex)
                        pscS[pp].read(ex)
                        negUS.read(ex)
                        lastex = ex
                    else:
                        for half in range(2):
                            s = 2 * p + half
                            pt_v = ptile[:, pp * 1024 + half * 512:
                                         pp * 1024 + (half + 1) * 512]
                            w1 = ptileS[pp].rw() + vhS[par].rw() + \
                                (patS[b].ww() if s == 0 else [])
                            pv = TC(lambda e, s=s, b=b, pt_v=pt_v, vh_v=vh_v: e.matmul(
                                pats[b][:], vh_v[:, s * 128:(s + 1) * 128], pt_v,
                                start=(s == 0), stop=(s == NTT - 1)), waits=w1)
                            ptileS[pp].read(pv)
                            vhS[par].read(pv)
                            w2 = ptileS[pp].rw() + (pdenS[b].ww() if s == 0 else [])
                            dn = TC(lambda e, s=s, b=b, pt_v=pt_v: e.matmul(
                                pden[0:1, b * 512:(b + 1) * 512], onescb[:], pt_v,
                                start=(s == 0), stop=(s == NTT - 1)), waits=w2)
                            ptileS[pp].read(dn)
                            if s == NTT - 1:
                                patS[b].wrote(pv)
                                pdenS[b].wrote(dn)
            e1t = []
            for b in range(2):
                ev = VC(lambda e, b=b: e.tensor_copy(
                    attnsb[:, b * 512:(b + 1) * 512], pats[b][:]),
                    waits=patS[b].rw() + (attnsbS.ww() if b == 0 else []))
                attnsbS.wrote(ev) if b == 0 else attnsbS.add(ev)
                patS[b].read(ev)
                e1t.append(ev)
                trs = VC(lambda e, b=b: e.reciprocal(
                    rs2[0:1, b * 512:(b + 1) * 512], pden[0:1, b * 512:(b + 1) * 512]),
                    waits=pdenS[b].rw() + (rs2S.ww() if b == 0 else []))
                rs2S.wrote(trs) if b == 0 else rs2S.add(trs)
                pdenS[b].read(trs)
                e1t.append(trs)
            tg2 = VC(lambda e, h=h: e.tensor_scalar(rs2[:], rs2[:],
                                                    ogrow[0:1, h:h + 1], None, ALU.mult),
                     waits=ogrowS.rw())
            rs2S.wrote(tg2)
            ogrowS.read(tg2)
            e1t.append(tg2)
        e1t.append(lastex)
        if h + 1 < NHEAD:
            evm1, evn1 = stageA(h + 1, guard=e1t)
            evb1 = stageB(h + 1, evm1, evn1)
            dguard = [evb1]
        with nc.psum_tensor(f"pmoE_{h}", [128, 512], F32) as pmoE, \
             nc.psum_tensor(f"pbcE_{h}", [128, 512], F32) as pbcE:
            pmoES = Slot(guard=e1t)
            pbcES = Slot(guard=e1t)
            ecombs = []
            for b in range(2):
                mo = TC(lambda e, b=b, par=par: e.matmul(
                    pmoE[:], memsb[:, par * 128:(par + 1) * 128],
                    qfsb[:, par * OWN + b * 512:par * OWN + (b + 1) * 512],
                    start=True, stop=True),
                    waits=memsbS[par].rw() + qfsbS[par].rw() + pmoES.ww())
                memsbS[par].read(mo)
                qfsbS[par].read(mo)
                pmoES.wrote(mo)
                bc = TC(lambda e, b=b: e.matmul(
                    pbcE[:], onesr[0:1, :], rs2[0:1, b * 512:(b + 1) * 512],
                    start=True, stop=True),
                    waits=rs2S.rw() + pbcES.ww())
                rs2S.read(bc)
                pbcES.wrote(bc)
                evb2 = VC(lambda e, b=b: e.tensor_copy(rsgb[:], pbcE[:]),
                          waits=pbcES.rw() + rsgbS.ww())
                rsgbS.wrote(evb2)
                pbcES.read(evb2)
                tc1 = VC(lambda e, b=b: e.tensor_tensor(
                    scr[3][:, 0:512], attnsb[:, b * 512:(b + 1) * 512],
                    rsgb[:], ALU.mult),
                    waits=attnsbS.rw() + rsgbS.rw() + scrS[3].ww())
                scrS[3].wrote(tc1)
                rsgbS.read(tc1)
                attnsbS.read(tc1)
                tc2 = VC(lambda e, b=b: e.tensor_tensor(
                    combb[:, b * 512:(b + 1) * 512], scr[3][:, 0:512], pmoE[:],
                    ALU.add),
                    waits=pmoES.rw() + (combbS.ww() if b == 0 else []))
                if b == 0:
                    combbS.wrote(tc2)
                else:
                    combbS.add(tc2)
                scrS[3].read(tc2)
                pmoES.read(tc2)
                ecombs.append(tc2)
            st = SCD("st_comb", lambda e, h=h: e.dma_start(out=comb_s[h], in_=combb[:]),
                     waits=ecombs)
            combbS.read(st)
            combH[h].wrote(st)

    sG.close()

    sH = contextlib.ExitStack()
    combsl = sbuf(sH, "combsl", [128, NCH * OWN], BF16)
    combslS = [Slot() for _ in range(NCH)]
    bar7 = P.barrier()
    load_W(woT, guard=bar7)
    glast = None
    for j in range(NCH):
        glast = SYD("combsl", lambda e, j=j: e.dma_start(
            out=combsl[:, j * OWN:(j + 1) * OWN], in_=comb_s[j]),
            waits=combslS[j].ww() + combH[j].rw() + (bar7 if j < 2 else []))
        combH[j].read(glast)
    for j in range(NCH):
        combslS[j].wrote(glast)
    with nc.psum_tensor("ps0", [128, OWN], F32) as ps0, \
         nc.psum_tensor("ps1", [128, OWN], F32) as ps1:
        pss = [ps0, ps1]
        psS = [Slot(guard=bar7), Slot(guard=bar7)]
        for ic in range(NCH):
            pp = ic % 2
            ps, pS = pss[pp], psS[pp]
            lastmm = None
            for j in range(NCH):
                for bb in range(2):
                    w = combslS[j].rw() + WsbS[j].rw()
                    if j == 0 and bb == 0:
                        w += pS.ww()
                    mm = TC(lambda e, j=j, bb=bb, ic=ic, ps=ps: e.matmul(
                        ps[:, bb * 512:(bb + 1) * 512],
                        Wsb[:, j * H + ic * 128:j * H + (ic + 1) * 128],
                        combsl[:, j * OWN + bb * 512:j * OWN + (bb + 1) * 512],
                        start=(j == 0), stop=(j == NCH - 1)), waits=w)
                    combslS[j].read(mm)
                    WsbS[j].read(mm)
                    lastmm = mm
            pS.wrote(lastmm)
            ev = VC(lambda e, ps=ps, pp=pp: e.tensor_copy(
                vob[:, pp * S:pp * S + OWN], ps[:]),
                waits=pS.rw() + vobS[pp].ww())
            vobS[pp].wrote(ev)
            pS.read(ev)
            st = SCD(f"st_vob{pp}", lambda e, ic=ic, pp=pp: e.dma_start(
                out=ss_s[ic], in_=vob[:, pp * S:pp * S + OWN]),
                waits=vobS[pp].rw())
            vobS[pp].read(st)
            ssH[ic].wrote(st)
    sH.close()
    sE.close()

    sI = contextlib.ExitStack()
    xnown = sbuf(sI, "xnown", [128, NCH * OWN], BF16)
    xnownS = [Slot() for _ in range(NCH)]
    wgsl = sbuf(sI, "wgsl", [128, 2 * NCH * 256], BF16)
    wgslS = [[Slot() for _ in range(4)] for _ in range(2)]
    wusl = sbuf(sI, "wusl", [128, 2 * NCH * 256], BF16)
    wuslS = [[Slot() for _ in range(4)] for _ in range(2)]
    fuseb = sbuf(sI, "fuseb", [128, 2 * 512], BF16); fusebS = [Slot(), Slot()]
    silug = sbuf(sI, "silug", [128, 2 * 512]); silugS = [Slot(), Slot()]
    wdsl = sbuf(sI, "wdsl", [128, 2 * 1024], BF16); wdslS = [Slot(), Slot()]
    gbuf = sbuf(sI, "gbuf", [128, 2 * 512], BF16); gbufS = [Slot(), Slot()]
    xtile = sbuf(sI, "xtile", [128, 2 * 512]); xtileS = [Slot(), Slot()]
    ybuf = sbuf(sI, "ybuf", [128, 2 * 512]); ybufS = [Slot(), Slot()]
    stile = sbuf(sI, "stile", [128, 2 * 512], BF16); stileS = [Slot(), Slot()]
    fuse_sb = sbuf(sI, "fuse_sb", [128, NOC * 512], BF16)
    fuse_sbS = [Slot() for _ in range(NOC)]

    bar8 = P.barrier()
    glast = None
    for j in range(NCH):
        glast = SYD("xnown", lambda e, j=j: e.dma_start(
            out=xnown[:, j * OWN:(j + 1) * OWN], in_=xn_s[j][:, 0:OWN]),
            waits=xnownS[j].ww() + xnH[j].rw() + (bar8 if j < 2 else []))
        xnH[j].read(glast)
    for j in range(NCH):
        xnownS[j].wrote(glast)
    final_stores_by_chan = {}
    for tb in range(NOB):
        bargu = P.barrier()
        with nc.psum_tensor(f"pgu{tb}", [128, 8 * 512], F32) as pgu:
            pguS = [Slot(guard=bargu) for _ in range(8)]
            for og2 in range(32):
                gpar2 = og2 % 2
                dpar = og2 % 2
                base = gpar2 * 4
                lasts = [None] * 4
                for q in range(4):
                    wg_v = wgsl[:, (dpar * NCH + q * 4) * 256:(dpar * NCH + q * 4 + 4) * 256]
                    src_g = wgb[q * 4:(q + 1) * 4, :, og2 * 256:(og2 + 1) * 256] \
                        .rearrange("j p f -> p j f")
                    tg_ = SYD(f"wg{dpar}q{q}", lambda e, wg_v=wg_v, src_g=src_g: e.dma_start(
                        out=wg_v.rearrange("p (j f) -> p j f", j=4), in_=src_g),
                        waits=wgslS[dpar][q].ww() + wgbH[q * 4].rw())
                    wgslS[dpar][q].wrote(tg_)
                    wgbH[q * 4].read(tg_)
                    wu_v = wusl[:, (dpar * NCH + q * 4) * 256:(dpar * NCH + q * 4 + 4) * 256]
                    src_u = wub[q * 4:(q + 1) * 4, :, og2 * 256:(og2 + 1) * 256] \
                        .rearrange("j p f -> p j f")
                    tu_ = SYD(f"wu{dpar}q{q}", lambda e, wu_v=wu_v, src_u=src_u: e.dma_start(
                        out=wu_v.rearrange("p (j f) -> p j f", j=4), in_=src_u),
                        waits=wuslS[dpar][q].ww() + wubH[q * 4].rw())
                    wuslS[dpar][q].wrote(tu_)
                    wubH[q * 4].read(tu_)
                for j in range(NCH):
                    q = j // 4
                    wg_v = wgsl[:, (dpar * NCH + j) * 256:(dpar * NCH + j + 1) * 256]
                    wu_v = wusl[:, (dpar * NCH + j) * 256:(dpar * NCH + j + 1) * 256]
                    for c2 in range(2):
                        w = wgslS[dpar][q].rw() + xnownS[j].rw() + \
                            (pguS[base + c2].ww() if j == 0 else [])
                        mm = TC(lambda e, j=j, c2=c2, tb=tb, wg_v=wg_v, base=base: e.matmul(
                            pgu[:, (base + c2) * 512:(base + c2 + 1) * 512],
                            wg_v[:, c2 * 128:(c2 + 1) * 128],
                            xnown[:, j * OWN + tb * 512:j * OWN + (tb + 1) * 512],
                            start=(j == 0), stop=(j == NCH - 1)), waits=w)
                        wgslS[dpar][q].read(mm)
                        xnownS[j].read(mm)
                        lasts[c2] = mm
                        w = wuslS[dpar][q].rw() + xnownS[j].rw() + \
                            (pguS[base + 2 + c2].ww() if j == 0 else [])
                        mm = TC(lambda e, j=j, c2=c2, tb=tb, wu_v=wu_v, base=base: e.matmul(
                            pgu[:, (base + 2 + c2) * 512:(base + 3 + c2) * 512],
                            wu_v[:, c2 * 128:(c2 + 1) * 128],
                            xnown[:, j * OWN + tb * 512:j * OWN + (tb + 1) * 512],
                            start=(j == 0), stop=(j == NCH - 1)), waits=w)
                        wuslS[dpar][q].read(mm)
                        xnownS[j].read(mm)
                        lasts[2 + c2] = mm
                for c2 in range(2):
                    oc = og2 * 2 + c2
                    pguS[base + c2].wrote(lasts[c2])
                    pguS[base + 2 + c2].wrote(lasts[2 + c2])
                    fpar = c2 % 2
                    sg_ = SC(lambda e, c2=c2, oc=oc, fpar=fpar, base=base: e.activation(
                        silug[:, fpar * 512:(fpar + 1) * 512],
                        pgu[:, (base + c2) * 512:(base + c2 + 1) * 512],
                        AF.Silu, bias=zeroc[:, 0:1], scale=sgb[:, oc:oc + 1]),
                        waits=pguS[base + c2].rw() + silugS[fpar].ww() + sgS.rw())
                    silugS[fpar].wrote(sg_)
                    pguS[base + c2].read(sg_)
                    fu_ = VC(lambda e, c2=c2, oc=oc, fpar=fpar, base=base:
                             e.scalar_tensor_tensor(
                        fuse_sb[:, oc * 512:(oc + 1) * 512],
                        pgu[:, (base + 2 + c2) * 512:(base + 3 + c2) * 512],
                        sub_[:, oc:oc + 1], silug[:, fpar * 512:(fpar + 1) * 512],
                        ALU.mult, ALU.mult),
                        waits=pguS[base + 2 + c2].rw() + silugS[fpar].rw() +
                        fuse_sbS[oc].ww() + suS.rw())
                    fuse_sbS[oc].wrote(fu_)
                    silugS[fpar].read(fu_)
                    pguS[base + 2 + c2].read(fu_)
        with nc.psum_tensor(f"pd{tb}", [128, 8 * 512], F32) as pd:
            bard = P.barrier()
            pdS = [Slot(guard=bard) for _ in range(8)]
            for ig in range(2):
                lasts = [None] * 8
                for o in range(NOC):
                    dpar = o % 2
                    wd_v = wdsl[:, dpar * 1024:(dpar + 1) * 1024]
                    td = SYD(f"wd{dpar}", lambda e, o=o, ig=ig, wd_v=wd_v: e.dma_start(
                        out=wd_v, in_=wdb[o][:, ig * 1024:(ig + 1) * 1024]),
                        waits=wdslS[dpar].ww() + wdbH[o].rw())
                    wdslS[dpar].wrote(td)
                    wdbH[o].read(td)
                    ft_v = fuse_sb[:, o * 512:(o + 1) * 512]
                    for i8 in range(8):
                        w = wdslS[dpar].rw() + fuse_sbS[o].rw() + \
                            (pdS[i8].ww() if o == 0 else [])
                        mm = TC(lambda e, i8=i8, o=o, wd_v=wd_v, ft_v=ft_v: e.matmul(
                            pd[:, i8 * 512:(i8 + 1) * 512],
                            wd_v[:, i8 * 128:(i8 + 1) * 128], ft_v,
                            start=(o == 0), stop=(o == NOC - 1)), waits=w)
                        wdslS[dpar].read(mm)
                        fuse_sbS[o].read(mm)
                        lasts[i8] = mm
                for i8 in range(8):
                    ic = ig * 8 + i8
                    pdS[i8].wrote(lasts[i8])
                    gpar = i8 % 2
                    sg_ = SC(lambda e, i8=i8, ic=ic, gpar=gpar: e.activation(
                        gbuf[:, gpar * 512:(gpar + 1) * 512],
                        pd[:, i8 * 512:(i8 + 1) * 512],
                        AF.Silu, bias=zeroc[:, 0:1], scale=sdb[:, ic:ic + 1]),
                        waits=pdS[i8].rw() + gbufS[gpar].ww() + sdS.rw())
                    gbufS[gpar].wrote(sg_)
                    pdS[i8].read(sg_)
                    ts_ = SYD(f"stile{gpar}", lambda e, ic=ic, tb=tb, gpar=gpar: e.dma_start(
                        out=stile[:, gpar * 512:(gpar + 1) * 512],
                        in_=ss_s[ic][:, tb * 512:(tb + 1) * 512]),
                        waits=stileS[gpar].ww() + ssH[ic].rw())
                    stileS[gpar].wrote(ts_)
                    tx_ = SYD(f"xtile{gpar}", lambda e, ic=ic, tb=tb, gpar=gpar: e.dma_start(
                        out=xtile[:, gpar * 512:(gpar + 1) * 512],
                        in_=xT[ic * 128:(ic + 1) * 128, tb * 512:(tb + 1) * 512]),
                        waits=xtileS[gpar].ww())
                    xtileS[gpar].wrote(tx_)
                    t1 = VC(lambda e, gpar=gpar: e.tensor_tensor(
                        ybuf[:, gpar * 512:(gpar + 1) * 512],
                        stile[:, gpar * 512:(gpar + 1) * 512],
                        gbuf[:, gpar * 512:(gpar + 1) * 512], ALU.mult),
                        waits=stileS[gpar].rw() + gbufS[gpar].rw() + ybufS[gpar].ww())
                    ybufS[gpar].wrote(t1)
                    stileS[gpar].read(t1)
                    gbufS[gpar].read(t1)
                    t2 = VC(lambda e, gpar=gpar: e.tensor_tensor(
                        ybuf[:, gpar * 512:(gpar + 1) * 512],
                        ybuf[:, gpar * 512:(gpar + 1) * 512],
                        xtile[:, gpar * 512:(gpar + 1) * 512], ALU.add),
                        waits=xtileS[gpar].rw())
                    ybufS[gpar].wrote(t2)
                    xtileS[gpar].read(t2)
                    sty = SCD(f"st_y{gpar}", lambda e, ic=ic, tb=tb, gpar=gpar: e.dma_start(
                        out=yT[ic * 128:(ic + 1) * 128, tb * 512:(tb + 1) * 512],
                        in_=ybuf[:, gpar * 512:(gpar + 1) * 512]),
                        waits=ybufS[gpar].rw())
                    ybufS[gpar].read(sty)
                    final_stores_by_chan[gpar] = sty
    sI.close()

    P.c("scalar", lambda e: e.nop(), waits=list(final_stores_by_chan.values()))
    return _finish(nc, P, perm)


_NC_CACHE = {}


def _get_nc(dbg=()):
    key = tuple(sorted(dbg))
    if key not in _NC_CACHE:
        _NC_CACHE[key] = build_nc(key)
    return _NC_CACHE[key]


def make_in_maps(inputs):
    x = np.asarray(inputs["L0_attn_hidden_states"], dtype=np.float32)
    wq = np.asarray(inputs["L0_attn_Wq"], dtype=np.float32)
    wk = np.asarray(inputs["L0_attn_Wk"], dtype=np.float32)
    wv = np.asarray(inputs["L0_attn_Wv"], dtype=np.float32)
    wo = np.asarray(inputs["L0_attn_Wo"], dtype=np.float32)
    wg8 = np.ascontiguousarray(np.asarray(inputs["L0_ffn_w_gate"], dtype=np.int8))
    wu8 = np.ascontiguousarray(np.asarray(inputs["L0_ffn_w_up"], dtype=np.int8))
    wd8 = np.ascontiguousarray(np.asarray(inputs["L0_ffn_w_down"], dtype=np.int8))
    sg = np.asarray(inputs["L0_ffn_s_gate"], dtype=np.float32)
    su = np.asarray(inputs["L0_ffn_s_up"], dtype=np.float32)
    sd = np.asarray(inputs["L0_ffn_s_down"], dtype=np.float32)
    gate = np.asarray(inputs["L0_attn_gate"], dtype=np.float32)

    wqT = np.ascontiguousarray(wq.T)
    wkT = np.ascontiguousarray(wk.T)
    wvT = np.ascontiguousarray(wv.T)
    woT = np.ascontiguousarray(wo.T)
    sgI = np.ascontiguousarray(sg.reshape(64, 128).T)
    suI = np.ascontiguousarray(su.reshape(64, 128).T)
    sdI = np.ascontiguousarray(sd.reshape(16, 128).T)
    gateI = np.ascontiguousarray(gate.reshape(1, 16))

    in_maps = []
    for core in range(8):
        b, half = core // 2, core % 2
        prm = np.concatenate([np.arange(half * OWN, (half + 1) * OWN),
                              np.arange((1 - half) * OWN, (2 - half) * OWN)])
        xTc = np.ascontiguousarray(x[b].T[:, prm])
        posc = np.ascontiguousarray(
            np.tile(prm.astype(np.float32)[None, :], (128, 1)))
        in_maps.append({
            "xT": xTc, "wqT": wqT, "wkT": wkT, "wvT": wvT, "woT": woT,
            "wg8": wg8, "wu8": wu8, "wd8": wd8,
            "sgI": sgI, "suI": suI, "sdI": sdI, "gateI": gateI, "posB": posc,
        })
    return in_maps


def kernel(**inputs):
    nc = _get_nc()
    in_maps = make_in_maps(inputs)
    res = run_bass_kernel_spmd(nc, in_maps, core_ids=list(range(8)))
    y = np.empty((NB, S, H), dtype=np.float32)
    for core in range(8):
        b, half = core // 2, core % 2
        y[b, half * OWN:(half + 1) * OWN, :] = np.asarray(res.results[core]["yT"]).T
    return y


# version 17
# speedup vs baseline: 1.0290x; 1.0108x over previous
import contextlib
import math
import numpy as np
import ml_dtypes
import concourse.bass as bass
import concourse.mybir as mybir
from concourse.bass_utils import run_bass_kernel_spmd

F32 = mybir.dt.float32
BF16 = mybir.dt.bfloat16
I8 = mybir.dt.int8
I32 = mybir.dt.int32
AF = mybir.ActivationFunctionType
ALU = mybir.AluOpType
AX = mybir.AxisListType.X

H = 2048
S = 2048
OWN = 1024
NB = 4
NHEAD = 16
INTER = 8192
NCH = 16
NTT = 16
NTB = 4
NOB = 2
NOC = 64

TWO_PI = 2.0 * math.pi
CW_C1 = float(np.float32(6.28125))
CW_C2 = float(np.float32(TWO_PI - CW_C1))
CW_C3 = float(np.float32(TWO_PI - CW_C1 - np.float64(np.float32(TWO_PI - CW_C1))))
INV_2PI = 1.0 / TWO_PI
PI = math.pi
PI_LO = float(np.float32(3.1415925))
HALF_PI = float(np.float32(PI / 2.0))
TWO_PI_F = float(np.float32(TWO_PI))
SCALE_QK = 1.0 / math.sqrt(128.0)

ENGS = ("sync", "tensor", "vector", "scalar", "gpsimd")
CSEMS = ("tensor.c", "vector.c", "scalar.c", "gpsimd.c")


class _Op:
    __slots__ = ("eng", "sem", "amt", "fn", "waits", "need", "val")

    def __init__(self, eng, sem, amt, fn, waits, need):
        self.eng, self.sem, self.amt, self.fn = eng, sem, amt, fn
        self.waits, self.need, self.val = waits, need, None


class Prog:

    def __init__(self):
        self.streams = {e: [] for e in ENGS}
        self.chans = {}

    def _emit(self, eng, sem, amt, fn, waits, need):
        kept = []
        for w in waits:
            if w is None:
                continue
            if w.eng == eng and w.sem in CSEMS and sem in CSEMS:
                continue
            kept.append(w)
        op = _Op(eng, sem, amt, fn, kept, need)
        for w in kept:
            w.need = True
        self.streams[eng].append(op)
        return op

    def c(self, eng, fn, waits=()):
        return self._emit(eng, eng + ".c", 1, fn, list(waits), False)

    def d(self, eng, chan, fn, waits=()):
        op = self._emit(eng, chan, 16, fn, list(waits), True)
        n = self.chans.get(chan, 0) + 16
        self.chans[chan] = n
        op.val = n
        return op

    def barrier(self, exclude=()):
        out = []
        seen = {}
        for e in ENGS:
            for op in reversed(self.streams[e]):
                if op.sem in CSEMS:
                    op.need = True
                    out.append(op)
                    break
            for op in reversed(self.streams[e]):
                if op.sem not in CSEMS and op.sem not in seen \
                        and op.sem not in exclude:
                    seen[op.sem] = op
        out.extend(seen.values())
        return out

    def resolve(self):
        counts = {s: 0 for s in CSEMS}
        for e in ENGS:
            for op in self.streams[e]:
                if op.need and op.sem in CSEMS:
                    counts[op.sem] += op.amt
                    op.val = counts[op.sem]
        return counts

    def replay(self, eng_obj, eng_name, semh):
        last = {}
        for op in self.streams[eng_name]:
            for w in op.waits:
                if w.val is None:
                    raise RuntimeError("unresolved wait")
                if last.get(w.sem, 0) < w.val:
                    eng_obj.wait_ge(semh[w.sem], w.val)
                    last[w.sem] = w.val
            inst = op.fn(eng_obj)
            if op.need:
                inst.then_inc(semh[op.sem], op.amt)


PRECAST_CH = ("wgbp", "wubp", "wdbp")


class Slot:

    def __init__(self, guard=()):
        self.prods = []
        self.cons = {}
        self.guards = [g for g in guard if g is not None]

    def ww(self):
        return self.guards + list(self.cons.values()) + self.prods

    def wrote(self, t, reset=True):
        if reset:
            self.prods = [t]
            self.cons = {}
            self.guards = []
        else:
            self.prods.append(t)

    def add(self, t):
        self.prods.append(t)

    def rw(self):
        return list(self.prods)

    def read(self, t):
        self.cons[t.eng] = t


def _finish(nc, P, perm):
    P.resolve()
    semh = {}
    for s in CSEMS:
        semh[s] = perm.enter_context(nc.semaphore(s.replace(".", "_")))
    for s in P.chans:
        semh[s] = perm.enter_context(nc.semaphore("ch_" + s))
    with nc.Block() as block:
        @block.sync
        def _(e):
            P.replay(e, "sync", semh)

        @block.tensor
        def _(e):
            P.replay(e, "tensor", semh)

        @block.vector
        def _(e):
            P.replay(e, "vector", semh)

        @block.scalar
        def _(e):
            P.replay(e, "scalar", semh)

        @block.gpsimd
        def _(e):
            P.replay(e, "gpsimd", semh)

    perm.close()
    return nc


def build_nc(dbg=(), stop_after=None):
    nc = bass.Bass("TRN2", target_bir_lowering=False, detect_race_conditions=False)
    P = Prog()

    def dram(name, shape, dtype, kind="Internal"):
        k = "ExternalOutput" if (name in dbg and kind == "Internal") else kind
        return nc.dram_tensor(name, shape, dtype, kind=k).ap()

    xT = dram("xT", [H, S], F32, "ExternalInput")
    wqT = dram("wqT", [H, H], F32, "ExternalInput")
    wkT = dram("wkT", [H, H], F32, "ExternalInput")
    wvT = dram("wvT", [H, H], F32, "ExternalInput")
    woT = dram("woT", [H, H], F32, "ExternalInput")
    wg8 = dram("wg8", [H, INTER], I8, "ExternalInput")
    wu8 = dram("wu8", [H, INTER], I8, "ExternalInput")
    wd8 = dram("wd8", [INTER, H], I8, "ExternalInput")
    sgI = dram("sgI", [128, 64], F32, "ExternalInput")
    suI = dram("suI", [128, 64], F32, "ExternalInput")
    sdI = dram("sdI", [128, 16], F32, "ExternalInput")
    gateI = dram("gateI", [1, 16], F32, "ExternalInput")
    posB = dram("posB", [128, S], F32, "ExternalInput")
    yT = dram("yT", [H, OWN], F32, "ExternalOutput")

    xn_s = dram("xn_s", [NCH, 128, S], BF16)
    tbl_s = dram("tbl_s", [8, 2, 128, S], BF16)
    kfm_s = dram("kfm_s", [NCH, 128, S], BF16)
    qs_s = dram("qs_s", [NCH, 128, OWN], BF16)
    qf_s = dram("qf_s", [NCH, 128, OWN], BF16)
    vt_s = dram("vt_s", [NCH, NTT, 128, 128], BF16)
    wgb = dram("wgb", [NCH, 128, INTER], BF16)
    wub = dram("wub", [NCH, 128, INTER], BF16)
    wdb = dram("wdb", [NOC, 128, H], BF16)
    fuse_s = dram("fuse_s", [NOC, 128, OWN], BF16)
    comb_s = dram("comb_s", [NCH, 128, OWN], BF16)
    ss_s = dram("ss_s", [NCH, 128, OWN], BF16)
    diag_s = dram("diag_s", [NHEAD, 4, 1024], F32)
    diagU_s = dram("diagU_s", [128, 16], F32)

    xnH = [Slot() for _ in range(NCH)]
    tblH = [[Slot(), Slot()] for _ in range(8)]
    kfmH = [Slot() for _ in range(NCH)]
    qsH = [Slot() for _ in range(NCH)]
    qfH = [Slot() for _ in range(NCH)]
    vtH = [Slot() for _ in range(NCH)]
    wgbH = [Slot() for _ in range(NCH)]
    wubH = [Slot() for _ in range(NCH)]
    wdbH = [Slot() for _ in range(NOC)]
    fuseH = [Slot() for _ in range(NOC)]
    combH = [Slot() for _ in range(NCH)]
    ssH = [Slot() for _ in range(NCH)]

    perm = contextlib.ExitStack()

    def sbuf(stk, name, shape, dt=F32):
        return stk.enter_context(nc.sbuf_tensor(name, shape, dt))

    scr = [sbuf(perm, f"scr{i}", [128, S]) for i in range(4)]
    scrS = [Slot() for _ in range(4)]
    tblb = sbuf(perm, "tblb", [128, 2 * S], BF16)
    tblbS = [Slot() for _ in range(4)]
    tblfS = [Slot(), Slot()]
    omgc = sbuf(perm, "omgc", [128, 8]); omgcS = Slot()
    grow = sbuf(perm, "grow", [1, 16]); growS = Slot()
    ogrow = sbuf(perm, "ogrow", [1, 16]); ogrowS = Slot()
    qm2 = sbuf(perm, "qm2", [1, 16]); qm2S = Slot()
    km2 = sbuf(perm, "km2", [1, 16]); km2S = Slot()
    urow = sbuf(perm, "urow", [1, 16]); urowS = Slot()
    u2row = sbuf(perm, "u2row", [1, 16]); u2S = Slot()
    tmp11 = sbuf(perm, "tmp11", [1, 2]); tmp11S = Slot()
    m2all = sbuf(perm, "m2all", [1, 128]); m2allS = Slot()
    negU = sbuf(perm, "negU", [128, 16]); negUS = Slot()
    rl_r = sbuf(perm, "rl_r", [1, 1024]); rlS = Slot()
    rs_r = sbuf(perm, "rs_r", [1, 512]); rsS = Slot()
    onesc = sbuf(perm, "onesc", [128, 1])
    onescb = sbuf(perm, "onescb", [128, 1], BF16)
    onesM = sbuf(perm, "onesM", [128, 128], BF16)
    onesr = sbuf(perm, "onesr", [1, 128])
    zeroc = sbuf(perm, "zeroc", [128, 1])
    epsc = sbuf(perm, "epsc", [128, 1])
    pjt = sbuf(perm, "pjt", [128, 128])
    ident = sbuf(perm, "ident", [128, 128], BF16); identS = Slot()
    constS = Slot()
    sgb = sbuf(perm, "sgb", [128, 64]); sgS = Slot()
    sub_ = sbuf(perm, "sub_", [128, 64]); suS = Slot()
    sdb = sbuf(perm, "sdb", [128, 16]); sdS = Slot()
    k2scr = sbuf(perm, "k2scr", [128, 2 * 512], BF16); k2S = [Slot(), Slot()]

    SYD = lambda ch, fn, waits=(): P.d("sync", ch, fn, waits)
    GPD = lambda ch, fn, waits=(): P.d("gpsimd", ch, fn, waits)
    SCD = lambda ch, fn, waits=(): P.d("scalar", ch, fn, waits)
    TC = lambda fn, waits=(): P.c("tensor", fn, waits)
    VC = lambda fn, waits=(): P.c("vector", fn, waits)
    SC = lambda fn, waits=(): P.c("scalar", fn, waits)
    GC = lambda fn, waits=(): P.c("gpsimd", fn, waits)

    s12 = contextlib.ExitStack()

    sE = contextlib.ExitStack()
    Wsb = sbuf(sE, "Wsb", [128, NCH * H], BF16)
    WsbS = [Slot() for _ in range(NCH)]
    vob = sbuf(sE, "vob", [128, 2 * S], BF16)
    vobS = [Slot(), Slot()]

    def load_W(src, guard=()):
        gw = list(guard)
        for g in range(4):
            last = None
            for j in range(g * 4, (g + 1) * 4):
                last = GPD(f"wsbg{g}", lambda e, j=j, src=src: e.dma_start(
                    out=Wsb[:, j * H:(j + 1) * H], in_=src[j * 128:(j + 1) * 128, :]),
                    waits=WsbS[j].ww() + gw)
            for j in range(g * 4, (g + 1) * 4):
                WsbS[j].wrote(last)

    t = GC(lambda e: e.iota(omgc[:], [[128, 8]], channel_multiplier=1,
                            allow_small_or_imprecise_dtypes=True))
    omgcS.wrote(t)
    pjS = Slot()
    t = GC(lambda e: e.iota(pjt[:], [[-1, 128]], channel_multiplier=1,
                            allow_small_or_imprecise_dtypes=True))
    pjS.wrote(t)
    load_W(wvT)
    c1 = SYD("consts", lambda e: e.dma_start(out=grow[:], in_=gateI[:]))
    c2 = SYD("consts", lambda e: e.dma_start(out=sgb[:], in_=sgI[:]))
    c3 = SYD("consts", lambda e: e.dma_start(out=sub_[:], in_=suI[:]))
    c4 = SYD("consts", lambda e: e.dma_start(out=sdb[:], in_=sdI[:]))
    for _s in (growS, sgS, suS, sdS):
        _s.wrote(c4)
    VC(lambda e: e.memset(onesc[:], 1.0))
    VC(lambda e: e.memset(onesM[:], 1.0))
    VC(lambda e: e.memset(onescb[:], 1.0))
    VC(lambda e: e.memset(onesr[:], 1.0))
    VC(lambda e: e.memset(zeroc[:], 0.0))
    VC(lambda e: e.memset(epsc[:], 1e-5))
    qm2S.wrote(VC(lambda e: e.memset(qm2[:], 0.0)))
    km2S.wrote(VC(lambda e: e.memset(km2[:], 0.0)))
    VC(lambda e: e.memset(tmp11[:], 0.0))
    t = VC(lambda e: e.tensor_scalar(ident[:], pjt[:], 0.0, None, ALU.is_equal),
           waits=pjS.rw())
    identS.wrote(t)
    constS.wrote(t)
    tg = SC(lambda e: e.activation(grow[:], grow[:], AF.Sigmoid,
                                   bias=zeroc[0:1, 0:1], scale=1.0),
            waits=growS.rw() + constS.rw())
    growS.wrote(tg)
    tog = VC(lambda e: e.tensor_scalar(ogrow[:], grow[:], -1.0, 1.0, ALU.mult, ALU.add),
             waits=growS.rw())
    ogrowS.wrote(tog)
    growS.read(tog)
    tw = SC(lambda e: e.activation(omgc[:], omgc[:], AF.Exp, bias=zeroc[:, 0:1],
                                   scale=-math.log(10000.0) / 1024.0),
            waits=omgcS.rw() + constS.rw())
    omgcS.wrote(tw)


    rows_a = sbuf(s12, "rows_a", [1, S]); rowsaS = Slot()
    rows_b = sbuf(s12, "rows_b", [1, S]); rowsbS = Slot()
    rb_b = sbuf(s12, "rb_b", [128, S]); rbS = Slot()
    nm_b = sbuf(s12, "nm_b", [128, S]); nmS = Slot()
    xnp = sbuf(s12, "xnp", [128, 2 * S], BF16); xnpS = [Slot(), Slot()]

    with nc.psum_tensor("pstx", [1, S], F32) as pstx, \
         nc.psum_tensor("pstx2", [1, S], F32) as pstx2:
        pstxS, pstx2S = Slot(), Slot()
        lastx = lastx2 = None
        for j in range(NCH):
            xb, xbS = scr[j % 2], scrS[j % 2]
            t = (SYD if j % 2 == 0 else SCD)(
                f"scrx{j % 2}", lambda e, j=j, xb=xb: e.dma_start(
                    out=xb[:], in_=xT[j * 128:(j + 1) * 128, :]), waits=xbS.ww())
            xbS.wrote(t)
            tq = VC(lambda e, xb=xb: e.tensor_tensor(scr[2][:], xb[:], xb[:], ALU.mult),
                    waits=xbS.rw() + scrS[2].ww())
            xbS.read(tq)
            scrS[2].wrote(tq)
            for blk in range(4):
                mm = TC(lambda e, xb=xb, blk=blk, j=j: e.matmul(
                    pstx[0:1, blk * 512:(blk + 1) * 512], onesc[:],
                    xb[:, blk * 512:(blk + 1) * 512],
                    start=(j == 0), stop=(j == NCH - 1)),
                    waits=xbS.rw() + (constS.rw() if j == 0 and blk == 0 else []))
                xbS.read(mm)
                lastx = mm
            for blk in range(4):
                mm = TC(lambda e, blk=blk, j=j: e.matmul(
                    pstx2[0:1, blk * 512:(blk + 1) * 512], onesc[:],
                    scr[2][:, blk * 512:(blk + 1) * 512],
                    start=(j == 0), stop=(j == NCH - 1)), waits=scrS[2].rw())
                scrS[2].read(mm)
                lastx2 = mm
        pstxS.wrote(lastx)
        pstx2S.wrote(lastx2)
        tm = VC(lambda e: e.tensor_scalar(rows_b[:], pstx[0:1, :], 1.0 / H, None, ALU.mult),
                waits=pstxS.rw())
        rowsbS.wrote(tm)
        pstxS.read(tm)
        te = VC(lambda e: e.tensor_scalar(rows_a[:], pstx2[0:1, :], 1.0 / H, None, ALU.mult),
                waits=pstx2S.rw())
        rowsaS.wrote(te)
        pstx2S.read(te)
        t3 = VC(lambda e: e.tensor_tensor(pstx[0:1, :], rows_b[:], rows_b[:], ALU.mult))
        pstxS.wrote(t3)
        t4 = VC(lambda e: e.tensor_tensor(rows_a[:], rows_a[:], pstx[0:1, :], ALU.subtract))
        rowsaS.wrote(t4)
        pstxS.read(t4)
        t5 = SC(lambda e: e.activation(pstx2[0:1, :], rows_a[:], AF.Sqrt,
                                       bias=epsc[0:1, 0:1], scale=1.0),
                waits=rowsaS.rw() + pstx2S.rw())
        pstx2S.wrote(t5)
        rowsaS.read(t5)
        t6 = VC(lambda e: e.reciprocal(rows_a[:], pstx2[0:1, :]),
                waits=pstx2S.rw() + rowsaS.ww())
        rowsaS.wrote(t6)
        pstx2S.read(t6)
        t7 = VC(lambda e: e.scalar_tensor_tensor(rows_b[:], rows_b[:], -1.0, rows_a[:],
                                                 ALU.mult, ALU.mult))
        rowsbS.wrote(t7)
        rowsaS.read(t7)
    with nc.psum_tensor("pbc1", [128, S], F32) as pbc1:
        pbc1S = Slot(guard=[t7])
        lastmm = None
        for blk in range(4):
            mm = TC(lambda e, blk=blk: e.matmul(
                pbc1[:, blk * 512:(blk + 1) * 512], onesr[0:1, :],
                rows_a[0:1, blk * 512:(blk + 1) * 512], start=True, stop=True),
                waits=rowsaS.rw() + (pbc1S.ww() if blk == 0 else []))
            rowsaS.read(mm)
            lastmm = mm
        pbc1S.wrote(lastmm)
        ev = VC(lambda e: e.tensor_copy(rb_b[:], pbc1[:]), waits=pbc1S.rw())
        rbS.wrote(ev)
        pbc1S.read(ev)
        lastmm = None
        for blk in range(4):
            mm = TC(lambda e, blk=blk: e.matmul(
                pbc1[:, blk * 512:(blk + 1) * 512], onesr[0:1, :],
                rows_b[0:1, blk * 512:(blk + 1) * 512], start=True, stop=True),
                waits=rowsbS.rw() + [pbc1S.cons.get("vector")])
            rowsbS.read(mm)
            lastmm = mm
        pbc1S.wrote(lastmm)
        ev = VC(lambda e: e.tensor_copy(nm_b[:], pbc1[:]), waits=pbc1S.rw())
        nmS.wrote(ev)
        pbc1S.read(ev)

    cp_ = SYD("posbL", lambda e: e.dma_start(out=scr[3][:], in_=posB[:]),
              waits=scrS[3].ww())
    scrS[3].wrote(cp_)

    for j in range(NCH):
        xb, xbS = scr[j % 2], scrS[j % 2]
        t = (SYD if j % 2 == 0 else SCD)(
            f"scrx{j % 2}", lambda e, j=j, xb=xb: e.dma_start(
                out=xb[:], in_=xT[j * 128:(j + 1) * 128, :]), waits=xbS.ww())
        xbS.wrote(t)
        t1 = VC(lambda e, xb=xb: e.tensor_tensor(scr[2][:], xb[:], rb_b[:], ALU.mult),
                waits=xbS.rw() + rbS.rw() + scrS[2].ww())
        xbS.read(t1)
        rbS.read(t1)
        scrS[2].wrote(t1)
        par = j % 2
        xo = xnp[:, par * S:(par + 1) * S]
        t2 = VC(lambda e, xo=xo: e.tensor_tensor(xo, scr[2][:], nm_b[:], ALU.add),
                waits=nmS.rw() + xnpS[par].ww())
        nmS.read(t2)
        scrS[2].read(t2)
        xnpS[par].wrote(t2)
        st = SCD(f"st_xnp{par}", lambda e, j=j, xo=xo: e.dma_start(out=xn_s[j], in_=xo),
                 waits=xnpS[par].rw())
        xnpS[par].read(st)
        xnH[j].wrote(st)

    def emit_table_block(pr):
        if True:
            a0 = SC(lambda e, pr=pr: e.activation(scr[2][:], scr[3][:], AF.Copy,
                                                  bias=0.0,
                                                  scale=omgc[:, pr:pr + 1]),
                    waits=scrS[2].ww() + scrS[3].rw() + omgcS.rw())
            scrS[2].wrote(a0)
            scrS[3].read(a0)
            omgcS.read(a0)
            t1 = VC(lambda e: e.tensor_scalar(i32scr[:], scr[2][:], INV_2PI, None,
                                              ALU.mult),
                    waits=scrS[2].rw() + i32S.ww())
            i32S.wrote(t1)
            scrS[2].read(t1)
            t3 = SC(lambda e: e.copy(scr[1][:], i32scr[:]),
                    waits=i32S.rw() + scrS[1].ww())
            scrS[1].wrote(t3)
            i32S.read(t3)
            t4a = VC(lambda e: e.scalar_tensor_tensor(scr[2][:], scr[1][:], -CW_C1,
                                                      scr[2][:], ALU.mult, ALU.add),
                     waits=scrS[1].rw() + scrS[2].rw())
            scrS[2].wrote(t4a)
            t4b = VC(lambda e: e.scalar_tensor_tensor(scr[2][:], scr[1][:], -CW_C2,
                                                      scr[2][:], ALU.mult, ALU.add))
            scrS[2].wrote(t4b)
            t4 = VC(lambda e: e.scalar_tensor_tensor(scr[2][:], scr[1][:], -CW_C3,
                                                     scr[2][:], ALU.mult, ALU.add))
            scrS[2].wrote(t4)
            scrS[1].read(t4)
            wh1 = VC(lambda e: e.tensor_scalar(scr[0][:], scr[2][:], PI_LO, None,
                                               ALU.is_gt), waits=scrS[0].ww())
            scrS[0].wrote(wh1)
            wh2 = VC(lambda e: e.scalar_tensor_tensor(scr[2][:], scr[0][:], -TWO_PI_F,
                                                      scr[2][:], ALU.mult, ALU.add))
            scrS[2].wrote(wh2)
            scrS[0].read(wh2)
            wl1 = VC(lambda e: e.tensor_scalar(scr[0][:], scr[2][:], -PI_LO, None,
                                               ALU.is_lt), waits=scrS[0].ww())
            scrS[0].wrote(wl1)
            wl2 = VC(lambda e: e.scalar_tensor_tensor(scr[2][:], scr[0][:], TWO_PI_F,
                                                      scr[2][:], ALU.mult, ALU.add))
            scrS[2].wrote(wl2)
            scrS[0].read(wl2)
            for ti in (0, 1):
                if ti == 0:
                    t5 = VC(lambda e: e.tensor_scalar(scr[0][:], scr[2][:],
                                                      PI_LO, -PI_LO, ALU.min, ALU.max),
                            waits=scrS[0].ww())
                else:
                    tm_ = VC(lambda e: e.tensor_scalar(scr[0][:], scr[2][:],
                                                       HALF_PI, None, ALU.is_gt),
                             waits=scrS[0].ww())
                    scrS[0].wrote(tm_)
                    tw_ = VC(lambda e: e.scalar_tensor_tensor(scr[0][:], scr[0][:],
                                                              -TWO_PI_F, scr[2][:],
                                                              ALU.mult, ALU.add))
                    scrS[0].wrote(tw_)
                    t5 = VC(lambda e: e.tensor_scalar(scr[0][:], scr[0][:],
                                                      HALF_PI, PI_LO, ALU.add, ALU.min))
                scrS[0].wrote(t5)
                scrS[2].read(t5)
                dst = tblb[:, ti * S:(ti + 1) * S]
                s5 = SC(lambda e, dst=dst: e.activation(dst, scr[0][:], AF.Sin,
                                                        bias=zeroc[:, 0:1], scale=1.0),
                        waits=scrS[0].rw() + tblfS[ti].ww())
                tblfS[ti].wrote(s5)
                scrS[0].read(s5)
                d5 = SCD(f"st_tbl{ti}", lambda e, pr=pr, ti=ti, dst=dst: e.dma_start(
                    out=tbl_s[pr, ti], in_=dst), waits=tblfS[ti].rw())
                tblfS[ti].read(d5)
                tblH[pr][ti].wrote(d5)

    s12.close()
    if stop_after == "tables":
        i32scr = sbuf(perm, "i32scr", [128, S], I32)
        i32S = Slot()
        for _pr in range(8):
            emit_table_block(_pr)
        bar_end = P.barrier()
        P.c("scalar", lambda e: e.nop(), waits=bar_end)
        return _finish(nc, P, perm)

    bar_pre3 = P.barrier(exclude=PRECAST_CH)
    i32scr = sbuf(sE, "i32scr", [128, S], I32)
    i32S = Slot(guard=bar_pre3)
    sF = contextlib.ExitStack()
    xncol = sbuf(sF, "xncol", [128, 2 * NCH * 512], BF16)
    xncolS = [[Slot() for _ in range(NCH)] for _ in range(2)]
    xncol_cur = {0: None, 1: None}
    kraw = sbuf(sF, "kraw", [128, NCH * 512], BF16)
    krawS = [Slot() for _ in range(NCH)]
    ropeb = sbuf(sF, "ropeb", [128, 4 * 512], BF16)
    ropebS = [Slot() for _ in range(4)]
    qsb = sbuf(sF, "qsb", [128, 2 * 512], BF16)
    qsbS = [Slot(), Slot()]
    qfb = sbuf(sF, "qfb", [128, 2 * 512], BF16)
    qfbS = [Slot(), Slot()]

    bar3 = P.barrier(exclude=PRECAST_CH)
    with nc.psum_tensor("pv0", [128, S], F32) as pv0, \
         nc.psum_tensor("pv1", [128, S], F32) as pv1:
        pvs = [pv0, pv1]
        pvS = [Slot(guard=bar3), Slot(guard=bar3)]
        first_ld = [True]
        for tb in (2, 3, 0, 1):
            par = tb % 2
            glast = None
            for j in range(NCH):
                sl = xncolS[par][j]
                glast = SYD(f"xncol{par}", lambda e, j=j, tb=tb, par=par: e.dma_start(
                    out=xncol[:, (par * NCH + j) * 512:(par * NCH + j + 1) * 512],
                    in_=xn_s[j][:, tb * 512:(tb + 1) * 512]),
                    waits=sl.ww() + xnH[j].rw() +
                    (bar_pre3 if first_ld[0] else []))
                first_ld[0] = False
                xnH[j].read(glast)
            for j in range(NCH):
                xncolS[par][j].wrote(glast)
            xncol_cur[par] = tb
            for tloc in range(4):
                tt = tb * 4 + tloc
                pp = tt % 2
                pv, pS = pvs[pp], pvS[pp]
                lastmm = None
                for j in range(NCH):
                    sl = xncolS[par][j]
                    lhs = xncol[:, (par * NCH + j) * 512 + tloc * 128:
                                (par * NCH + j) * 512 + (tloc + 1) * 128]
                    for ib in range(4):
                        w = sl.rw() + WsbS[j].rw()
                        if j == 0 and ib == 0:
                            w += pS.ww()
                        mm = TC(lambda e, lhs=lhs, j=j, ib=ib, pv=pv: e.matmul(
                            pv[:, ib * 512:(ib + 1) * 512], lhs,
                            Wsb[:, j * H + ib * 512:j * H + (ib + 1) * 512],
                            start=(j == 0), stop=(j == NCH - 1)), waits=w)
                        sl.read(mm)
                        WsbS[j].read(mm)
                        lastmm = mm
                pS.wrote(lastmm)
                ev = VC(lambda e, pv=pv, pp=pp: e.tensor_copy(
                    vob[:, pp * S:(pp + 1) * S], pv[:]),
                    waits=pS.rw() + vobS[pp].ww())
                vobS[pp].wrote(ev)
                pS.read(ev)
                st = SCD(f"st_vt{pp}", lambda e, tt=tt, pp=pp: e.dma_start(
                    out=vt_s[:, tt].rearrange("c p f -> p c f"),
                    in_=vob[:, pp * S:(pp + 1) * S].rearrange(
                        "p (c f) -> p c f", c=NCH)),
                    waits=vobS[pp].rw())
                vobS[pp].read(st)
                for ci in range(NCH):
                    vtH[ci].add(st)
                if tt % 2 == 0:
                    emit_table_block(tt // 2)

    def proj_rope(kind):
        bar = P.barrier(exclude=PRECAST_CH)
        for _i in range(4):
            tblbS[_i] = Slot(guard=bar)
        load_W(wkT if kind == "k" else wqT)
        nblk = NTB if kind == "k" else NOB
        with nc.psum_tensor(f"pk0_{kind}", [128, 512], F32) as pk0, \
             nc.psum_tensor(f"pk1_{kind}", [128, 512], F32) as pk1, \
             nc.psum_tensor(f"pkn_{kind}", [128, 1024], F32) as pkn:
            pks = [pk0, pk1]
            pkS = [Slot(guard=bar), Slot(guard=bar)]
            pknS = [Slot(guard=bar), Slot(guard=bar)]
            for tb in range(nblk):
                par = tb % 2
                if xncol_cur[par] != tb:
                    glast = None
                    for j in range(NCH):
                        sl = xncolS[par][j]
                        glast = SYD(f"xncol{par}", lambda e, j=j, tb=tb, par=par: e.dma_start(
                            out=xncol[:, (par * NCH + j) * 512:(par * NCH + j + 1) * 512],
                            in_=xn_s[j][:, tb * 512:(tb + 1) * 512]),
                            waits=sl.ww() + xnH[j].rw())
                        xnH[j].read(glast)
                    for j in range(NCH):
                        xncolS[par][j].wrote(glast)
                    xncol_cur[par] = tb
                for c in range(NCH):
                    pp = c % 2
                    pk, pS = pks[pp], pkS[pp]
                    lastmm = None
                    for j in range(NCH):
                        sl = xncolS[par][j]
                        w = sl.rw() + WsbS[j].rw()
                        if j == 0:
                            w += pS.ww()
                        mm = TC(lambda e, j=j, c=c, pk=pk, par=par: e.matmul(
                            pk[:], Wsb[:, j * H + c * 128:j * H + (c + 1) * 128],
                            xncol[:, (par * NCH + j) * 512:(par * NCH + j + 1) * 512],
                            start=(j == 0), stop=(j == NCH - 1)), waits=w)
                        sl.read(mm)
                        WsbS[j].read(mm)
                        lastmm = mm
                    pS.wrote(lastmm)
                    ev = VC(lambda e, pk=pk, c=c: e.tensor_copy(
                        kraw[:, c * 512:(c + 1) * 512], pk[:]),
                        waits=pS.rw() + krawS[c].ww())
                    krawS[c].wrote(ev)
                    pS.read(ev)
                for pr in range(8):
                    c0, c8 = pr, pr + 8
                    hp = pr % 2
                    slS, slC = tblbS[2 * hp], tblbS[2 * hp + 1]
                    sinb = tblb[:, (2 * hp) * 512:(2 * hp + 1) * 512]
                    cosb = tblb[:, (2 * hp + 1) * 512:(2 * hp + 2) * 512]
                    tS = SYD(f"tblq{2 * hp}", lambda e, pr=pr, tb=tb, sinb=sinb: e.dma_start(
                        out=sinb, in_=tbl_s[pr, 0][:, tb * 512:(tb + 1) * 512]),
                        waits=slS.ww() + tblH[pr][0].rw())
                    slS.wrote(tS)
                    tblH[pr][0].read(tS)
                    tCo = SYD(f"tblq{2 * hp + 1}", lambda e, pr=pr, tb=tb, cosb=cosb: e.dma_start(
                        out=cosb, in_=tbl_s[pr, 1][:, tb * 512:(tb + 1) * 512]),
                        waits=slC.ww() + tblH[pr][1].rw())
                    slC.wrote(tCo)
                    tblH[pr][1].read(tCo)
                    kr0 = kraw[:, c0 * 512:(c0 + 1) * 512]
                    kr8 = kraw[:, c8 * 512:(c8 + 1) * 512]
                    f0 = ropeb[:, (2 * hp) * 512:(2 * hp + 1) * 512]
                    f8 = ropeb[:, (2 * hp + 1) * 512:(2 * hp + 2) * 512]
                    f0S, f8S = ropebS[2 * hp], ropebS[2 * hp + 1]
                    sa = scr[0][:, 0:512]
                    sb2 = scr[0][:, 512:1024]
                    t1 = VC(lambda e, kr8=kr8, cosb=cosb, sa=sa: e.tensor_tensor(
                        sa, kr8, cosb, ALU.mult),
                        waits=krawS[c8].rw() + slC.rw() + scrS[0].ww())
                    krawS[c8].read(t1)
                    slC.read(t1)
                    scrS[0].wrote(t1)
                    t2 = VC(lambda e, kr0=kr0, sinb=sinb, sb2=sb2: e.tensor_tensor(
                        sb2, kr0, sinb, ALU.mult),
                        waits=krawS[c0].rw() + slS.rw())
                    krawS[c0].read(t2)
                    slS.read(t2)
                    t3 = VC(lambda e, f8=f8, sa=sa, sb2=sb2: e.tensor_tensor(
                        f8, sa, sb2, ALU.add), waits=f8S.ww())
                    f8S.wrote(t3)
                    scrS[0].read(t3)
                    t4 = VC(lambda e, kr0=kr0, cosb=cosb, sa=sa: e.tensor_tensor(
                        sa, kr0, cosb, ALU.mult),
                        waits=krawS[c0].rw() + slC.rw())
                    krawS[c0].read(t4)
                    slC.read(t4)
                    t5 = VC(lambda e, kr8=kr8, sinb=sinb, sb2=sb2: e.tensor_tensor(
                        sb2, kr8, sinb, ALU.mult),
                        waits=krawS[c8].rw() + slS.rw())
                    krawS[c8].read(t5)
                    slS.read(t5)
                    t6 = VC(lambda e, f0=f0, sa=sa, sb2=sb2: e.tensor_tensor(
                        f0, sa, sb2, ALU.subtract), waits=f0S.ww())
                    f0S.wrote(t6)
                    scrS[0].read(t6)
                    for (cc, fb, fbS) in ((c0, f0, f0S), (c8, f8, f8S)):
                        if kind == "k":
                            st = SCD(f"st_rope{2 * hp + (0 if cc == c0 else 1)}",
                                     lambda e, cc=cc, tb=tb, fb=fb: e.dma_start(
                                out=kfm_s[cc][:, tb * 512:(tb + 1) * 512], in_=fb),
                                waits=fbS.rw())
                            fbS.read(st)
                            kfmH[cc].add(st)
                            nsrc, nsrcS = fb, fbS
                        else:
                            qp = cc % 2
                            qv = qsb[:, qp * 512:(qp + 1) * 512]
                            tqs = VC(lambda e, fb=fb, qv=qv: e.tensor_scalar(
                                qv, fb, SCALE_QK, None, ALU.mult),
                                waits=qsbS[qp].ww())
                            qsbS[qp].wrote(tqs)
                            fbS.read(tqs)
                            st = SCD(f"st_qsb{qp}", lambda e, cc=cc, tb=tb, qv=qv: e.dma_start(
                                out=qs_s[cc][:, tb * 512:(tb + 1) * 512], in_=qv),
                                waits=qsbS[qp].rw())
                            qsbS[qp].read(st)
                            qsH[cc].add(st)
                            smn = scr[1][:, 0:512]
                            sex = scr[1][:, 512:1024]
                            srl = scr[1][:, 1024:1536]
                            tmn = VC(lambda e, fb=fb, smn=smn: e.tensor_scalar(
                                smn, fb, 0.0, None, ALU.min),
                                waits=scrS[1].ww())
                            scrS[1].wrote(tmn)
                            fbS.read(tmn)
                            tex = SC(lambda e, smn=smn, sex=sex: e.activation(
                                sex, smn, AF.Exp, bias=zeroc[:, 0:1], scale=1.0),
                                waits=scrS[1].rw())
                            scrS[1].wrote(tex)
                            trl = VC(lambda e, fb=fb, srl=srl: e.tensor_scalar(
                                srl, fb, 0.0, None, ALU.max),
                                waits=[tex])
                            scrS[1].wrote(trl)
                            fbS.read(trl)
                            qv2 = qfb[:, qp * 512:(qp + 1) * 512]
                            tqf = VC(lambda e, qv2=qv2, srl=srl, sex=sex: e.tensor_tensor(
                                qv2, srl, sex, ALU.add),
                                waits=qfbS[qp].ww())
                            qfbS[qp].wrote(tqf)
                            scrS[1].read(tqf)
                            st2 = SCD(f"st_qfb{qp}", lambda e, cc=cc, tb=tb, qv2=qv2: e.dma_start(
                                out=qf_s[cc][:, tb * 512:(tb + 1) * 512], in_=qv2),
                                waits=qfbS[qp].rw())
                            qfbS[qp].read(st2)
                            qfH[cc].add(st2)
                            nsrc, nsrcS = qv, qsbS[qp]
                        kp = cc % 2
                        k2v = k2scr[:, kp * 512:(kp + 1) * 512]
                        tk2 = VC(lambda e, nsrc=nsrc, k2v=k2v: e.tensor_tensor(
                            k2v, nsrc, nsrc, ALU.mult),
                            waits=k2S[kp].ww())
                        k2S[kp].wrote(tk2)
                        nsrcS.read(tk2)
                        pknv = pkn[:, kp * 512:(kp + 1) * 512]
                        mmn = TC(lambda e, k2v=k2v, pknv=pknv: e.matmul(
                            pknv, onesM[:], k2v, start=True, stop=True),
                                 waits=k2S[kp].rw() + pknS[kp].ww())
                        k2S[kp].read(mmn)
                        pknS[kp].wrote(mmn)
                        slot = (0 if kind == "k" else 64) + cc * 4 + tb
                        trm = VC(lambda e, pknv=pknv, slot=slot: e.reduce_max(
                            m2all[0:1, slot:slot + 1], pknv[0:1, :], axis=AX),
                                 waits=pknS[kp].rw())
                        m2allS.wrote(trm, reset=False)
                        pknS[kp].read(trm)

    proj_rope("k")
    proj_rope("q")

    for cc in range(NCH):
        t_ = VC(lambda e, cc=cc: e.reduce_max(km2[0:1, cc:cc + 1],
                                              m2all[0:1, cc * 4:cc * 4 + 4], axis=AX),
                waits=m2allS.rw() + km2S.ww())
        km2S.wrote(t_, reset=False)
        t_ = VC(lambda e, cc=cc: e.reduce_max(qm2[0:1, cc:cc + 1],
                                              m2all[0:1, 64 + cc * 4:64 + cc * 4 + 2], axis=AX),
                waits=m2allS.rw() + qm2S.ww())
        qm2S.wrote(t_, reset=False)
    tqm = SC(lambda e: e.activation(urow[:], qm2[:], AF.Sqrt, bias=zeroc[0:1, 0:1], scale=1.0),
             waits=qm2S.rw() + urowS.ww())
    urowS.wrote(tqm)
    qm2S.read(tqm)
    tkm = SC(lambda e: e.activation(u2row[:], km2[:], AF.Sqrt, bias=zeroc[0:1, 0:1], scale=1.0),
             waits=km2S.rw() + u2S.ww())
    u2S.wrote(tkm)
    km2S.read(tkm)
    if "diag_s" in dbg:
        tdq = SCD("st_diag", lambda e: e.dma_start(out=diag_s[0, 1:2, 0:16], in_=urow[:]),
                  waits=urowS.rw())
        urowS.read(tdq)
        tdk = SCD("st_diag", lambda e: e.dma_start(out=diag_s[1, 1:2, 0:16], in_=u2row[:]),
                  waits=u2S.rw())
        u2S.read(tdk)
    tun = VC(lambda e: e.scalar_tensor_tensor(urow[:], urow[:], -1.0, u2row[:],
                                              ALU.mult, ALU.mult),
             waits=urowS.rw() + u2S.rw())
    urowS.wrote(tun)
    u2S.read(tun)
    barU = P.barrier(exclude=PRECAST_CH)
    with nc.psum_tensor("pbu", [128, 16], F32) as pbu:
        pbuS = Slot(guard=barU)
        mm = TC(lambda e: e.matmul(pbu[:], onesr[0:1, :], urow[0:1, :], start=True, stop=True),
                waits=urowS.rw() + pbuS.ww())
        urowS.read(mm)
        pbuS.wrote(mm)
        ev = VC(lambda e: e.tensor_copy(negU[:], pbu[:]), waits=pbuS.rw() + negUS.ww())
        negUS.wrote(ev)
        pbuS.read(ev)
    if "diag_s" in dbg:
        tdu = SCD("st_diag", lambda e: e.dma_start(out=diagU_s[:], in_=negU[:]), waits=negUS.rw())
        negUS.read(tdu)

    sF.close()

    if stop_after == "proj":
        bar_end = P.barrier()
        P.c("scalar", lambda e: e.nop(), waits=bar_end)
        sE.close()
        return _finish(nc, P, perm)

    sG = contextlib.ExitStack()
    kfmh = sbuf(sG, "kfmh", [128, 2 * S], BF16); kfmhS = [Slot(), Slot()]
    vh = sbuf(sG, "vh", [128, 2 * S], BF16); vhS = [Slot(), Slot()]
    qsh = sbuf(sG, "qsh", [128, 2 * OWN], BF16); qshS = [Slot(), Slot()]
    qfh = sbuf(sG, "qfh", [128, 2 * OWN], BF16); qfhS = [Slot(), Slot()]
    kfbuf = sbuf(sG, "kfbuf", [128, 2 * S], BF16); kfbufS = [Slot(), Slot()]
    kftb = sbuf(sG, "kftb", [128, 2 * 128], BF16); kftbS = [Slot(), Slot()]
    memsb = sbuf(sG, "memsb", [128, 2 * 128], BF16); memsbS = [Slot(), Slot()]
    normsb = sbuf(sG, "normsb", [128, 2], BF16); normsbS = [Slot(), Slot()]
    ptile = sbuf(sG, "ptile", [128, 2 * 1024], BF16); ptileS = [Slot(), Slot()]
    rlgb = sbuf(sG, "rlgb", [128, 2 * OWN]); rlgbS = [Slot(), Slot()]
    qfsb = sbuf(sG, "qfsb", [128, 2 * OWN], BF16); qfsbS = [Slot(), Slot()]
    combb = sbuf(sG, "combb", [128, OWN], BF16); combbS = Slot()
    attnsb = sbuf(sG, "attnsb", [128, OWN]); attnsbS = Slot()
    rs2 = sbuf(sG, "rs2", [1, OWN]); rs2S = Slot()
    rsgb = sbuf(sG, "rsgb", [128, 512]); rsgbS = Slot()

    bar6 = P.barrier(exclude=PRECAST_CH)

    def pre_head(h, guard=()):
        par = h % 2
        t = SYD(f"kfmh{par}", lambda e, h=h, par=par: e.dma_start(
            out=kfmh[:, par * S:(par + 1) * S], in_=kfm_s[h]),
            waits=kfmhS[par].ww() + kfmH[h].rw() + list(guard))
        kfmhS[par].wrote(t)
        vdst = vh[:, par * S:(par + 1) * S].rearrange("p (t f) -> p t f", t=NTT)
        t = SYD(f"vh{par}", lambda e, vdst=vdst, h=h: e.dma_start(
            out=vdst, in_=vt_s[h].rearrange("t p f -> p t f")),
            waits=vhS[par].ww() + vtH[h].rw())
        vhS[par].wrote(t)
        t = SYD(f"qsh{par}", lambda e, h=h, par=par: e.dma_start(
            out=qsh[:, par * OWN:(par + 1) * OWN], in_=qs_s[h]),
            waits=qshS[par].ww() + qsH[h].rw())
        qshS[par].wrote(t)
        t = SYD(f"qfh{par}", lambda e, h=h, par=par: e.dma_start(
            out=qfh[:, par * OWN:(par + 1) * OWN], in_=qf_s[h]),
            waits=qfhS[par].ww() + qfH[h].rw())
        qfhS[par].wrote(t)

    def eluA(h):
        par = h % 2
        kfm_v = kfmh[:, par * S:(par + 1) * S]
        kfb_v = kfbuf[:, par * S:(par + 1) * S]
        t1 = VC(lambda e, kfm_v=kfm_v: e.tensor_scalar(scr[0][:], kfm_v, 0.0, None, ALU.min),
                waits=kfmhS[par].rw() + scrS[0].ww())
        scrS[0].wrote(t1)
        kfmhS[par].read(t1)
        t2 = SC(lambda e: e.activation(scr[1][:], scr[0][:], AF.Exp,
                                       bias=zeroc[:, 0:1], scale=1.0),
                waits=scrS[0].rw() + scrS[1].ww())
        scrS[1].wrote(t2)
        scrS[0].read(t2)
        t3 = VC(lambda e, kfm_v=kfm_v: e.tensor_scalar(scr[0][:], kfm_v, 0.0, None, ALU.max),
                waits=[t2] + scrS[0].ww())
        scrS[0].wrote(t3)
        kfmhS[par].read(t3)
        t4 = VC(lambda e, kfb_v=kfb_v: e.tensor_tensor(kfb_v, scr[0][:], scr[1][:], ALU.add),
                waits=kfbufS[par].ww())
        kfbufS[par].wrote(t4)
        scrS[0].read(t4)
        scrS[1].read(t4)

    def stageA(h, guard):
        par = h % 2
        vh_v = vh[:, par * S:(par + 1) * S]
        kfb_v = kfbuf[:, par * S:(par + 1) * S]
        with nc.psum_tensor(f"ptr0_{h}", [128, 128], BF16) as ptr0, \
             nc.psum_tensor(f"ptr1_{h}", [128, 128], BF16) as ptr1, \
             nc.psum_tensor(f"pmem_{h}", [128, 128], F32) as pmem, \
             nc.psum_tensor(f"pnorm_{h}", [128, 1], F32) as pnorm:
            ptrs = [ptr0, ptr1]
            ptrS = [Slot(guard=guard), Slot(guard=guard)]
            pmemS, pnormS = Slot(guard=guard), Slot(guard=guard)
            mm1 = mm2 = None
            for tt in range(NTT):
                pp = tt % 2
                tr = TC(lambda e, tt=tt, pp=pp, kfb_v=kfb_v: e.transpose(
                    ptrs[pp][:], kfb_v[:, tt * 128:(tt + 1) * 128], ident[:]),
                    waits=kfbufS[par].rw() + identS.rw() + ptrS[pp].ww())
                kfbufS[par].read(tr)
                identS.read(tr)
                ptrS[pp].wrote(tr)
                ev = VC(lambda e, pp=pp: e.tensor_copy(
                    kftb[:, pp * 128:(pp + 1) * 128], ptrs[pp][:]),
                    waits=ptrS[pp].rw() + kftbS[pp].ww())
                kftbS[pp].wrote(ev)
                ptrS[pp].read(ev)
                w1 = kftbS[pp].rw() + vhS[par].rw() + (pmemS.ww() if tt == 0 else [])
                mm1 = TC(lambda e, tt=tt, pp=pp, vh_v=vh_v: e.matmul(
                    pmem[:], kftb[:, pp * 128:(pp + 1) * 128],
                    vh_v[:, tt * 128:(tt + 1) * 128],
                    start=(tt == 0), stop=(tt == NTT - 1)), waits=w1)
                kftbS[pp].read(mm1)
                vhS[par].read(mm1)
                w2 = kftbS[pp].rw() + (pnormS.ww() if tt == 0 else [])
                mm2 = TC(lambda e, tt=tt, pp=pp: e.matmul(
                    pnorm[:], kftb[:, pp * 128:(pp + 1) * 128], onescb[:],
                    start=(tt == 0), stop=(tt == NTT - 1)), waits=w2)
                kftbS[pp].read(mm2)
            pmemS.wrote(mm1)
            pnormS.wrote(mm2)
            evm = VC(lambda e, par=par: e.tensor_copy(
                memsb[:, par * 128:(par + 1) * 128], pmem[:]),
                waits=pmemS.rw() + memsbS[par].ww())
            memsbS[par].wrote(evm)
            pmemS.read(evm)
            evn = VC(lambda e, par=par: e.tensor_copy(
                normsb[:, par:par + 1], pnorm[:]),
                waits=pnormS.rw() + normsbS[par].ww())
            normsbS[par].wrote(evn)
            pnormS.read(evn)
        return evm, evn

    def stageB(h, evm, evn):
        par = h % 2
        qf_v = qfh[:, par * OWN:(par + 1) * OWN]
        with nc.psum_tensor(f"pdl_{h}", [1, OWN], F32) as pdl, \
             nc.psum_tensor(f"pbcb_{h}", [128, OWN], F32) as pbcb:
            pdlS = Slot(guard=[evm, evn])
            pbcbS = Slot(guard=[evm, evn])
            mm = None
            for bb in range(2):
                w = normsbS[par].rw() + qfhS[par].rw() + (pdlS.ww() if bb == 0 else [])
                mm = TC(lambda e, bb=bb, qf_v=qf_v, par=par: e.matmul(
                    pdl[0:1, bb * 512:(bb + 1) * 512], normsb[:, par:par + 1],
                    qf_v[:, bb * 512:(bb + 1) * 512], start=True, stop=True), waits=w)
                normsbS[par].read(mm)
                qfhS[par].read(mm)
            pdlS.wrote(mm)
            trc = VC(lambda e: e.reciprocal(rl_r[:], pdl[0:1, :]),
                     waits=pdlS.rw() + rlS.ww())
            rlS.wrote(trc)
            pdlS.read(trc)
            trg = VC(lambda e, h=h: e.tensor_scalar(rl_r[:], rl_r[:],
                                                    grow[0:1, h:h + 1], None, ALU.mult),
                     waits=growS.rw())
            rlS.wrote(trg)
            growS.read(trg)
            mmb = None
            for bb in range(2):
                w = rlS.rw() + (pbcbS.ww() if bb == 0 else [])
                mmb = TC(lambda e, bb=bb: e.matmul(
                    pbcb[:, bb * 512:(bb + 1) * 512], onesr[0:1, :],
                    rl_r[0:1, bb * 512:(bb + 1) * 512], start=True, stop=True), waits=w)
                rlS.read(mmb)
            pbcbS.wrote(mmb)
            evb = VC(lambda e, par=par: e.tensor_copy(rlgb[:, par * OWN:(par + 1) * OWN],
                                                      pbcb[:]),
                     waits=pbcbS.rw() + rlgbS[par].ww())
            rlgbS[par].wrote(evb)
            pbcbS.read(evb)
        tqf = VC(lambda e, qf_v=qf_v, par=par: e.tensor_tensor(
            qfsb[:, par * OWN:(par + 1) * OWN], qf_v,
            rlgb[:, par * OWN:(par + 1) * OWN], ALU.mult),
            waits=qfhS[par].rw() + qfsbS[par].ww())
        qfsbS[par].wrote(tqf)
        qfhS[par].read(tqf)
        rlgbS[par].read(tqf)
        return evb

    pre_head(0, guard=bar6)
    eluA(0)
    evm0, evn0 = stageA(0, guard=bar6)
    evb0 = stageB(0, evm0, evn0)
    dguard = [evb0]
    for h in range(NHEAD):
        par = h % 2
        if h + 1 < NHEAD:
            pre_head(h + 1)
            eluA(h + 1)
            if h == 1:
                pg_w = kfmhS[0].rw()
                lg_ = lu_ = ld2_ = None
                for j in range(NCH):
                    lg_ = GPD("wgbp", lambda e, j=j: e.dma_start(
                        out=wgb[j], in_=wg8[j * 128:(j + 1) * 128, :]), waits=pg_w)
                for j in range(NCH):
                    lu_ = GPD("wubp", lambda e, j=j: e.dma_start(
                        out=wub[j], in_=wu8[j * 128:(j + 1) * 128, :]), waits=pg_w)
                for o in range(NOC):
                    ld2_ = GPD("wdbp", lambda e, o=o: e.dma_start(
                        out=wdb[o], in_=wd8[o * 128:(o + 1) * 128, :]), waits=pg_w)
                for j in range(NCH):
                    wgbH[j].wrote(lg_)
                    wubH[j].wrote(lu_)
                for o in range(NOC):
                    wdbH[o].wrote(ld2_)
        kfm_v = kfmh[:, par * S:(par + 1) * S]
        vh_v = vh[:, par * S:(par + 1) * S]
        qs_v = qsh[:, par * OWN:(par + 1) * OWN]
        with nc.psum_tensor(f"pat0_{h}", [128, 512], F32) as pat0, \
             nc.psum_tensor(f"pat1_{h}", [128, 512], F32) as pat1, \
             nc.psum_tensor(f"pden_{h}", [1, OWN], F32) as pden:
            pats = [pat0, pat1]
            patS = [Slot(guard=dguard), Slot(guard=dguard)]
            pdenS = [Slot(guard=dguard), Slot(guard=dguard)]
            lastex = None
            with nc.psum_tensor(f"psc0_{h}", [128, 1024], F32) as psc0, \
                 nc.psum_tensor(f"psc1_{h}", [128, 1024], F32) as psc1:
                pscs = [psc0, psc1]
                pscS = [Slot(guard=dguard), Slot(guard=dguard)]
                NP = NTT // 2
                sched = []
                for b in range(2):
                    for p in range(NP):
                        sched.append(("sc", b, p))
                        if p >= 1:
                            sched.append(("pvdn", b, p - 1))
                    sched.append(("pvdn", b, NP - 1))
                for item in sched:
                    kind_, b, p = item
                    pp = p % 2
                    if kind_ == "sc":
                        sc_mm = None
                        for half in range(2):
                            s = 2 * p + half
                            w = kfmhS[par].rw() + qshS[par].rw() + \
                                (pscS[pp].ww() if half == 0 else [])
                            sc_mm = TC(lambda e, s=s, b=b, pp=pp, half=half,
                                       kfm_v=kfm_v, qs_v=qs_v:
                                       e.matmul(pscs[pp][:, half * 512:(half + 1) * 512],
                                                kfm_v[:, s * 128:(s + 1) * 128],
                                                qs_v[:, b * 512:(b + 1) * 512],
                                                start=True, stop=True), waits=w)
                            kfmhS[par].read(sc_mm)
                            qshS[par].read(sc_mm)
                        pscS[pp].wrote(sc_mm)
                        ex = SC(lambda e, pp=pp, h=h: e.activation(
                            ptile[:, pp * 1024:(pp + 1) * 1024], pscs[pp][:], AF.Exp,
                            bias=negU[:, h:h + 1], scale=1.0),
                            waits=pscS[pp].rw() + ptileS[pp].ww() + negUS.rw())
                        ptileS[pp].wrote(ex)
                        pscS[pp].read(ex)
                        negUS.read(ex)
                        lastex = ex
                    else:
                        for half in range(2):
                            s = 2 * p + half
                            pt_v = ptile[:, pp * 1024 + half * 512:
                                         pp * 1024 + (half + 1) * 512]
                            w1 = ptileS[pp].rw() + vhS[par].rw() + \
                                (patS[b].ww() if s == 0 else [])
                            pv = TC(lambda e, s=s, b=b, pt_v=pt_v, vh_v=vh_v: e.matmul(
                                pats[b][:], vh_v[:, s * 128:(s + 1) * 128], pt_v,
                                start=(s == 0), stop=(s == NTT - 1)), waits=w1)
                            ptileS[pp].read(pv)
                            vhS[par].read(pv)
                            w2 = ptileS[pp].rw() + (pdenS[b].ww() if s == 0 else [])
                            dn = TC(lambda e, s=s, b=b, pt_v=pt_v: e.matmul(
                                pden[0:1, b * 512:(b + 1) * 512], onescb[:], pt_v,
                                start=(s == 0), stop=(s == NTT - 1)), waits=w2)
                            ptileS[pp].read(dn)
                            if s == NTT - 1:
                                patS[b].wrote(pv)
                                pdenS[b].wrote(dn)
            e1t = []
            for b in range(2):
                ev = VC(lambda e, b=b: e.tensor_copy(
                    attnsb[:, b * 512:(b + 1) * 512], pats[b][:]),
                    waits=patS[b].rw() + (attnsbS.ww() if b == 0 else []))
                attnsbS.wrote(ev) if b == 0 else attnsbS.add(ev)
                patS[b].read(ev)
                e1t.append(ev)
                trs = VC(lambda e, b=b: e.reciprocal(
                    rs2[0:1, b * 512:(b + 1) * 512], pden[0:1, b * 512:(b + 1) * 512]),
                    waits=pdenS[b].rw() + (rs2S.ww() if b == 0 else []))
                rs2S.wrote(trs) if b == 0 else rs2S.add(trs)
                pdenS[b].read(trs)
                e1t.append(trs)
            tg2 = VC(lambda e, h=h: e.tensor_scalar(rs2[:], rs2[:],
                                                    ogrow[0:1, h:h + 1], None, ALU.mult),
                     waits=ogrowS.rw())
            rs2S.wrote(tg2)
            ogrowS.read(tg2)
            e1t.append(tg2)
        e1t.append(lastex)
        if h + 1 < NHEAD:
            evm1, evn1 = stageA(h + 1, guard=e1t)
            evb1 = stageB(h + 1, evm1, evn1)
            dguard = [evb1]
        with nc.psum_tensor(f"pmoE_{h}", [128, 512], F32) as pmoE, \
             nc.psum_tensor(f"pbcE_{h}", [128, 512], F32) as pbcE:
            pmoES = Slot(guard=e1t)
            pbcES = Slot(guard=e1t)
            ecombs = []
            for b in range(2):
                mo = TC(lambda e, b=b, par=par: e.matmul(
                    pmoE[:], memsb[:, par * 128:(par + 1) * 128],
                    qfsb[:, par * OWN + b * 512:par * OWN + (b + 1) * 512],
                    start=True, stop=True),
                    waits=memsbS[par].rw() + qfsbS[par].rw() + pmoES.ww())
                memsbS[par].read(mo)
                qfsbS[par].read(mo)
                pmoES.wrote(mo)
                bc = TC(lambda e, b=b: e.matmul(
                    pbcE[:], onesr[0:1, :], rs2[0:1, b * 512:(b + 1) * 512],
                    start=True, stop=True),
                    waits=rs2S.rw() + pbcES.ww())
                rs2S.read(bc)
                pbcES.wrote(bc)
                evb2 = VC(lambda e, b=b: e.tensor_copy(rsgb[:], pbcE[:]),
                          waits=pbcES.rw() + rsgbS.ww())
                rsgbS.wrote(evb2)
                pbcES.read(evb2)
                tc1 = VC(lambda e, b=b: e.tensor_tensor(
                    scr[3][:, 0:512], attnsb[:, b * 512:(b + 1) * 512],
                    rsgb[:], ALU.mult),
                    waits=attnsbS.rw() + rsgbS.rw() + scrS[3].ww())
                scrS[3].wrote(tc1)
                rsgbS.read(tc1)
                attnsbS.read(tc1)
                tc2 = VC(lambda e, b=b: e.tensor_tensor(
                    combb[:, b * 512:(b + 1) * 512], scr[3][:, 0:512], pmoE[:],
                    ALU.add),
                    waits=pmoES.rw() + (combbS.ww() if b == 0 else []))
                if b == 0:
                    combbS.wrote(tc2)
                else:
                    combbS.add(tc2)
                scrS[3].read(tc2)
                pmoES.read(tc2)
                ecombs.append(tc2)
            st = SCD("st_comb", lambda e, h=h: e.dma_start(out=comb_s[h], in_=combb[:]),
                     waits=ecombs)
            combbS.read(st)
            combH[h].wrote(st)

    sG.close()

    sH = contextlib.ExitStack()
    combsl = sbuf(sH, "combsl", [128, NCH * OWN], BF16)
    combslS = [Slot() for _ in range(NCH)]
    bar7 = P.barrier()
    load_W(woT, guard=bar7)
    glast = None
    for j in range(NCH):
        glast = SYD("combsl", lambda e, j=j: e.dma_start(
            out=combsl[:, j * OWN:(j + 1) * OWN], in_=comb_s[j]),
            waits=combslS[j].ww() + combH[j].rw() + (bar7 if j < 2 else []))
        combH[j].read(glast)
    for j in range(NCH):
        combslS[j].wrote(glast)
    with nc.psum_tensor("ps0", [128, OWN], F32) as ps0, \
         nc.psum_tensor("ps1", [128, OWN], F32) as ps1:
        pss = [ps0, ps1]
        psS = [Slot(guard=bar7), Slot(guard=bar7)]
        for ic in range(NCH):
            pp = ic % 2
            ps, pS = pss[pp], psS[pp]
            lastmm = None
            for j in range(NCH):
                for bb in range(2):
                    w = combslS[j].rw() + WsbS[j].rw()
                    if j == 0 and bb == 0:
                        w += pS.ww()
                    mm = TC(lambda e, j=j, bb=bb, ic=ic, ps=ps: e.matmul(
                        ps[:, bb * 512:(bb + 1) * 512],
                        Wsb[:, j * H + ic * 128:j * H + (ic + 1) * 128],
                        combsl[:, j * OWN + bb * 512:j * OWN + (bb + 1) * 512],
                        start=(j == 0), stop=(j == NCH - 1)), waits=w)
                    combslS[j].read(mm)
                    WsbS[j].read(mm)
                    lastmm = mm
            pS.wrote(lastmm)
            ev = VC(lambda e, ps=ps, pp=pp: e.tensor_copy(
                vob[:, pp * S:pp * S + OWN], ps[:]),
                waits=pS.rw() + vobS[pp].ww())
            vobS[pp].wrote(ev)
            pS.read(ev)
            st = SCD(f"st_vob{pp}", lambda e, ic=ic, pp=pp: e.dma_start(
                out=ss_s[ic], in_=vob[:, pp * S:pp * S + OWN]),
                waits=vobS[pp].rw())
            vobS[pp].read(st)
            ssH[ic].wrote(st)
    sH.close()
    sE.close()

    sI = contextlib.ExitStack()
    xnown = sbuf(sI, "xnown", [128, NCH * OWN], BF16)
    xnownS = [Slot() for _ in range(NCH)]
    wgsl = sbuf(sI, "wgsl", [128, 2 * NCH * 256], BF16)
    wgslS = [[Slot() for _ in range(4)] for _ in range(2)]
    wusl = sbuf(sI, "wusl", [128, 2 * NCH * 256], BF16)
    wuslS = [[Slot() for _ in range(4)] for _ in range(2)]
    fuseb = sbuf(sI, "fuseb", [128, 2 * 512], BF16); fusebS = [Slot(), Slot()]
    silug = sbuf(sI, "silug", [128, 2 * 512]); silugS = [Slot(), Slot()]
    wdsl = sbuf(sI, "wdsl", [128, 2 * 1024], BF16); wdslS = [Slot(), Slot()]
    gbuf = sbuf(sI, "gbuf", [128, 2 * 512], BF16); gbufS = [Slot(), Slot()]
    xtile = sbuf(sI, "xtile", [128, 2 * 512]); xtileS = [Slot(), Slot()]
    ybuf = sbuf(sI, "ybuf", [128, 2 * 512]); ybufS = [Slot(), Slot()]
    stile = sbuf(sI, "stile", [128, 2 * 512], BF16); stileS = [Slot(), Slot()]
    fuse_sb = sbuf(sI, "fuse_sb", [128, NOC * 512], BF16)
    fuse_sbS = [Slot() for _ in range(NOC)]

    bar8 = P.barrier()
    glast = None
    for j in range(NCH):
        glast = SYD("xnown", lambda e, j=j: e.dma_start(
            out=xnown[:, j * OWN:(j + 1) * OWN], in_=xn_s[j][:, 0:OWN]),
            waits=xnownS[j].ww() + xnH[j].rw() + (bar8 if j < 2 else []))
        xnH[j].read(glast)
    for j in range(NCH):
        xnownS[j].wrote(glast)
    final_stores_by_chan = {}
    for tb in range(NOB):
        bargu = P.barrier()
        with nc.psum_tensor(f"pgu{tb}", [128, 8 * 512], F32) as pgu:
            pguS = [Slot(guard=bargu) for _ in range(8)]
            for og2 in range(32):
                gpar2 = og2 % 2
                dpar = og2 % 2
                base = gpar2 * 4
                lasts = [None] * 4
                for q in range(4):
                    wg_v = wgsl[:, (dpar * NCH + q * 4) * 256:(dpar * NCH + q * 4 + 4) * 256]
                    src_g = wgb[q * 4:(q + 1) * 4, :, og2 * 256:(og2 + 1) * 256] \
                        .rearrange("j p f -> p j f")
                    tg_ = SYD(f"wg{dpar}q{q}", lambda e, wg_v=wg_v, src_g=src_g: e.dma_start(
                        out=wg_v.rearrange("p (j f) -> p j f", j=4), in_=src_g),
                        waits=wgslS[dpar][q].ww() + wgbH[q * 4].rw())
                    wgslS[dpar][q].wrote(tg_)
                    wgbH[q * 4].read(tg_)
                    wu_v = wusl[:, (dpar * NCH + q * 4) * 256:(dpar * NCH + q * 4 + 4) * 256]
                    src_u = wub[q * 4:(q + 1) * 4, :, og2 * 256:(og2 + 1) * 256] \
                        .rearrange("j p f -> p j f")
                    tu_ = SYD(f"wu{dpar}q{q}", lambda e, wu_v=wu_v, src_u=src_u: e.dma_start(
                        out=wu_v.rearrange("p (j f) -> p j f", j=4), in_=src_u),
                        waits=wuslS[dpar][q].ww() + wubH[q * 4].rw())
                    wuslS[dpar][q].wrote(tu_)
                    wubH[q * 4].read(tu_)
                for j in range(NCH):
                    q = j // 4
                    wg_v = wgsl[:, (dpar * NCH + j) * 256:(dpar * NCH + j + 1) * 256]
                    wu_v = wusl[:, (dpar * NCH + j) * 256:(dpar * NCH + j + 1) * 256]
                    for c2 in range(2):
                        w = wgslS[dpar][q].rw() + xnownS[j].rw() + \
                            (pguS[base + c2].ww() if j == 0 else [])
                        mm = TC(lambda e, j=j, c2=c2, tb=tb, wg_v=wg_v, base=base: e.matmul(
                            pgu[:, (base + c2) * 512:(base + c2 + 1) * 512],
                            wg_v[:, c2 * 128:(c2 + 1) * 128],
                            xnown[:, j * OWN + tb * 512:j * OWN + (tb + 1) * 512],
                            start=(j == 0), stop=(j == NCH - 1)), waits=w)
                        wgslS[dpar][q].read(mm)
                        xnownS[j].read(mm)
                        lasts[c2] = mm
                        w = wuslS[dpar][q].rw() + xnownS[j].rw() + \
                            (pguS[base + 2 + c2].ww() if j == 0 else [])
                        mm = TC(lambda e, j=j, c2=c2, tb=tb, wu_v=wu_v, base=base: e.matmul(
                            pgu[:, (base + 2 + c2) * 512:(base + 3 + c2) * 512],
                            wu_v[:, c2 * 128:(c2 + 1) * 128],
                            xnown[:, j * OWN + tb * 512:j * OWN + (tb + 1) * 512],
                            start=(j == 0), stop=(j == NCH - 1)), waits=w)
                        wuslS[dpar][q].read(mm)
                        xnownS[j].read(mm)
                        lasts[2 + c2] = mm
                for c2 in range(2):
                    oc = og2 * 2 + c2
                    pguS[base + c2].wrote(lasts[c2])
                    pguS[base + 2 + c2].wrote(lasts[2 + c2])
                    fpar = c2 % 2
                    sg_ = SC(lambda e, c2=c2, oc=oc, fpar=fpar, base=base: e.activation(
                        silug[:, fpar * 512:(fpar + 1) * 512],
                        pgu[:, (base + c2) * 512:(base + c2 + 1) * 512],
                        AF.Silu, bias=zeroc[:, 0:1], scale=sgb[:, oc:oc + 1]),
                        waits=pguS[base + c2].rw() + silugS[fpar].ww() + sgS.rw())
                    silugS[fpar].wrote(sg_)
                    pguS[base + c2].read(sg_)
                    fu_ = VC(lambda e, c2=c2, oc=oc, fpar=fpar, base=base:
                             e.scalar_tensor_tensor(
                        fuse_sb[:, oc * 512:(oc + 1) * 512],
                        pgu[:, (base + 2 + c2) * 512:(base + 3 + c2) * 512],
                        sub_[:, oc:oc + 1], silug[:, fpar * 512:(fpar + 1) * 512],
                        ALU.mult, ALU.mult),
                        waits=pguS[base + 2 + c2].rw() + silugS[fpar].rw() +
                        fuse_sbS[oc].ww() + suS.rw())
                    fuse_sbS[oc].wrote(fu_)
                    silugS[fpar].read(fu_)
                    pguS[base + 2 + c2].read(fu_)
        with nc.psum_tensor(f"pd{tb}", [128, 8 * 512], F32) as pd:
            bard = P.barrier()
            pdS = [Slot(guard=bard) for _ in range(8)]
            for ig in range(2):
                lasts = [None] * 8
                for o in range(NOC):
                    dpar = o % 2
                    wd_v = wdsl[:, dpar * 1024:(dpar + 1) * 1024]
                    td = SYD(f"wd{dpar}", lambda e, o=o, ig=ig, wd_v=wd_v: e.dma_start(
                        out=wd_v, in_=wdb[o][:, ig * 1024:(ig + 1) * 1024]),
                        waits=wdslS[dpar].ww() + wdbH[o].rw())
                    wdslS[dpar].wrote(td)
                    wdbH[o].read(td)
                    ft_v = fuse_sb[:, o * 512:(o + 1) * 512]
                    for i8 in range(8):
                        w = wdslS[dpar].rw() + fuse_sbS[o].rw() + \
                            (pdS[i8].ww() if o == 0 else [])
                        mm = TC(lambda e, i8=i8, o=o, wd_v=wd_v, ft_v=ft_v: e.matmul(
                            pd[:, i8 * 512:(i8 + 1) * 512],
                            wd_v[:, i8 * 128:(i8 + 1) * 128], ft_v,
                            start=(o == 0), stop=(o == NOC - 1)), waits=w)
                        wdslS[dpar].read(mm)
                        fuse_sbS[o].read(mm)
                        lasts[i8] = mm
                for i8 in range(8):
                    ic = ig * 8 + i8
                    pdS[i8].wrote(lasts[i8])
                    gpar = i8 % 2
                    sg_ = SC(lambda e, i8=i8, ic=ic, gpar=gpar: e.activation(
                        gbuf[:, gpar * 512:(gpar + 1) * 512],
                        pd[:, i8 * 512:(i8 + 1) * 512],
                        AF.Silu, bias=zeroc[:, 0:1], scale=sdb[:, ic:ic + 1]),
                        waits=pdS[i8].rw() + gbufS[gpar].ww() + sdS.rw())
                    gbufS[gpar].wrote(sg_)
                    pdS[i8].read(sg_)
                    ts_ = SYD(f"stile{gpar}", lambda e, ic=ic, tb=tb, gpar=gpar: e.dma_start(
                        out=stile[:, gpar * 512:(gpar + 1) * 512],
                        in_=ss_s[ic][:, tb * 512:(tb + 1) * 512]),
                        waits=stileS[gpar].ww() + ssH[ic].rw())
                    stileS[gpar].wrote(ts_)
                    tx_ = SYD(f"xtile{gpar}", lambda e, ic=ic, tb=tb, gpar=gpar: e.dma_start(
                        out=xtile[:, gpar * 512:(gpar + 1) * 512],
                        in_=xT[ic * 128:(ic + 1) * 128, tb * 512:(tb + 1) * 512]),
                        waits=xtileS[gpar].ww())
                    xtileS[gpar].wrote(tx_)
                    t1 = VC(lambda e, gpar=gpar: e.tensor_tensor(
                        ybuf[:, gpar * 512:(gpar + 1) * 512],
                        stile[:, gpar * 512:(gpar + 1) * 512],
                        gbuf[:, gpar * 512:(gpar + 1) * 512], ALU.mult),
                        waits=stileS[gpar].rw() + gbufS[gpar].rw() + ybufS[gpar].ww())
                    ybufS[gpar].wrote(t1)
                    stileS[gpar].read(t1)
                    gbufS[gpar].read(t1)
                    t2 = VC(lambda e, gpar=gpar: e.tensor_tensor(
                        ybuf[:, gpar * 512:(gpar + 1) * 512],
                        ybuf[:, gpar * 512:(gpar + 1) * 512],
                        xtile[:, gpar * 512:(gpar + 1) * 512], ALU.add),
                        waits=xtileS[gpar].rw())
                    ybufS[gpar].wrote(t2)
                    xtileS[gpar].read(t2)
                    sty = SCD(f"st_y{gpar}", lambda e, ic=ic, tb=tb, gpar=gpar: e.dma_start(
                        out=yT[ic * 128:(ic + 1) * 128, tb * 512:(tb + 1) * 512],
                        in_=ybuf[:, gpar * 512:(gpar + 1) * 512]),
                        waits=ybufS[gpar].rw())
                    ybufS[gpar].read(sty)
                    final_stores_by_chan[gpar] = sty
    sI.close()

    P.c("scalar", lambda e: e.nop(), waits=list(final_stores_by_chan.values()))
    return _finish(nc, P, perm)


_NC_CACHE = {}


def _get_nc(dbg=()):
    key = tuple(sorted(dbg))
    if key not in _NC_CACHE:
        _NC_CACHE[key] = build_nc(key)
    return _NC_CACHE[key]


def make_in_maps(inputs):
    x = np.asarray(inputs["L0_attn_hidden_states"], dtype=np.float32)
    wq = np.asarray(inputs["L0_attn_Wq"], dtype=np.float32)
    wk = np.asarray(inputs["L0_attn_Wk"], dtype=np.float32)
    wv = np.asarray(inputs["L0_attn_Wv"], dtype=np.float32)
    wo = np.asarray(inputs["L0_attn_Wo"], dtype=np.float32)
    wg8 = np.ascontiguousarray(np.asarray(inputs["L0_ffn_w_gate"], dtype=np.int8))
    wu8 = np.ascontiguousarray(np.asarray(inputs["L0_ffn_w_up"], dtype=np.int8))
    wd8 = np.ascontiguousarray(np.asarray(inputs["L0_ffn_w_down"], dtype=np.int8))
    sg = np.asarray(inputs["L0_ffn_s_gate"], dtype=np.float32)
    su = np.asarray(inputs["L0_ffn_s_up"], dtype=np.float32)
    sd = np.asarray(inputs["L0_ffn_s_down"], dtype=np.float32)
    gate = np.asarray(inputs["L0_attn_gate"], dtype=np.float32)

    wqT = np.ascontiguousarray(wq.T)
    wkT = np.ascontiguousarray(wk.T)
    wvT = np.ascontiguousarray(wv.T)
    woT = np.ascontiguousarray(wo.T)
    sgI = np.ascontiguousarray(sg.reshape(64, 128).T)
    suI = np.ascontiguousarray(su.reshape(64, 128).T)
    sdI = np.ascontiguousarray(sd.reshape(16, 128).T)
    gateI = np.ascontiguousarray(gate.reshape(1, 16))

    in_maps = []
    for core in range(8):
        b, half = core // 2, core % 2
        prm = np.concatenate([np.arange(half * OWN, (half + 1) * OWN),
                              np.arange((1 - half) * OWN, (2 - half) * OWN)])
        xTc = np.ascontiguousarray(x[b].T[:, prm])
        posc = np.ascontiguousarray(
            np.tile(prm.astype(np.float32)[None, :], (128, 1)))
        in_maps.append({
            "xT": xTc, "wqT": wqT, "wkT": wkT, "wvT": wvT, "woT": woT,
            "wg8": wg8, "wu8": wu8, "wd8": wd8,
            "sgI": sgI, "suI": suI, "sdI": sdI, "gateI": gateI, "posB": posc,
        })
    return in_maps


def kernel(**inputs):
    nc = _get_nc()
    in_maps = make_in_maps(inputs)
    res = run_bass_kernel_spmd(nc, in_maps, core_ids=list(range(8)))
    y = np.empty((NB, S, H), dtype=np.float32)
    for core in range(8):
        b, half = core // 2, core % 2
        y[b, half * OWN:(half + 1) * OWN, :] = np.asarray(res.results[core]["yT"]).T
    return y


# version 18
# speedup vs baseline: 1.0495x; 1.0199x over previous
import contextlib
import math
import numpy as np
import ml_dtypes
import concourse.bass as bass
import concourse.mybir as mybir
from concourse.bass_utils import run_bass_kernel_spmd

F32 = mybir.dt.float32
BF16 = mybir.dt.bfloat16
I8 = mybir.dt.int8
I32 = mybir.dt.int32
AF = mybir.ActivationFunctionType
ALU = mybir.AluOpType
AX = mybir.AxisListType.X

H = 2048
S = 2048
OWN = 1024
NB = 4
NHEAD = 16
INTER = 8192
NCH = 16
NTT = 16
NTB = 4
NOB = 2
NOC = 64

TWO_PI = 2.0 * math.pi
CW_C1 = float(np.float32(6.28125))
CW_C2 = float(np.float32(TWO_PI - CW_C1))
CW_C3 = float(np.float32(TWO_PI - CW_C1 - np.float64(np.float32(TWO_PI - CW_C1))))
INV_2PI = 1.0 / TWO_PI
PI = math.pi
PI_LO = float(np.float32(3.1415925))
HALF_PI = float(np.float32(PI / 2.0))
TWO_PI_F = float(np.float32(TWO_PI))
SCALE_QK = 1.0 / math.sqrt(128.0)

ENGS = ("sync", "tensor", "vector", "scalar", "gpsimd")
CSEMS = ("tensor.c", "vector.c", "scalar.c", "gpsimd.c")


class _Op:
    __slots__ = ("eng", "sem", "amt", "fn", "waits", "need", "val")

    def __init__(self, eng, sem, amt, fn, waits, need):
        self.eng, self.sem, self.amt, self.fn = eng, sem, amt, fn
        self.waits, self.need, self.val = waits, need, None


class Prog:

    def __init__(self):
        self.streams = {e: [] for e in ENGS}
        self.chans = {}

    def _emit(self, eng, sem, amt, fn, waits, need):
        kept = []
        for w in waits:
            if w is None:
                continue
            if w.eng == eng and w.sem in CSEMS and sem in CSEMS:
                continue
            kept.append(w)
        op = _Op(eng, sem, amt, fn, kept, need)
        for w in kept:
            w.need = True
        self.streams[eng].append(op)
        return op

    def c(self, eng, fn, waits=()):
        return self._emit(eng, eng + ".c", 1, fn, list(waits), False)

    def d(self, eng, chan, fn, waits=()):
        op = self._emit(eng, chan, 16, fn, list(waits), True)
        n = self.chans.get(chan, 0) + 16
        self.chans[chan] = n
        op.val = n
        return op

    def barrier(self, exclude=()):
        out = []
        seen = {}
        for e in ENGS:
            for op in reversed(self.streams[e]):
                if op.sem in CSEMS:
                    op.need = True
                    out.append(op)
                    break
            for op in reversed(self.streams[e]):
                if op.sem not in CSEMS and op.sem not in seen \
                        and op.sem not in exclude:
                    seen[op.sem] = op
        out.extend(seen.values())
        return out

    def resolve(self):
        counts = {s: 0 for s in CSEMS}
        for e in ENGS:
            for op in self.streams[e]:
                if op.need and op.sem in CSEMS:
                    counts[op.sem] += op.amt
                    op.val = counts[op.sem]
        return counts

    def replay(self, eng_obj, eng_name, semh):
        last = {}
        for op in self.streams[eng_name]:
            for w in op.waits:
                if w.val is None:
                    raise RuntimeError("unresolved wait")
                if last.get(w.sem, 0) < w.val:
                    eng_obj.wait_ge(semh[w.sem], w.val)
                    last[w.sem] = w.val
            inst = op.fn(eng_obj)
            if op.need:
                inst.then_inc(semh[op.sem], op.amt)


PRECAST_CH = ("wgbp", "wubp", "wdbp")


class Slot:

    def __init__(self, guard=()):
        self.prods = []
        self.cons = {}
        self.guards = [g for g in guard if g is not None]

    def ww(self):
        return self.guards + list(self.cons.values()) + self.prods

    def wrote(self, t, reset=True):
        if reset:
            self.prods = [t]
            self.cons = {}
            self.guards = []
        else:
            self.prods.append(t)

    def add(self, t):
        self.prods.append(t)

    def rw(self):
        return list(self.prods)

    def read(self, t):
        self.cons[t.eng] = t


def _finish(nc, P, perm):
    P.resolve()
    semh = {}
    for s in CSEMS:
        semh[s] = perm.enter_context(nc.semaphore(s.replace(".", "_")))
    for s in P.chans:
        semh[s] = perm.enter_context(nc.semaphore("ch_" + s))
    with nc.Block() as block:
        @block.sync
        def _(e):
            P.replay(e, "sync", semh)

        @block.tensor
        def _(e):
            P.replay(e, "tensor", semh)

        @block.vector
        def _(e):
            P.replay(e, "vector", semh)

        @block.scalar
        def _(e):
            P.replay(e, "scalar", semh)

        @block.gpsimd
        def _(e):
            P.replay(e, "gpsimd", semh)

    perm.close()
    return nc


def build_nc(dbg=(), stop_after=None):
    nc = bass.Bass("TRN2", target_bir_lowering=False, detect_race_conditions=False)
    P = Prog()

    def dram(name, shape, dtype, kind="Internal"):
        k = "ExternalOutput" if (name in dbg and kind == "Internal") else kind
        return nc.dram_tensor(name, shape, dtype, kind=k).ap()

    xT = dram("xT", [H, S], F32, "ExternalInput")
    wqT = dram("wqT", [H, H], F32, "ExternalInput")
    wkT = dram("wkT", [H, H], F32, "ExternalInput")
    wvT = dram("wvT", [H, H], F32, "ExternalInput")
    woT = dram("woT", [H, H], F32, "ExternalInput")
    wg8 = dram("wg8", [H, INTER], I8, "ExternalInput")
    wu8 = dram("wu8", [H, INTER], I8, "ExternalInput")
    wd8 = dram("wd8", [INTER, H], I8, "ExternalInput")
    sgI = dram("sgI", [128, 64], F32, "ExternalInput")
    suI = dram("suI", [128, 64], F32, "ExternalInput")
    sdI = dram("sdI", [128, 16], F32, "ExternalInput")
    gateI = dram("gateI", [1, 16], F32, "ExternalInput")
    posB = dram("posB", [128, S], F32, "ExternalInput")
    yT = dram("yT", [H, OWN], F32, "ExternalOutput")

    xn_s = dram("xn_s", [NCH, 128, S], BF16)
    tbl_s = dram("tbl_s", [8, 2, 128, S], BF16)
    kfm_s = dram("kfm_s", [NCH, 128, S], BF16)
    qs_s = dram("qs_s", [NCH, 128, OWN], BF16)
    qf_s = dram("qf_s", [NCH, 128, OWN], BF16)
    vt_s = dram("vt_s", [NCH, NTT, 128, 128], BF16)
    wgb = dram("wgb", [NCH, 128, INTER], BF16)
    wub = dram("wub", [NCH, 128, INTER], BF16)
    wdb = dram("wdb", [NOC, 128, H], BF16)
    fuse_s = dram("fuse_s", [NOC, 128, OWN], BF16)
    comb_s = dram("comb_s", [NCH, 128, OWN], BF16)
    ss_s = dram("ss_s", [NCH, 128, OWN], BF16)
    diag_s = dram("diag_s", [NHEAD, 4, 1024], F32)
    diagU_s = dram("diagU_s", [128, 16], F32)

    xnH = [Slot() for _ in range(NCH)]
    tblH = [[Slot(), Slot()] for _ in range(8)]
    kfmH = [Slot() for _ in range(NCH)]
    qsH = [Slot() for _ in range(NCH)]
    qfH = [Slot() for _ in range(NCH)]
    vtH = [Slot() for _ in range(NCH)]
    wgbH = [Slot() for _ in range(NCH)]
    wubH = [Slot() for _ in range(NCH)]
    wdbH = [Slot() for _ in range(NOC)]
    fuseH = [Slot() for _ in range(NOC)]
    combH = [Slot() for _ in range(NCH)]
    ssH = [Slot() for _ in range(NCH)]

    perm = contextlib.ExitStack()

    def sbuf(stk, name, shape, dt=F32):
        return stk.enter_context(nc.sbuf_tensor(name, shape, dt))

    scr = [sbuf(perm, f"scr{i}", [128, S]) for i in range(4)]
    scrS = [Slot() for _ in range(4)]
    tblb = sbuf(perm, "tblb", [128, 2 * S], BF16)
    tblbS = [Slot() for _ in range(4)]
    tblfS = [Slot(), Slot()]
    omgc = sbuf(perm, "omgc", [128, 8]); omgcS = Slot()
    grow = sbuf(perm, "grow", [1, 16]); growS = Slot()
    ogrow = sbuf(perm, "ogrow", [1, 16]); ogrowS = Slot()
    qm2 = sbuf(perm, "qm2", [1, 16]); qm2S = Slot()
    km2 = sbuf(perm, "km2", [1, 16]); km2S = Slot()
    urow = sbuf(perm, "urow", [1, 16]); urowS = Slot()
    u2row = sbuf(perm, "u2row", [1, 16]); u2S = Slot()
    tmp11 = sbuf(perm, "tmp11", [1, 2]); tmp11S = Slot()
    m2all = sbuf(perm, "m2all", [1, 128]); m2allS = Slot()
    negU = sbuf(perm, "negU", [128, 16]); negUS = Slot()
    rl_r = sbuf(perm, "rl_r", [1, 1024]); rlS = Slot()
    rs_r = sbuf(perm, "rs_r", [1, 512]); rsS = Slot()
    onesc = sbuf(perm, "onesc", [128, 1])
    onescb = sbuf(perm, "onescb", [128, 1], BF16)
    onesM = sbuf(perm, "onesM", [128, 128], BF16)
    onesr = sbuf(perm, "onesr", [1, 128])
    zeroc = sbuf(perm, "zeroc", [128, 1])
    epsc = sbuf(perm, "epsc", [128, 1])
    pjt = sbuf(perm, "pjt", [128, 128])
    ident = sbuf(perm, "ident", [128, 128], BF16); identS = Slot()
    constS = Slot()
    sgb = sbuf(perm, "sgb", [128, 64]); sgS = Slot()
    sub_ = sbuf(perm, "sub_", [128, 64]); suS = Slot()
    sdb = sbuf(perm, "sdb", [128, 16]); sdS = Slot()
    k2scr = sbuf(perm, "k2scr", [128, 2 * 512], BF16); k2S = [Slot(), Slot()]

    SYD = lambda ch, fn, waits=(): P.d("sync", ch, fn, waits)
    GPD = lambda ch, fn, waits=(): P.d("gpsimd", ch, fn, waits)
    SCD = lambda ch, fn, waits=(): P.d("scalar", ch, fn, waits)
    TC = lambda fn, waits=(): P.c("tensor", fn, waits)
    VC = lambda fn, waits=(): P.c("vector", fn, waits)
    SC = lambda fn, waits=(): P.c("scalar", fn, waits)
    GC = lambda fn, waits=(): P.c("gpsimd", fn, waits)

    s12 = contextlib.ExitStack()

    sE = contextlib.ExitStack()
    Wsb = sbuf(sE, "Wsb", [128, NCH * H], BF16)
    WsbS = [Slot() for _ in range(NCH)]
    vob = sbuf(sE, "vob", [128, 2 * S], BF16)
    vobS = [Slot(), Slot()]

    def load_W(src, guard=()):
        gw = list(guard)
        for g in range(4):
            last = None
            for j in range(g * 4, (g + 1) * 4):
                last = GPD(f"wsbg{g}", lambda e, j=j, src=src: e.dma_start(
                    out=Wsb[:, j * H:(j + 1) * H], in_=src[j * 128:(j + 1) * 128, :]),
                    waits=WsbS[j].ww() + gw)
            for j in range(g * 4, (g + 1) * 4):
                WsbS[j].wrote(last)

    t = GC(lambda e: e.iota(omgc[:], [[128, 8]], channel_multiplier=1,
                            allow_small_or_imprecise_dtypes=True))
    omgcS.wrote(t)
    pjS = Slot()
    t = GC(lambda e: e.iota(pjt[:], [[-1, 128]], channel_multiplier=1,
                            allow_small_or_imprecise_dtypes=True))
    pjS.wrote(t)
    load_W(wvT)
    c1 = SYD("consts", lambda e: e.dma_start(out=grow[:], in_=gateI[:]))
    c2 = SYD("consts", lambda e: e.dma_start(out=sgb[:], in_=sgI[:]))
    c3 = SYD("consts", lambda e: e.dma_start(out=sub_[:], in_=suI[:]))
    c4 = SYD("consts", lambda e: e.dma_start(out=sdb[:], in_=sdI[:]))
    for _s in (growS, sgS, suS, sdS):
        _s.wrote(c4)
    VC(lambda e: e.memset(onesc[:], 1.0))
    VC(lambda e: e.memset(onesM[:], 1.0))
    VC(lambda e: e.memset(onescb[:], 1.0))
    VC(lambda e: e.memset(onesr[:], 1.0))
    VC(lambda e: e.memset(zeroc[:], 0.0))
    VC(lambda e: e.memset(epsc[:], 1e-5))
    qm2S.wrote(VC(lambda e: e.memset(qm2[:], 0.0)))
    km2S.wrote(VC(lambda e: e.memset(km2[:], 0.0)))
    VC(lambda e: e.memset(tmp11[:], 0.0))
    t = VC(lambda e: e.tensor_scalar(ident[:], pjt[:], 0.0, None, ALU.is_equal),
           waits=pjS.rw())
    identS.wrote(t)
    constS.wrote(t)
    tg = SC(lambda e: e.activation(grow[:], grow[:], AF.Sigmoid,
                                   bias=zeroc[0:1, 0:1], scale=1.0),
            waits=growS.rw() + constS.rw())
    growS.wrote(tg)
    tog = VC(lambda e: e.tensor_scalar(ogrow[:], grow[:], -1.0, 1.0, ALU.mult, ALU.add),
             waits=growS.rw())
    ogrowS.wrote(tog)
    growS.read(tog)
    tw = SC(lambda e: e.activation(omgc[:], omgc[:], AF.Exp, bias=zeroc[:, 0:1],
                                   scale=-math.log(10000.0) / 1024.0),
            waits=omgcS.rw() + constS.rw())
    omgcS.wrote(tw)


    rows_a = sbuf(s12, "rows_a", [1, S]); rowsaS = Slot()
    rows_b = sbuf(s12, "rows_b", [1, S]); rowsbS = Slot()
    rb_b = sbuf(s12, "rb_b", [128, S]); rbS = Slot()
    nm_b = sbuf(s12, "nm_b", [128, S]); nmS = Slot()
    xnp = sbuf(s12, "xnp", [128, 2 * S], BF16); xnpS = [Slot(), Slot()]

    with nc.psum_tensor("pstx", [1, S], F32) as pstx, \
         nc.psum_tensor("pstx2", [1, S], F32) as pstx2:
        pstxS, pstx2S = Slot(), Slot()
        lastx = lastx2 = None
        for j in range(NCH):
            xb, xbS = scr[j % 2], scrS[j % 2]
            t = (SYD if j % 2 == 0 else SCD)(
                f"scrx{j % 2}", lambda e, j=j, xb=xb: e.dma_start(
                    out=xb[:], in_=xT[j * 128:(j + 1) * 128, :]), waits=xbS.ww())
            xbS.wrote(t)
            tq = VC(lambda e, xb=xb: e.tensor_tensor(scr[2][:], xb[:], xb[:], ALU.mult),
                    waits=xbS.rw() + scrS[2].ww())
            xbS.read(tq)
            scrS[2].wrote(tq)
            for blk in range(4):
                mm = TC(lambda e, xb=xb, blk=blk, j=j: e.matmul(
                    pstx[0:1, blk * 512:(blk + 1) * 512], onesc[:],
                    xb[:, blk * 512:(blk + 1) * 512],
                    start=(j == 0), stop=(j == NCH - 1)),
                    waits=xbS.rw() + (constS.rw() if j == 0 and blk == 0 else []))
                xbS.read(mm)
                lastx = mm
            for blk in range(4):
                mm = TC(lambda e, blk=blk, j=j: e.matmul(
                    pstx2[0:1, blk * 512:(blk + 1) * 512], onesc[:],
                    scr[2][:, blk * 512:(blk + 1) * 512],
                    start=(j == 0), stop=(j == NCH - 1)), waits=scrS[2].rw())
                scrS[2].read(mm)
                lastx2 = mm
        pstxS.wrote(lastx)
        pstx2S.wrote(lastx2)
        tm = VC(lambda e: e.tensor_scalar(rows_b[:], pstx[0:1, :], 1.0 / H, None, ALU.mult),
                waits=pstxS.rw())
        rowsbS.wrote(tm)
        pstxS.read(tm)
        te = VC(lambda e: e.tensor_scalar(rows_a[:], pstx2[0:1, :], 1.0 / H, None, ALU.mult),
                waits=pstx2S.rw())
        rowsaS.wrote(te)
        pstx2S.read(te)
        t3 = VC(lambda e: e.tensor_tensor(pstx[0:1, :], rows_b[:], rows_b[:], ALU.mult))
        pstxS.wrote(t3)
        t4 = VC(lambda e: e.tensor_tensor(rows_a[:], rows_a[:], pstx[0:1, :], ALU.subtract))
        rowsaS.wrote(t4)
        pstxS.read(t4)
        t5 = SC(lambda e: e.activation(pstx2[0:1, :], rows_a[:], AF.Sqrt,
                                       bias=epsc[0:1, 0:1], scale=1.0),
                waits=rowsaS.rw() + pstx2S.rw())
        pstx2S.wrote(t5)
        rowsaS.read(t5)
        t6 = VC(lambda e: e.reciprocal(rows_a[:], pstx2[0:1, :]),
                waits=pstx2S.rw() + rowsaS.ww())
        rowsaS.wrote(t6)
        pstx2S.read(t6)
        t7 = VC(lambda e: e.scalar_tensor_tensor(rows_b[:], rows_b[:], -1.0, rows_a[:],
                                                 ALU.mult, ALU.mult))
        rowsbS.wrote(t7)
        rowsaS.read(t7)
    with nc.psum_tensor("pbc1", [128, S], F32) as pbc1:
        pbc1S = Slot(guard=[t7])
        lastmm = None
        for blk in range(4):
            mm = TC(lambda e, blk=blk: e.matmul(
                pbc1[:, blk * 512:(blk + 1) * 512], onesr[0:1, :],
                rows_a[0:1, blk * 512:(blk + 1) * 512], start=True, stop=True),
                waits=rowsaS.rw() + (pbc1S.ww() if blk == 0 else []))
            rowsaS.read(mm)
            lastmm = mm
        pbc1S.wrote(lastmm)
        ev = VC(lambda e: e.tensor_copy(rb_b[:], pbc1[:]), waits=pbc1S.rw())
        rbS.wrote(ev)
        pbc1S.read(ev)
        lastmm = None
        for blk in range(4):
            mm = TC(lambda e, blk=blk: e.matmul(
                pbc1[:, blk * 512:(blk + 1) * 512], onesr[0:1, :],
                rows_b[0:1, blk * 512:(blk + 1) * 512], start=True, stop=True),
                waits=rowsbS.rw() + [pbc1S.cons.get("vector")])
            rowsbS.read(mm)
            lastmm = mm
        pbc1S.wrote(lastmm)
        ev = VC(lambda e: e.tensor_copy(nm_b[:], pbc1[:]), waits=pbc1S.rw())
        nmS.wrote(ev)
        pbc1S.read(ev)

    cp_ = SYD("posbL", lambda e: e.dma_start(out=scr[3][:], in_=posB[:]),
              waits=scrS[3].ww())
    scrS[3].wrote(cp_)

    for j in range(NCH):
        xb, xbS = scr[j % 2], scrS[j % 2]
        t = (SYD if j % 2 == 0 else SCD)(
            f"scrx{j % 2}", lambda e, j=j, xb=xb: e.dma_start(
                out=xb[:], in_=xT[j * 128:(j + 1) * 128, :]), waits=xbS.ww())
        xbS.wrote(t)
        t1 = VC(lambda e, xb=xb: e.tensor_tensor(scr[2][:], xb[:], rb_b[:], ALU.mult),
                waits=xbS.rw() + rbS.rw() + scrS[2].ww())
        xbS.read(t1)
        rbS.read(t1)
        scrS[2].wrote(t1)
        par = j % 2
        xo = xnp[:, par * S:(par + 1) * S]
        t2 = VC(lambda e, xo=xo: e.tensor_tensor(xo, scr[2][:], nm_b[:], ALU.add),
                waits=nmS.rw() + xnpS[par].ww())
        nmS.read(t2)
        scrS[2].read(t2)
        xnpS[par].wrote(t2)
        st = SCD(f"st_xnp{par}", lambda e, j=j, xo=xo: e.dma_start(out=xn_s[j], in_=xo),
                 waits=xnpS[par].rw())
        xnpS[par].read(st)
        xnH[j].wrote(st)

    def emit_table_block(pr):
        if True:
            a0 = SC(lambda e, pr=pr: e.activation(scr[2][:], scr[3][:], AF.Copy,
                                                  bias=0.0,
                                                  scale=omgc[:, pr:pr + 1]),
                    waits=scrS[2].ww() + scrS[3].rw() + omgcS.rw())
            scrS[2].wrote(a0)
            scrS[3].read(a0)
            omgcS.read(a0)
            t1 = VC(lambda e: e.tensor_scalar(i32scr[:], scr[2][:], INV_2PI, None,
                                              ALU.mult),
                    waits=scrS[2].rw() + i32S.ww())
            i32S.wrote(t1)
            scrS[2].read(t1)
            t3 = SC(lambda e: e.copy(scr[1][:], i32scr[:]),
                    waits=i32S.rw() + scrS[1].ww())
            scrS[1].wrote(t3)
            i32S.read(t3)
            t4a = VC(lambda e: e.scalar_tensor_tensor(scr[2][:], scr[1][:], -CW_C1,
                                                      scr[2][:], ALU.mult, ALU.add),
                     waits=scrS[1].rw() + scrS[2].rw())
            scrS[2].wrote(t4a)
            t4b = VC(lambda e: e.scalar_tensor_tensor(scr[2][:], scr[1][:], -CW_C2,
                                                      scr[2][:], ALU.mult, ALU.add))
            scrS[2].wrote(t4b)
            t4 = VC(lambda e: e.scalar_tensor_tensor(scr[2][:], scr[1][:], -CW_C3,
                                                     scr[2][:], ALU.mult, ALU.add))
            scrS[2].wrote(t4)
            scrS[1].read(t4)
            wh1 = VC(lambda e: e.tensor_scalar(scr[0][:], scr[2][:], PI_LO, None,
                                               ALU.is_gt), waits=scrS[0].ww())
            scrS[0].wrote(wh1)
            wh2 = VC(lambda e: e.scalar_tensor_tensor(scr[2][:], scr[0][:], -TWO_PI_F,
                                                      scr[2][:], ALU.mult, ALU.add))
            scrS[2].wrote(wh2)
            scrS[0].read(wh2)
            wl1 = VC(lambda e: e.tensor_scalar(scr[0][:], scr[2][:], -PI_LO, None,
                                               ALU.is_lt), waits=scrS[0].ww())
            scrS[0].wrote(wl1)
            wl2 = VC(lambda e: e.scalar_tensor_tensor(scr[2][:], scr[0][:], TWO_PI_F,
                                                      scr[2][:], ALU.mult, ALU.add))
            scrS[2].wrote(wl2)
            scrS[0].read(wl2)
            for ti in (0, 1):
                if ti == 0:
                    t5 = VC(lambda e: e.tensor_scalar(scr[0][:], scr[2][:],
                                                      PI_LO, -PI_LO, ALU.min, ALU.max),
                            waits=scrS[0].ww())
                else:
                    tm_ = VC(lambda e: e.tensor_scalar(scr[0][:], scr[2][:],
                                                       HALF_PI, None, ALU.is_gt),
                             waits=scrS[0].ww())
                    scrS[0].wrote(tm_)
                    tw_ = VC(lambda e: e.scalar_tensor_tensor(scr[0][:], scr[0][:],
                                                              -TWO_PI_F, scr[2][:],
                                                              ALU.mult, ALU.add))
                    scrS[0].wrote(tw_)
                    t5 = VC(lambda e: e.tensor_scalar(scr[0][:], scr[0][:],
                                                      HALF_PI, PI_LO, ALU.add, ALU.min))
                scrS[0].wrote(t5)
                scrS[2].read(t5)
                dst = tblb[:, ti * S:(ti + 1) * S]
                s5 = SC(lambda e, dst=dst: e.activation(dst, scr[0][:], AF.Sin,
                                                        bias=zeroc[:, 0:1], scale=1.0),
                        waits=scrS[0].rw() + tblfS[ti].ww())
                tblfS[ti].wrote(s5)
                scrS[0].read(s5)
                d5 = SCD(f"st_tbl{ti}", lambda e, pr=pr, ti=ti, dst=dst: e.dma_start(
                    out=tbl_s[pr, ti], in_=dst), waits=tblfS[ti].rw())
                tblfS[ti].read(d5)
                tblH[pr][ti].wrote(d5)

    s12.close()
    if stop_after == "tables":
        i32scr = sbuf(perm, "i32scr", [128, S], I32)
        i32S = Slot()
        for _pr in range(8):
            emit_table_block(_pr)
        bar_end = P.barrier()
        P.c("scalar", lambda e: e.nop(), waits=bar_end)
        return _finish(nc, P, perm)

    bar_pre3 = P.barrier(exclude=PRECAST_CH)
    i32scr = sbuf(sE, "i32scr", [128, S], I32)
    i32S = Slot(guard=bar_pre3)
    sF = contextlib.ExitStack()
    xncol = sbuf(sF, "xncol", [128, 2 * NCH * 512], BF16)
    xncolS = [[Slot() for _ in range(NCH)] for _ in range(2)]
    xncol_cur = {0: None, 1: None}
    kraw = sbuf(sF, "kraw", [128, NCH * 512], BF16)
    krawS = [Slot() for _ in range(NCH)]
    ropeb = sbuf(sF, "ropeb", [128, 4 * 512], BF16)
    ropebS = [Slot() for _ in range(4)]
    qsb = sbuf(sF, "qsb", [128, 2 * 512], BF16)
    qsbS = [Slot(), Slot()]
    qfb = sbuf(sF, "qfb", [128, 2 * 512], BF16)
    qfbS = [Slot(), Slot()]

    bar3 = P.barrier(exclude=PRECAST_CH)
    with nc.psum_tensor("pv0", [128, S], F32) as pv0, \
         nc.psum_tensor("pv1", [128, S], F32) as pv1:
        pvs = [pv0, pv1]
        pvS = [Slot(guard=bar3), Slot(guard=bar3)]
        first_ld = [True]
        for tb in (2, 3, 0, 1):
            par = tb % 2
            for g in range(4):
                glast = None
                for j in range(g * 4, (g + 1) * 4):
                    sl = xncolS[par][j]
                    glast = SYD(f"xnc{par}g{g}", lambda e, j=j, tb=tb, par=par: e.dma_start(
                        out=xncol[:, (par * NCH + j) * 512:(par * NCH + j + 1) * 512],
                        in_=xn_s[j][:, tb * 512:(tb + 1) * 512]),
                        waits=sl.ww() + xnH[j].rw() +
                        (bar_pre3 if first_ld[0] else []))
                    first_ld[0] = False
                    xnH[j].read(glast)
                for j in range(g * 4, (g + 1) * 4):
                    xncolS[par][j].wrote(glast)
            xncol_cur[par] = tb
            for tloc in range(4):
                tt = tb * 4 + tloc
                pp = tt % 2
                pv, pS = pvs[pp], pvS[pp]
                lastmm = None
                for j in range(NCH):
                    sl = xncolS[par][j]
                    lhs = xncol[:, (par * NCH + j) * 512 + tloc * 128:
                                (par * NCH + j) * 512 + (tloc + 1) * 128]
                    for ib in range(4):
                        w = sl.rw() + WsbS[j].rw()
                        if j == 0 and ib == 0:
                            w += pS.ww()
                        mm = TC(lambda e, lhs=lhs, j=j, ib=ib, pv=pv: e.matmul(
                            pv[:, ib * 512:(ib + 1) * 512], lhs,
                            Wsb[:, j * H + ib * 512:j * H + (ib + 1) * 512],
                            start=(j == 0), stop=(j == NCH - 1)), waits=w)
                        sl.read(mm)
                        WsbS[j].read(mm)
                        lastmm = mm
                pS.wrote(lastmm)
                ev = VC(lambda e, pv=pv, pp=pp: e.tensor_copy(
                    vob[:, pp * S:(pp + 1) * S], pv[:]),
                    waits=pS.rw() + vobS[pp].ww())
                vobS[pp].wrote(ev)
                pS.read(ev)
                st = SCD(f"st_vt{pp}", lambda e, tt=tt, pp=pp: e.dma_start(
                    out=vt_s[:, tt].rearrange("c p f -> p c f"),
                    in_=vob[:, pp * S:(pp + 1) * S].rearrange(
                        "p (c f) -> p c f", c=NCH)),
                    waits=vobS[pp].rw())
                vobS[pp].read(st)
                for ci in range(NCH):
                    vtH[ci].add(st)
                if tt % 2 == 0:
                    emit_table_block(tt // 2)

    def proj_rope(kind):
        bar = P.barrier(exclude=PRECAST_CH)
        for _i in range(4):
            tblbS[_i] = Slot(guard=bar)
        load_W(wkT if kind == "k" else wqT)
        nblk = NTB if kind == "k" else NOB
        with nc.psum_tensor(f"pk0_{kind}", [128, 512], F32) as pk0, \
             nc.psum_tensor(f"pk1_{kind}", [128, 512], F32) as pk1, \
             nc.psum_tensor(f"pkn_{kind}", [128, 1024], F32) as pkn:
            pks = [pk0, pk1]
            pkS = [Slot(guard=bar), Slot(guard=bar)]
            pknS = [Slot(guard=bar), Slot(guard=bar)]
            for tb in range(nblk):
                par = tb % 2
                if xncol_cur[par] != tb:
                    for g in range(4):
                        glast = None
                        for j in range(g * 4, (g + 1) * 4):
                            sl = xncolS[par][j]
                            glast = SYD(f"xnc{par}g{g}", lambda e, j=j, tb=tb, par=par: e.dma_start(
                                out=xncol[:, (par * NCH + j) * 512:(par * NCH + j + 1) * 512],
                                in_=xn_s[j][:, tb * 512:(tb + 1) * 512]),
                                waits=sl.ww() + xnH[j].rw())
                            xnH[j].read(glast)
                        for j in range(g * 4, (g + 1) * 4):
                            xncolS[par][j].wrote(glast)
                    xncol_cur[par] = tb
                for c in range(NCH):
                    pp = c % 2
                    pk, pS = pks[pp], pkS[pp]
                    lastmm = None
                    for j in range(NCH):
                        sl = xncolS[par][j]
                        w = sl.rw() + WsbS[j].rw()
                        if j == 0:
                            w += pS.ww()
                        mm = TC(lambda e, j=j, c=c, pk=pk, par=par: e.matmul(
                            pk[:], Wsb[:, j * H + c * 128:j * H + (c + 1) * 128],
                            xncol[:, (par * NCH + j) * 512:(par * NCH + j + 1) * 512],
                            start=(j == 0), stop=(j == NCH - 1)), waits=w)
                        sl.read(mm)
                        WsbS[j].read(mm)
                        lastmm = mm
                    pS.wrote(lastmm)
                    ev = VC(lambda e, pk=pk, c=c: e.tensor_copy(
                        kraw[:, c * 512:(c + 1) * 512], pk[:]),
                        waits=pS.rw() + krawS[c].ww())
                    krawS[c].wrote(ev)
                    pS.read(ev)
                for pr in range(8):
                    c0, c8 = pr, pr + 8
                    hp = pr % 2
                    slS, slC = tblbS[2 * hp], tblbS[2 * hp + 1]
                    sinb = tblb[:, (2 * hp) * 512:(2 * hp + 1) * 512]
                    cosb = tblb[:, (2 * hp + 1) * 512:(2 * hp + 2) * 512]
                    tS = SYD(f"tblq{2 * hp}", lambda e, pr=pr, tb=tb, sinb=sinb: e.dma_start(
                        out=sinb, in_=tbl_s[pr, 0][:, tb * 512:(tb + 1) * 512]),
                        waits=slS.ww() + tblH[pr][0].rw())
                    slS.wrote(tS)
                    tblH[pr][0].read(tS)
                    tCo = SYD(f"tblq{2 * hp + 1}", lambda e, pr=pr, tb=tb, cosb=cosb: e.dma_start(
                        out=cosb, in_=tbl_s[pr, 1][:, tb * 512:(tb + 1) * 512]),
                        waits=slC.ww() + tblH[pr][1].rw())
                    slC.wrote(tCo)
                    tblH[pr][1].read(tCo)
                    kr0 = kraw[:, c0 * 512:(c0 + 1) * 512]
                    kr8 = kraw[:, c8 * 512:(c8 + 1) * 512]
                    f0 = ropeb[:, (2 * hp) * 512:(2 * hp + 1) * 512]
                    f8 = ropeb[:, (2 * hp + 1) * 512:(2 * hp + 2) * 512]
                    f0S, f8S = ropebS[2 * hp], ropebS[2 * hp + 1]
                    sa = scr[0][:, 0:512]
                    sb2 = scr[0][:, 512:1024]
                    t1 = VC(lambda e, kr8=kr8, cosb=cosb, sa=sa: e.tensor_tensor(
                        sa, kr8, cosb, ALU.mult),
                        waits=krawS[c8].rw() + slC.rw() + scrS[0].ww())
                    krawS[c8].read(t1)
                    slC.read(t1)
                    scrS[0].wrote(t1)
                    t2 = VC(lambda e, kr0=kr0, sinb=sinb, sb2=sb2: e.tensor_tensor(
                        sb2, kr0, sinb, ALU.mult),
                        waits=krawS[c0].rw() + slS.rw())
                    krawS[c0].read(t2)
                    slS.read(t2)
                    t3 = VC(lambda e, f8=f8, sa=sa, sb2=sb2: e.tensor_tensor(
                        f8, sa, sb2, ALU.add), waits=f8S.ww())
                    f8S.wrote(t3)
                    scrS[0].read(t3)
                    t4 = VC(lambda e, kr0=kr0, cosb=cosb, sa=sa: e.tensor_tensor(
                        sa, kr0, cosb, ALU.mult),
                        waits=krawS[c0].rw() + slC.rw())
                    krawS[c0].read(t4)
                    slC.read(t4)
                    t5 = VC(lambda e, kr8=kr8, sinb=sinb, sb2=sb2: e.tensor_tensor(
                        sb2, kr8, sinb, ALU.mult),
                        waits=krawS[c8].rw() + slS.rw())
                    krawS[c8].read(t5)
                    slS.read(t5)
                    t6 = VC(lambda e, f0=f0, sa=sa, sb2=sb2: e.tensor_tensor(
                        f0, sa, sb2, ALU.subtract), waits=f0S.ww())
                    f0S.wrote(t6)
                    scrS[0].read(t6)
                    for (cc, fb, fbS) in ((c0, f0, f0S), (c8, f8, f8S)):
                        if kind == "k":
                            st = SCD(f"st_rope{2 * hp + (0 if cc == c0 else 1)}",
                                     lambda e, cc=cc, tb=tb, fb=fb: e.dma_start(
                                out=kfm_s[cc][:, tb * 512:(tb + 1) * 512], in_=fb),
                                waits=fbS.rw())
                            fbS.read(st)
                            kfmH[cc].add(st)
                            nsrc, nsrcS = fb, fbS
                        else:
                            qp = cc % 2
                            qv = qsb[:, qp * 512:(qp + 1) * 512]
                            tqs = VC(lambda e, fb=fb, qv=qv: e.tensor_scalar(
                                qv, fb, SCALE_QK, None, ALU.mult),
                                waits=qsbS[qp].ww())
                            qsbS[qp].wrote(tqs)
                            fbS.read(tqs)
                            st = SCD(f"st_qsb{qp}", lambda e, cc=cc, tb=tb, qv=qv: e.dma_start(
                                out=qs_s[cc][:, tb * 512:(tb + 1) * 512], in_=qv),
                                waits=qsbS[qp].rw())
                            qsbS[qp].read(st)
                            qsH[cc].add(st)
                            smn = scr[1][:, 0:512]
                            sex = scr[1][:, 512:1024]
                            srl = scr[1][:, 1024:1536]
                            tmn = VC(lambda e, fb=fb, smn=smn: e.tensor_scalar(
                                smn, fb, 0.0, None, ALU.min),
                                waits=scrS[1].ww())
                            scrS[1].wrote(tmn)
                            fbS.read(tmn)
                            tex = SC(lambda e, smn=smn, sex=sex: e.activation(
                                sex, smn, AF.Exp, bias=zeroc[:, 0:1], scale=1.0),
                                waits=scrS[1].rw())
                            scrS[1].wrote(tex)
                            trl = VC(lambda e, fb=fb, srl=srl: e.tensor_scalar(
                                srl, fb, 0.0, None, ALU.max),
                                waits=[tex])
                            scrS[1].wrote(trl)
                            fbS.read(trl)
                            qv2 = qfb[:, qp * 512:(qp + 1) * 512]
                            tqf = VC(lambda e, qv2=qv2, srl=srl, sex=sex: e.tensor_tensor(
                                qv2, srl, sex, ALU.add),
                                waits=qfbS[qp].ww())
                            qfbS[qp].wrote(tqf)
                            scrS[1].read(tqf)
                            st2 = SCD(f"st_qfb{qp}", lambda e, cc=cc, tb=tb, qv2=qv2: e.dma_start(
                                out=qf_s[cc][:, tb * 512:(tb + 1) * 512], in_=qv2),
                                waits=qfbS[qp].rw())
                            qfbS[qp].read(st2)
                            qfH[cc].add(st2)
                            nsrc, nsrcS = qv, qsbS[qp]
                        kp = cc % 2
                        k2v = k2scr[:, kp * 512:(kp + 1) * 512]
                        tk2 = VC(lambda e, nsrc=nsrc, k2v=k2v: e.tensor_tensor(
                            k2v, nsrc, nsrc, ALU.mult),
                            waits=k2S[kp].ww())
                        k2S[kp].wrote(tk2)
                        nsrcS.read(tk2)
                        pknv = pkn[:, kp * 512:(kp + 1) * 512]
                        mmn = TC(lambda e, k2v=k2v, pknv=pknv: e.matmul(
                            pknv, onesM[:], k2v, start=True, stop=True),
                                 waits=k2S[kp].rw() + pknS[kp].ww())
                        k2S[kp].read(mmn)
                        pknS[kp].wrote(mmn)
                        slot = (0 if kind == "k" else 64) + cc * 4 + tb
                        trm = VC(lambda e, pknv=pknv, slot=slot: e.reduce_max(
                            m2all[0:1, slot:slot + 1], pknv[0:1, :], axis=AX),
                                 waits=pknS[kp].rw())
                        m2allS.wrote(trm, reset=False)
                        pknS[kp].read(trm)

    proj_rope("k")
    proj_rope("q")

    for cc in range(NCH):
        t_ = VC(lambda e, cc=cc: e.reduce_max(km2[0:1, cc:cc + 1],
                                              m2all[0:1, cc * 4:cc * 4 + 4], axis=AX),
                waits=m2allS.rw() + km2S.ww())
        km2S.wrote(t_, reset=False)
        t_ = VC(lambda e, cc=cc: e.reduce_max(qm2[0:1, cc:cc + 1],
                                              m2all[0:1, 64 + cc * 4:64 + cc * 4 + 2], axis=AX),
                waits=m2allS.rw() + qm2S.ww())
        qm2S.wrote(t_, reset=False)
    tqm = SC(lambda e: e.activation(urow[:], qm2[:], AF.Sqrt, bias=zeroc[0:1, 0:1], scale=1.0),
             waits=qm2S.rw() + urowS.ww())
    urowS.wrote(tqm)
    qm2S.read(tqm)
    tkm = SC(lambda e: e.activation(u2row[:], km2[:], AF.Sqrt, bias=zeroc[0:1, 0:1], scale=1.0),
             waits=km2S.rw() + u2S.ww())
    u2S.wrote(tkm)
    km2S.read(tkm)
    if "diag_s" in dbg:
        tdq = SCD("st_diag", lambda e: e.dma_start(out=diag_s[0, 1:2, 0:16], in_=urow[:]),
                  waits=urowS.rw())
        urowS.read(tdq)
        tdk = SCD("st_diag", lambda e: e.dma_start(out=diag_s[1, 1:2, 0:16], in_=u2row[:]),
                  waits=u2S.rw())
        u2S.read(tdk)
    tun = VC(lambda e: e.scalar_tensor_tensor(urow[:], urow[:], -1.0, u2row[:],
                                              ALU.mult, ALU.mult),
             waits=urowS.rw() + u2S.rw())
    urowS.wrote(tun)
    u2S.read(tun)
    barU = P.barrier(exclude=PRECAST_CH)
    with nc.psum_tensor("pbu", [128, 16], F32) as pbu:
        pbuS = Slot(guard=barU)
        mm = TC(lambda e: e.matmul(pbu[:], onesr[0:1, :], urow[0:1, :], start=True, stop=True),
                waits=urowS.rw() + pbuS.ww())
        urowS.read(mm)
        pbuS.wrote(mm)
        ev = VC(lambda e: e.tensor_copy(negU[:], pbu[:]), waits=pbuS.rw() + negUS.ww())
        negUS.wrote(ev)
        pbuS.read(ev)
    if "diag_s" in dbg:
        tdu = SCD("st_diag", lambda e: e.dma_start(out=diagU_s[:], in_=negU[:]), waits=negUS.rw())
        negUS.read(tdu)

    sF.close()

    if stop_after == "proj":
        bar_end = P.barrier()
        P.c("scalar", lambda e: e.nop(), waits=bar_end)
        sE.close()
        return _finish(nc, P, perm)

    sG = contextlib.ExitStack()
    kfmh = sbuf(sG, "kfmh", [128, 2 * S], BF16); kfmhS = [Slot(), Slot()]
    vh = sbuf(sG, "vh", [128, 2 * S], BF16); vhS = [Slot(), Slot()]
    qsh = sbuf(sG, "qsh", [128, 2 * OWN], BF16); qshS = [Slot(), Slot()]
    qfh = sbuf(sG, "qfh", [128, 2 * OWN], BF16); qfhS = [Slot(), Slot()]
    kfbuf = sbuf(sG, "kfbuf", [128, 2 * S], BF16); kfbufS = [Slot(), Slot()]
    kftb = sbuf(sG, "kftb", [128, 2 * 128], BF16); kftbS = [Slot(), Slot()]
    memsb = sbuf(sG, "memsb", [128, 2 * 128], BF16); memsbS = [Slot(), Slot()]
    normsb = sbuf(sG, "normsb", [128, 2], BF16); normsbS = [Slot(), Slot()]
    ptile = sbuf(sG, "ptile", [128, 2 * 1024], BF16); ptileS = [Slot(), Slot()]
    rlgb = sbuf(sG, "rlgb", [128, 2 * OWN]); rlgbS = [Slot(), Slot()]
    qfsb = sbuf(sG, "qfsb", [128, 2 * OWN], BF16); qfsbS = [Slot(), Slot()]
    combb = sbuf(sG, "combb", [128, OWN], BF16); combbS = Slot()
    attnsb = sbuf(sG, "attnsb", [128, OWN]); attnsbS = Slot()
    rs2 = sbuf(sG, "rs2", [1, OWN]); rs2S = Slot()
    rsgb = sbuf(sG, "rsgb", [128, 512]); rsgbS = Slot()

    bar6 = P.barrier(exclude=PRECAST_CH)

    def pre_head(h, guard=()):
        par = h % 2
        t = SYD(f"kfmh{par}", lambda e, h=h, par=par: e.dma_start(
            out=kfmh[:, par * S:(par + 1) * S], in_=kfm_s[h]),
            waits=kfmhS[par].ww() + kfmH[h].rw() + list(guard))
        kfmhS[par].wrote(t)
        vdst = vh[:, par * S:(par + 1) * S].rearrange("p (t f) -> p t f", t=NTT)
        t = SYD(f"vh{par}", lambda e, vdst=vdst, h=h: e.dma_start(
            out=vdst, in_=vt_s[h].rearrange("t p f -> p t f")),
            waits=vhS[par].ww() + vtH[h].rw())
        vhS[par].wrote(t)
        t = SYD(f"qsh{par}", lambda e, h=h, par=par: e.dma_start(
            out=qsh[:, par * OWN:(par + 1) * OWN], in_=qs_s[h]),
            waits=qshS[par].ww() + qsH[h].rw())
        qshS[par].wrote(t)
        t = SYD(f"qfh{par}", lambda e, h=h, par=par: e.dma_start(
            out=qfh[:, par * OWN:(par + 1) * OWN], in_=qf_s[h]),
            waits=qfhS[par].ww() + qfH[h].rw())
        qfhS[par].wrote(t)

    def eluA(h):
        par = h % 2
        kfm_v = kfmh[:, par * S:(par + 1) * S]
        kfb_v = kfbuf[:, par * S:(par + 1) * S]
        t1 = VC(lambda e, kfm_v=kfm_v: e.tensor_scalar(scr[0][:], kfm_v, 0.0, None, ALU.min),
                waits=kfmhS[par].rw() + scrS[0].ww())
        scrS[0].wrote(t1)
        kfmhS[par].read(t1)
        t2 = SC(lambda e: e.activation(scr[1][:], scr[0][:], AF.Exp,
                                       bias=zeroc[:, 0:1], scale=1.0),
                waits=scrS[0].rw() + scrS[1].ww())
        scrS[1].wrote(t2)
        scrS[0].read(t2)
        t3 = VC(lambda e, kfm_v=kfm_v: e.tensor_scalar(scr[0][:], kfm_v, 0.0, None, ALU.max),
                waits=[t2] + scrS[0].ww())
        scrS[0].wrote(t3)
        kfmhS[par].read(t3)
        t4 = VC(lambda e, kfb_v=kfb_v: e.tensor_tensor(kfb_v, scr[0][:], scr[1][:], ALU.add),
                waits=kfbufS[par].ww())
        kfbufS[par].wrote(t4)
        scrS[0].read(t4)
        scrS[1].read(t4)

    def stageA(h, guard):
        par = h % 2
        vh_v = vh[:, par * S:(par + 1) * S]
        kfb_v = kfbuf[:, par * S:(par + 1) * S]
        with nc.psum_tensor(f"ptr0_{h}", [128, 128], BF16) as ptr0, \
             nc.psum_tensor(f"ptr1_{h}", [128, 128], BF16) as ptr1, \
             nc.psum_tensor(f"pmem_{h}", [128, 128], F32) as pmem, \
             nc.psum_tensor(f"pnorm_{h}", [128, 1], F32) as pnorm:
            ptrs = [ptr0, ptr1]
            ptrS = [Slot(guard=guard), Slot(guard=guard)]
            pmemS, pnormS = Slot(guard=guard), Slot(guard=guard)
            mm1 = mm2 = None
            for tt in range(NTT):
                pp = tt % 2
                tr = TC(lambda e, tt=tt, pp=pp, kfb_v=kfb_v: e.transpose(
                    ptrs[pp][:], kfb_v[:, tt * 128:(tt + 1) * 128], ident[:]),
                    waits=kfbufS[par].rw() + identS.rw() + ptrS[pp].ww())
                kfbufS[par].read(tr)
                identS.read(tr)
                ptrS[pp].wrote(tr)
                ev = VC(lambda e, pp=pp: e.tensor_copy(
                    kftb[:, pp * 128:(pp + 1) * 128], ptrs[pp][:]),
                    waits=ptrS[pp].rw() + kftbS[pp].ww())
                kftbS[pp].wrote(ev)
                ptrS[pp].read(ev)
                w1 = kftbS[pp].rw() + vhS[par].rw() + (pmemS.ww() if tt == 0 else [])
                mm1 = TC(lambda e, tt=tt, pp=pp, vh_v=vh_v: e.matmul(
                    pmem[:], kftb[:, pp * 128:(pp + 1) * 128],
                    vh_v[:, tt * 128:(tt + 1) * 128],
                    start=(tt == 0), stop=(tt == NTT - 1)), waits=w1)
                kftbS[pp].read(mm1)
                vhS[par].read(mm1)
                w2 = kftbS[pp].rw() + (pnormS.ww() if tt == 0 else [])
                mm2 = TC(lambda e, tt=tt, pp=pp: e.matmul(
                    pnorm[:], kftb[:, pp * 128:(pp + 1) * 128], onescb[:],
                    start=(tt == 0), stop=(tt == NTT - 1)), waits=w2)
                kftbS[pp].read(mm2)
            pmemS.wrote(mm1)
            pnormS.wrote(mm2)
            evm = VC(lambda e, par=par: e.tensor_copy(
                memsb[:, par * 128:(par + 1) * 128], pmem[:]),
                waits=pmemS.rw() + memsbS[par].ww())
            memsbS[par].wrote(evm)
            pmemS.read(evm)
            evn = VC(lambda e, par=par: e.tensor_copy(
                normsb[:, par:par + 1], pnorm[:]),
                waits=pnormS.rw() + normsbS[par].ww())
            normsbS[par].wrote(evn)
            pnormS.read(evn)
        return evm, evn

    def stageB(h, evm, evn):
        par = h % 2
        qf_v = qfh[:, par * OWN:(par + 1) * OWN]
        with nc.psum_tensor(f"pdl_{h}", [1, OWN], F32) as pdl, \
             nc.psum_tensor(f"pbcb_{h}", [128, OWN], F32) as pbcb:
            pdlS = Slot(guard=[evm, evn])
            pbcbS = Slot(guard=[evm, evn])
            mm = None
            for bb in range(2):
                w = normsbS[par].rw() + qfhS[par].rw() + (pdlS.ww() if bb == 0 else [])
                mm = TC(lambda e, bb=bb, qf_v=qf_v, par=par: e.matmul(
                    pdl[0:1, bb * 512:(bb + 1) * 512], normsb[:, par:par + 1],
                    qf_v[:, bb * 512:(bb + 1) * 512], start=True, stop=True), waits=w)
                normsbS[par].read(mm)
                qfhS[par].read(mm)
            pdlS.wrote(mm)
            trc = VC(lambda e: e.reciprocal(rl_r[:], pdl[0:1, :]),
                     waits=pdlS.rw() + rlS.ww())
            rlS.wrote(trc)
            pdlS.read(trc)
            trg = VC(lambda e, h=h: e.tensor_scalar(rl_r[:], rl_r[:],
                                                    grow[0:1, h:h + 1], None, ALU.mult),
                     waits=growS.rw())
            rlS.wrote(trg)
            growS.read(trg)
            mmb = None
            for bb in range(2):
                w = rlS.rw() + (pbcbS.ww() if bb == 0 else [])
                mmb = TC(lambda e, bb=bb: e.matmul(
                    pbcb[:, bb * 512:(bb + 1) * 512], onesr[0:1, :],
                    rl_r[0:1, bb * 512:(bb + 1) * 512], start=True, stop=True), waits=w)
                rlS.read(mmb)
            pbcbS.wrote(mmb)
            evb = VC(lambda e, par=par: e.tensor_copy(rlgb[:, par * OWN:(par + 1) * OWN],
                                                      pbcb[:]),
                     waits=pbcbS.rw() + rlgbS[par].ww())
            rlgbS[par].wrote(evb)
            pbcbS.read(evb)
        tqf = VC(lambda e, qf_v=qf_v, par=par: e.tensor_tensor(
            qfsb[:, par * OWN:(par + 1) * OWN], qf_v,
            rlgb[:, par * OWN:(par + 1) * OWN], ALU.mult),
            waits=qfhS[par].rw() + qfsbS[par].ww())
        qfsbS[par].wrote(tqf)
        qfhS[par].read(tqf)
        rlgbS[par].read(tqf)
        return evb

    pre_head(0, guard=bar6)
    eluA(0)
    evm0, evn0 = stageA(0, guard=bar6)
    evb0 = stageB(0, evm0, evn0)
    dguard = [evb0]
    pre_ops = [[None] * NCH, [None] * NCH, [None] * NOC]
    for h in range(NHEAD):
        par = h % 2
        if h + 1 < NHEAD:
            pre_head(h + 1)
            eluA(h + 1)
            if 1 <= h <= 12:
                pg_w = kfmhS[(h + 1) % 2].rw()
                base = (h - 1) * 8
                for k_ in range(base, base + 8):
                    if k_ < NCH:
                        op_ = GPD("wgbp", lambda e, j=k_: e.dma_start(
                            out=wgb[j], in_=wg8[j * 128:(j + 1) * 128, :]),
                            waits=pg_w)
                        pre_ops[0][k_] = op_
                    elif k_ < 2 * NCH:
                        j2 = k_ - NCH
                        op_ = GPD("wubp", lambda e, j=j2: e.dma_start(
                            out=wub[j], in_=wu8[j * 128:(j + 1) * 128, :]),
                            waits=pg_w)
                        pre_ops[1][j2] = op_
                    else:
                        o2 = k_ - 2 * NCH
                        op_ = GPD("wdbp", lambda e, o=o2: e.dma_start(
                            out=wdb[o], in_=wd8[o * 128:(o + 1) * 128, :]),
                            waits=pg_w)
                        pre_ops[2][o2] = op_
                if h == 12:
                    for j in range(NCH):
                        wgbH[j].wrote(pre_ops[0][NCH - 1])
                        wubH[j].wrote(pre_ops[1][NCH - 1])
                    for o in range(NOC):
                        wdbH[o].wrote(pre_ops[2][NOC - 1])
        kfm_v = kfmh[:, par * S:(par + 1) * S]
        vh_v = vh[:, par * S:(par + 1) * S]
        qs_v = qsh[:, par * OWN:(par + 1) * OWN]
        with nc.psum_tensor(f"pat0_{h}", [128, 512], F32) as pat0, \
             nc.psum_tensor(f"pat1_{h}", [128, 512], F32) as pat1, \
             nc.psum_tensor(f"pden_{h}", [1, OWN], F32) as pden:
            pats = [pat0, pat1]
            patS = [Slot(guard=dguard), Slot(guard=dguard)]
            pdenS = [Slot(guard=dguard), Slot(guard=dguard)]
            lastex = None
            with nc.psum_tensor(f"psc0_{h}", [128, 1024], F32) as psc0, \
                 nc.psum_tensor(f"psc1_{h}", [128, 1024], F32) as psc1:
                pscs = [psc0, psc1]
                pscS = [Slot(guard=dguard), Slot(guard=dguard)]
                NP = NTT // 2
                sched = []
                for b in range(2):
                    for p in range(NP):
                        sched.append(("sc", b, p))
                        if p >= 1:
                            sched.append(("pvdn", b, p - 1))
                    sched.append(("pvdn", b, NP - 1))
                for item in sched:
                    kind_, b, p = item
                    pp = p % 2
                    if kind_ == "sc":
                        sc_mm = None
                        for half in range(2):
                            s = 2 * p + half
                            w = kfmhS[par].rw() + qshS[par].rw() + \
                                (pscS[pp].ww() if half == 0 else [])
                            sc_mm = TC(lambda e, s=s, b=b, pp=pp, half=half,
                                       kfm_v=kfm_v, qs_v=qs_v:
                                       e.matmul(pscs[pp][:, half * 512:(half + 1) * 512],
                                                kfm_v[:, s * 128:(s + 1) * 128],
                                                qs_v[:, b * 512:(b + 1) * 512],
                                                start=True, stop=True), waits=w)
                            kfmhS[par].read(sc_mm)
                            qshS[par].read(sc_mm)
                        pscS[pp].wrote(sc_mm)
                        ex = SC(lambda e, pp=pp, h=h: e.activation(
                            ptile[:, pp * 1024:(pp + 1) * 1024], pscs[pp][:], AF.Exp,
                            bias=negU[:, h:h + 1], scale=1.0),
                            waits=pscS[pp].rw() + ptileS[pp].ww() + negUS.rw())
                        ptileS[pp].wrote(ex)
                        pscS[pp].read(ex)
                        negUS.read(ex)
                        lastex = ex
                    else:
                        for half in range(2):
                            s = 2 * p + half
                            pt_v = ptile[:, pp * 1024 + half * 512:
                                         pp * 1024 + (half + 1) * 512]
                            w1 = ptileS[pp].rw() + vhS[par].rw() + \
                                (patS[b].ww() if s == 0 else [])
                            pv = TC(lambda e, s=s, b=b, pt_v=pt_v, vh_v=vh_v: e.matmul(
                                pats[b][:], vh_v[:, s * 128:(s + 1) * 128], pt_v,
                                start=(s == 0), stop=(s == NTT - 1)), waits=w1)
                            ptileS[pp].read(pv)
                            vhS[par].read(pv)
                            w2 = ptileS[pp].rw() + (pdenS[b].ww() if s == 0 else [])
                            dn = TC(lambda e, s=s, b=b, pt_v=pt_v: e.matmul(
                                pden[0:1, b * 512:(b + 1) * 512], onescb[:], pt_v,
                                start=(s == 0), stop=(s == NTT - 1)), waits=w2)
                            ptileS[pp].read(dn)
                            if s == NTT - 1:
                                patS[b].wrote(pv)
                                pdenS[b].wrote(dn)
            e1t = []
            for b in range(2):
                ev = VC(lambda e, b=b: e.tensor_copy(
                    attnsb[:, b * 512:(b + 1) * 512], pats[b][:]),
                    waits=patS[b].rw() + (attnsbS.ww() if b == 0 else []))
                attnsbS.wrote(ev) if b == 0 else attnsbS.add(ev)
                patS[b].read(ev)
                e1t.append(ev)
                trs = VC(lambda e, b=b: e.reciprocal(
                    rs2[0:1, b * 512:(b + 1) * 512], pden[0:1, b * 512:(b + 1) * 512]),
                    waits=pdenS[b].rw() + (rs2S.ww() if b == 0 else []))
                rs2S.wrote(trs) if b == 0 else rs2S.add(trs)
                pdenS[b].read(trs)
                e1t.append(trs)
            tg2 = VC(lambda e, h=h: e.tensor_scalar(rs2[:], rs2[:],
                                                    ogrow[0:1, h:h + 1], None, ALU.mult),
                     waits=ogrowS.rw())
            rs2S.wrote(tg2)
            ogrowS.read(tg2)
            e1t.append(tg2)
        e1t.append(lastex)
        if h + 1 < NHEAD:
            evm1, evn1 = stageA(h + 1, guard=e1t)
            evb1 = stageB(h + 1, evm1, evn1)
            dguard = [evb1]
        with nc.psum_tensor(f"pmoE_{h}", [128, 512], F32) as pmoE, \
             nc.psum_tensor(f"pbcE_{h}", [128, 512], F32) as pbcE:
            pmoES = Slot(guard=e1t)
            pbcES = Slot(guard=e1t)
            ecombs = []
            for b in range(2):
                mo = TC(lambda e, b=b, par=par: e.matmul(
                    pmoE[:], memsb[:, par * 128:(par + 1) * 128],
                    qfsb[:, par * OWN + b * 512:par * OWN + (b + 1) * 512],
                    start=True, stop=True),
                    waits=memsbS[par].rw() + qfsbS[par].rw() + pmoES.ww())
                memsbS[par].read(mo)
                qfsbS[par].read(mo)
                pmoES.wrote(mo)
                bc = TC(lambda e, b=b: e.matmul(
                    pbcE[:], onesr[0:1, :], rs2[0:1, b * 512:(b + 1) * 512],
                    start=True, stop=True),
                    waits=rs2S.rw() + pbcES.ww())
                rs2S.read(bc)
                pbcES.wrote(bc)
                evb2 = VC(lambda e, b=b: e.tensor_copy(rsgb[:], pbcE[:]),
                          waits=pbcES.rw() + rsgbS.ww())
                rsgbS.wrote(evb2)
                pbcES.read(evb2)
                tc1 = VC(lambda e, b=b: e.tensor_tensor(
                    scr[3][:, 0:512], attnsb[:, b * 512:(b + 1) * 512],
                    rsgb[:], ALU.mult),
                    waits=attnsbS.rw() + rsgbS.rw() + scrS[3].ww())
                scrS[3].wrote(tc1)
                rsgbS.read(tc1)
                attnsbS.read(tc1)
                tc2 = VC(lambda e, b=b: e.tensor_tensor(
                    combb[:, b * 512:(b + 1) * 512], scr[3][:, 0:512], pmoE[:],
                    ALU.add),
                    waits=pmoES.rw() + (combbS.ww() if b == 0 else []))
                if b == 0:
                    combbS.wrote(tc2)
                else:
                    combbS.add(tc2)
                scrS[3].read(tc2)
                pmoES.read(tc2)
                ecombs.append(tc2)
            st = SCD("st_comb", lambda e, h=h: e.dma_start(out=comb_s[h], in_=combb[:]),
                     waits=ecombs)
            combbS.read(st)
            combH[h].wrote(st)

    sG.close()

    sH = contextlib.ExitStack()
    combsl = sbuf(sH, "combsl", [128, NCH * OWN], BF16)
    combslS = [Slot() for _ in range(NCH)]
    bar7 = P.barrier()
    load_W(woT, guard=bar7)
    glast = None
    for j in range(NCH):
        glast = SYD("combsl", lambda e, j=j: e.dma_start(
            out=combsl[:, j * OWN:(j + 1) * OWN], in_=comb_s[j]),
            waits=combslS[j].ww() + combH[j].rw() + (bar7 if j < 2 else []))
        combH[j].read(glast)
    for j in range(NCH):
        combslS[j].wrote(glast)
    with nc.psum_tensor("ps0", [128, OWN], F32) as ps0, \
         nc.psum_tensor("ps1", [128, OWN], F32) as ps1:
        pss = [ps0, ps1]
        psS = [Slot(guard=bar7), Slot(guard=bar7)]
        for ic in range(NCH):
            pp = ic % 2
            ps, pS = pss[pp], psS[pp]
            lastmm = None
            for j in range(NCH):
                for bb in range(2):
                    w = combslS[j].rw() + WsbS[j].rw()
                    if j == 0 and bb == 0:
                        w += pS.ww()
                    mm = TC(lambda e, j=j, bb=bb, ic=ic, ps=ps: e.matmul(
                        ps[:, bb * 512:(bb + 1) * 512],
                        Wsb[:, j * H + ic * 128:j * H + (ic + 1) * 128],
                        combsl[:, j * OWN + bb * 512:j * OWN + (bb + 1) * 512],
                        start=(j == 0), stop=(j == NCH - 1)), waits=w)
                    combslS[j].read(mm)
                    WsbS[j].read(mm)
                    lastmm = mm
            pS.wrote(lastmm)
            ev = VC(lambda e, ps=ps, pp=pp: e.tensor_copy(
                vob[:, pp * S:pp * S + OWN], ps[:]),
                waits=pS.rw() + vobS[pp].ww())
            vobS[pp].wrote(ev)
            pS.read(ev)
            st = SCD(f"st_vob{pp}", lambda e, ic=ic, pp=pp: e.dma_start(
                out=ss_s[ic], in_=vob[:, pp * S:pp * S + OWN]),
                waits=vobS[pp].rw())
            vobS[pp].read(st)
            ssH[ic].wrote(st)
    sH.close()
    sE.close()

    sI = contextlib.ExitStack()
    xnown = sbuf(sI, "xnown", [128, NCH * OWN], BF16)
    xnownS = [Slot() for _ in range(NCH)]
    wgsl = sbuf(sI, "wgsl", [128, 2 * NCH * 256], BF16)
    wgslS = [[Slot() for _ in range(4)] for _ in range(2)]
    wusl = sbuf(sI, "wusl", [128, 2 * NCH * 256], BF16)
    wuslS = [[Slot() for _ in range(4)] for _ in range(2)]
    fuseb = sbuf(sI, "fuseb", [128, 2 * 512], BF16); fusebS = [Slot(), Slot()]
    silug = sbuf(sI, "silug", [128, 2 * 512]); silugS = [Slot(), Slot()]
    wdsl = sbuf(sI, "wdsl", [128, 2 * 1024], BF16); wdslS = [Slot(), Slot()]
    gbuf = sbuf(sI, "gbuf", [128, 2 * 512], BF16); gbufS = [Slot(), Slot()]
    xtile = sbuf(sI, "xtile", [128, 2 * 512]); xtileS = [Slot(), Slot()]
    ybuf = sbuf(sI, "ybuf", [128, 2 * 512]); ybufS = [Slot(), Slot()]
    stile = sbuf(sI, "stile", [128, 2 * 512], BF16); stileS = [Slot(), Slot()]
    fuse_sb = sbuf(sI, "fuse_sb", [128, NOC * 512], BF16)
    fuse_sbS = [Slot() for _ in range(NOC)]

    bar8 = P.barrier()
    glast = None
    for j in range(NCH):
        glast = SYD("xnown", lambda e, j=j: e.dma_start(
            out=xnown[:, j * OWN:(j + 1) * OWN], in_=xn_s[j][:, 0:OWN]),
            waits=xnownS[j].ww() + xnH[j].rw() + (bar8 if j < 2 else []))
        xnH[j].read(glast)
    for j in range(NCH):
        xnownS[j].wrote(glast)
    final_stores_by_chan = {}
    for tb in range(NOB):
        bargu = P.barrier()
        with nc.psum_tensor(f"pgu{tb}", [128, 8 * 512], F32) as pgu:
            pguS = [Slot(guard=bargu) for _ in range(8)]
            for og2 in range(32):
                gpar2 = og2 % 2
                dpar = og2 % 2
                base = gpar2 * 4
                lasts = [None] * 4
                for q in range(4):
                    wg_v = wgsl[:, (dpar * NCH + q * 4) * 256:(dpar * NCH + q * 4 + 4) * 256]
                    src_g = wgb[q * 4:(q + 1) * 4, :, og2 * 256:(og2 + 1) * 256] \
                        .rearrange("j p f -> p j f")
                    tg_ = SYD(f"wg{dpar}q{q}", lambda e, wg_v=wg_v, src_g=src_g: e.dma_start(
                        out=wg_v.rearrange("p (j f) -> p j f", j=4), in_=src_g),
                        waits=wgslS[dpar][q].ww() + wgbH[q * 4].rw())
                    wgslS[dpar][q].wrote(tg_)
                    wgbH[q * 4].read(tg_)
                    wu_v = wusl[:, (dpar * NCH + q * 4) * 256:(dpar * NCH + q * 4 + 4) * 256]
                    src_u = wub[q * 4:(q + 1) * 4, :, og2 * 256:(og2 + 1) * 256] \
                        .rearrange("j p f -> p j f")
                    tu_ = SYD(f"wu{dpar}q{q}", lambda e, wu_v=wu_v, src_u=src_u: e.dma_start(
                        out=wu_v.rearrange("p (j f) -> p j f", j=4), in_=src_u),
                        waits=wuslS[dpar][q].ww() + wubH[q * 4].rw())
                    wuslS[dpar][q].wrote(tu_)
                    wubH[q * 4].read(tu_)
                for j in range(NCH):
                    q = j // 4
                    wg_v = wgsl[:, (dpar * NCH + j) * 256:(dpar * NCH + j + 1) * 256]
                    wu_v = wusl[:, (dpar * NCH + j) * 256:(dpar * NCH + j + 1) * 256]
                    for c2 in range(2):
                        w = wgslS[dpar][q].rw() + xnownS[j].rw() + \
                            (pguS[base + c2].ww() if j == 0 else [])
                        mm = TC(lambda e, j=j, c2=c2, tb=tb, wg_v=wg_v, base=base: e.matmul(
                            pgu[:, (base + c2) * 512:(base + c2 + 1) * 512],
                            wg_v[:, c2 * 128:(c2 + 1) * 128],
                            xnown[:, j * OWN + tb * 512:j * OWN + (tb + 1) * 512],
                            start=(j == 0), stop=(j == NCH - 1)), waits=w)
                        wgslS[dpar][q].read(mm)
                        xnownS[j].read(mm)
                        lasts[c2] = mm
                        w = wuslS[dpar][q].rw() + xnownS[j].rw() + \
                            (pguS[base + 2 + c2].ww() if j == 0 else [])
                        mm = TC(lambda e, j=j, c2=c2, tb=tb, wu_v=wu_v, base=base: e.matmul(
                            pgu[:, (base + 2 + c2) * 512:(base + 3 + c2) * 512],
                            wu_v[:, c2 * 128:(c2 + 1) * 128],
                            xnown[:, j * OWN + tb * 512:j * OWN + (tb + 1) * 512],
                            start=(j == 0), stop=(j == NCH - 1)), waits=w)
                        wuslS[dpar][q].read(mm)
                        xnownS[j].read(mm)
                        lasts[2 + c2] = mm
                for c2 in range(2):
                    oc = og2 * 2 + c2
                    pguS[base + c2].wrote(lasts[c2])
                    pguS[base + 2 + c2].wrote(lasts[2 + c2])
                    fpar = c2 % 2
                    sg_ = SC(lambda e, c2=c2, oc=oc, fpar=fpar, base=base: e.activation(
                        silug[:, fpar * 512:(fpar + 1) * 512],
                        pgu[:, (base + c2) * 512:(base + c2 + 1) * 512],
                        AF.Silu, bias=zeroc[:, 0:1], scale=sgb[:, oc:oc + 1]),
                        waits=pguS[base + c2].rw() + silugS[fpar].ww() + sgS.rw())
                    silugS[fpar].wrote(sg_)
                    pguS[base + c2].read(sg_)
                    fu_ = VC(lambda e, c2=c2, oc=oc, fpar=fpar, base=base:
                             e.scalar_tensor_tensor(
                        fuse_sb[:, oc * 512:(oc + 1) * 512],
                        pgu[:, (base + 2 + c2) * 512:(base + 3 + c2) * 512],
                        sub_[:, oc:oc + 1], silug[:, fpar * 512:(fpar + 1) * 512],
                        ALU.mult, ALU.mult),
                        waits=pguS[base + 2 + c2].rw() + silugS[fpar].rw() +
                        fuse_sbS[oc].ww() + suS.rw())
                    fuse_sbS[oc].wrote(fu_)
                    silugS[fpar].read(fu_)
                    pguS[base + 2 + c2].read(fu_)
        with nc.psum_tensor(f"pd{tb}", [128, 8 * 512], F32) as pd:
            bard = P.barrier()
            pdS = [Slot(guard=bard) for _ in range(8)]
            for ig in range(2):
                lasts = [None] * 8
                for o in range(NOC):
                    dpar = o % 2
                    wd_v = wdsl[:, dpar * 1024:(dpar + 1) * 1024]
                    td = SYD(f"wd{dpar}", lambda e, o=o, ig=ig, wd_v=wd_v: e.dma_start(
                        out=wd_v, in_=wdb[o][:, ig * 1024:(ig + 1) * 1024]),
                        waits=wdslS[dpar].ww() + wdbH[o].rw())
                    wdslS[dpar].wrote(td)
                    wdbH[o].read(td)
                    ft_v = fuse_sb[:, o * 512:(o + 1) * 512]
                    for i8 in range(8):
                        w = wdslS[dpar].rw() + fuse_sbS[o].rw() + \
                            (pdS[i8].ww() if o == 0 else [])
                        mm = TC(lambda e, i8=i8, o=o, wd_v=wd_v, ft_v=ft_v: e.matmul(
                            pd[:, i8 * 512:(i8 + 1) * 512],
                            wd_v[:, i8 * 128:(i8 + 1) * 128], ft_v,
                            start=(o == 0), stop=(o == NOC - 1)), waits=w)
                        wdslS[dpar].read(mm)
                        fuse_sbS[o].read(mm)
                        lasts[i8] = mm
                for i8 in range(8):
                    ic = ig * 8 + i8
                    pdS[i8].wrote(lasts[i8])
                    gpar = i8 % 2
                    sg_ = SC(lambda e, i8=i8, ic=ic, gpar=gpar: e.activation(
                        gbuf[:, gpar * 512:(gpar + 1) * 512],
                        pd[:, i8 * 512:(i8 + 1) * 512],
                        AF.Silu, bias=zeroc[:, 0:1], scale=sdb[:, ic:ic + 1]),
                        waits=pdS[i8].rw() + gbufS[gpar].ww() + sdS.rw())
                    gbufS[gpar].wrote(sg_)
                    pdS[i8].read(sg_)
                    ts_ = SYD(f"stile{gpar}", lambda e, ic=ic, tb=tb, gpar=gpar: e.dma_start(
                        out=stile[:, gpar * 512:(gpar + 1) * 512],
                        in_=ss_s[ic][:, tb * 512:(tb + 1) * 512]),
                        waits=stileS[gpar].ww() + ssH[ic].rw())
                    stileS[gpar].wrote(ts_)
                    tx_ = SYD(f"xtile{gpar}", lambda e, ic=ic, tb=tb, gpar=gpar: e.dma_start(
                        out=xtile[:, gpar * 512:(gpar + 1) * 512],
                        in_=xT[ic * 128:(ic + 1) * 128, tb * 512:(tb + 1) * 512]),
                        waits=xtileS[gpar].ww())
                    xtileS[gpar].wrote(tx_)
                    t1 = VC(lambda e, gpar=gpar: e.tensor_tensor(
                        ybuf[:, gpar * 512:(gpar + 1) * 512],
                        stile[:, gpar * 512:(gpar + 1) * 512],
                        gbuf[:, gpar * 512:(gpar + 1) * 512], ALU.mult),
                        waits=stileS[gpar].rw() + gbufS[gpar].rw() + ybufS[gpar].ww())
                    ybufS[gpar].wrote(t1)
                    stileS[gpar].read(t1)
                    gbufS[gpar].read(t1)
                    t2 = VC(lambda e, gpar=gpar: e.tensor_tensor(
                        ybuf[:, gpar * 512:(gpar + 1) * 512],
                        ybuf[:, gpar * 512:(gpar + 1) * 512],
                        xtile[:, gpar * 512:(gpar + 1) * 512], ALU.add),
                        waits=xtileS[gpar].rw())
                    ybufS[gpar].wrote(t2)
                    xtileS[gpar].read(t2)
                    sty = SCD(f"st_y{gpar}", lambda e, ic=ic, tb=tb, gpar=gpar: e.dma_start(
                        out=yT[ic * 128:(ic + 1) * 128, tb * 512:(tb + 1) * 512],
                        in_=ybuf[:, gpar * 512:(gpar + 1) * 512]),
                        waits=ybufS[gpar].rw())
                    ybufS[gpar].read(sty)
                    final_stores_by_chan[gpar] = sty
    sI.close()

    P.c("scalar", lambda e: e.nop(), waits=list(final_stores_by_chan.values()))
    return _finish(nc, P, perm)


_NC_CACHE = {}


def _get_nc(dbg=()):
    key = tuple(sorted(dbg))
    if key not in _NC_CACHE:
        _NC_CACHE[key] = build_nc(key)
    return _NC_CACHE[key]


def make_in_maps(inputs):
    x = np.asarray(inputs["L0_attn_hidden_states"], dtype=np.float32)
    wq = np.asarray(inputs["L0_attn_Wq"], dtype=np.float32)
    wk = np.asarray(inputs["L0_attn_Wk"], dtype=np.float32)
    wv = np.asarray(inputs["L0_attn_Wv"], dtype=np.float32)
    wo = np.asarray(inputs["L0_attn_Wo"], dtype=np.float32)
    wg8 = np.ascontiguousarray(np.asarray(inputs["L0_ffn_w_gate"], dtype=np.int8))
    wu8 = np.ascontiguousarray(np.asarray(inputs["L0_ffn_w_up"], dtype=np.int8))
    wd8 = np.ascontiguousarray(np.asarray(inputs["L0_ffn_w_down"], dtype=np.int8))
    sg = np.asarray(inputs["L0_ffn_s_gate"], dtype=np.float32)
    su = np.asarray(inputs["L0_ffn_s_up"], dtype=np.float32)
    sd = np.asarray(inputs["L0_ffn_s_down"], dtype=np.float32)
    gate = np.asarray(inputs["L0_attn_gate"], dtype=np.float32)

    wqT = np.ascontiguousarray(wq.T)
    wkT = np.ascontiguousarray(wk.T)
    wvT = np.ascontiguousarray(wv.T)
    woT = np.ascontiguousarray(wo.T)
    sgI = np.ascontiguousarray(sg.reshape(64, 128).T)
    suI = np.ascontiguousarray(su.reshape(64, 128).T)
    sdI = np.ascontiguousarray(sd.reshape(16, 128).T)
    gateI = np.ascontiguousarray(gate.reshape(1, 16))

    in_maps = []
    for core in range(8):
        b, half = core // 2, core % 2
        prm = np.concatenate([np.arange(half * OWN, (half + 1) * OWN),
                              np.arange((1 - half) * OWN, (2 - half) * OWN)])
        xTc = np.ascontiguousarray(x[b].T[:, prm])
        posc = np.ascontiguousarray(
            np.tile(prm.astype(np.float32)[None, :], (128, 1)))
        in_maps.append({
            "xT": xTc, "wqT": wqT, "wkT": wkT, "wvT": wvT, "woT": woT,
            "wg8": wg8, "wu8": wu8, "wd8": wd8,
            "sgI": sgI, "suI": suI, "sdI": sdI, "gateI": gateI, "posB": posc,
        })
    return in_maps


def kernel(**inputs):
    nc = _get_nc()
    in_maps = make_in_maps(inputs)
    res = run_bass_kernel_spmd(nc, in_maps, core_ids=list(range(8)))
    y = np.empty((NB, S, H), dtype=np.float32)
    for core in range(8):
        b, half = core // 2, core % 2
        y[b, half * OWN:(half + 1) * OWN, :] = np.asarray(res.results[core]["yT"]).T
    return y
